# version 16
# baseline (speedup 1.0000x reference)
import numpy as np
import ml_dtypes
from contextlib import ExitStack

import concourse.bass as bass
import concourse.mybir as mybir
from concourse.bass_utils import run_bass_kernel_spmd

F32 = mybir.dt.float32
BF16 = mybir.dt.bfloat16
I32 = mybir.dt.int32
AF = mybir.ActivationFunctionType
ALU = mybir.AluOpType
AX = mybir.AxisListType

BC = 4
S = 2048
D = 1024
NTOK = BC * S
NBLK = NTOK // 128
FF = 4096
EPS = 1e-5
MAGIC = 0x5F3759DF

F_CHUNKS = [(i * 3, 3) for i in range(21)] + [(63, 1)]

P2 = {}
P2["W1"] = 0
P2["W2"] = 32768
P2["xnt"] = 65536
P2["h1t"] = 71680
P2["xnb"] = 72448
P2["gt"] = 74496
P2["bt"] = 75520
P2["tall"] = 76544
P2["rows2"] = 77696
P2_SIZE = 80896
P2["xntA"] = 0
P2["z0c"] = 16384
P2["vt"] = 32768
P2["vnat"] = 49152
P2["wout"] = 65536
P2["wi"] = 73728
P2["wiA"] = 78848
P2["hpad"] = 65536
P2["growb"] = 66816
P2["betrb"] = 67840

P4 = {}
P4["xfr"] = 0
P4["xtl"] = 3072
P4["xfb"] = 5120
P4["t1"] = 8192
P4["gpre"] = 9216
P4["gbuf"] = 10240
P4["sqscr"] = 11264
P4["stats"] = 12288
P4_F_END = 12416
P4["stage"] = 0
P4["tconv"] = 2052
P4["z2c"] = 4100
P4["xinA"] = 6148
P4["ginA"] = 8196
P4["obuf"] = 10244
P4["vfb"] = 11268
P4["svec"] = 11780
P4["carry"] = 11940
P4_SIZE = 12416
P4["aw3"] = P4["xfb"]
P4["grow"] = P4["t1"]
P4["brow"] = P4["xfb"] + 1024
P4["aw2"] = P4["gpre"]
P4["wvec"] = P4["gbuf"]

SV = {"t00": 0, "t01": 8, "t02": 16, "t20": 24, "t21": 32, "t22": 40,
      "bin0": 48, "bin2": 56, "bc0": 64, "bc2": 72, "fb": 80}

ST_MU1 = lambda slot: 24 * 0 + slot * 8
ST_R1 = lambda slot: slot * 8 + 4
ST_TMP = 24
ST = {"ssum": 24, "ssq": 27, "vtmp": 30, "yA": 36, "yB": 39,
      "mu2": 42, "r2": 45, "s2sum": 48, "s2sq": 51, "vt2": 54}
ST_NB = lambda slot: 57 + slot * 3

SEMS = ["V", "SC", "GP", "T", "dX0", "dX1", "dXT0", "dXT1", "dSCw0", "dSCw1",
        "dX2", "dTP0", "dTP1", "dW1", "dW2", "dWI0", "dWI1", "dWO", "dG", "dGI0",
        "dGI1", "dXI0", "dXI1", "dVT", "dVN", "dY", "dROW", "dTall", "dHP",
        "dB1", "dXTA", "dVEC", "dDBG", "dVN2", "dXQ"]


def build(debug=False):
    import os as _os
    skip_f = bool(_os.environ.get("MKDEV_NOF")) and debug
    nc = bass.Bass("TRN2", target_bir_lowering=False)

    x_t = nc.dram_tensor("x", [NTOK, D], F32, kind="ExternalInput")
    y_t = nc.dram_tensor("y", [NTOK, D], F32, kind="ExternalOutput")
    win0_t = nc.dram_tensor("win0", [D, D], F32, kind="ExternalInput")
    win2_t = nc.dram_tensor("win2", [D, D], F32, kind="ExternalInput")
    bin0_t = nc.dram_tensor("bin0", [D], F32, kind="ExternalInput")
    bin2_t = nc.dram_tensor("bin2", [D], F32, kind="ExternalInput")
    wc0_t = nc.dram_tensor("wc0", [D, 3], F32, kind="ExternalInput")
    wc2_t = nc.dram_tensor("wc2", [D, 3], F32, kind="ExternalInput")
    bc0_t = nc.dram_tensor("bc0", [D], F32, kind="ExternalInput")
    bc2_t = nc.dram_tensor("bc2", [D], F32, kind="ExternalInput")
    ab1_t = nc.dram_tensor("ab1", [64], F32, kind="ExternalInput")
    aw2_t = nc.dram_tensor("aw2", [64, 64], F32, kind="ExternalInput")
    ab2_t = nc.dram_tensor("ab2", [64], F32, kind="ExternalInput")
    aw3_t = nc.dram_tensor("aw3", [64, D], F32, kind="ExternalInput")
    fb_t = nc.dram_tensor("fb", [D], F32, kind="ExternalInput")
    wout_t = nc.dram_tensor("wout", [D, D], F32, kind="ExternalInput")
    bout_t = nc.dram_tensor("bout", [1, D], F32, kind="ExternalInput")
    w1f_t = nc.dram_tensor("w1f", [D, FF], F32, kind="ExternalInput")
    b1f_t = nc.dram_tensor("b1f", [FF], F32, kind="ExternalInput")
    w2f_t = nc.dram_tensor("w2f", [FF, D], F32, kind="ExternalInput")
    b2f_t = nc.dram_tensor("b2f", [1, D], F32, kind="ExternalInput")
    gam_t = nc.dram_tensor("gam", [1, D], F32, kind="ExternalInput")
    bet_t = nc.dram_tensor("bet", [1, D], F32, kind="ExternalInput")

    if skip_f:
        s_xnbf = nc.dram_tensor("xnin", [NTOK, D], BF16, kind="ExternalInput")
    else:
        s_xnbf = nc.dram_tensor("s_xnbf", [NTOK, D], BF16, kind="Internal")
    s_g = nc.dram_tensor("s_g", [NTOK, D], F32, kind="Internal")
    s_vt = nc.dram_tensor("s_vt", [D, S], BF16, kind="Internal")
    s_hpad = nc.dram_tensor("s_hpad", [1, 1280], BF16, kind="Internal")
    if debug:
        dxn_t = nc.dram_tensor("dbg_xn", [NTOK, D], BF16, kind="ExternalOutput")
        dg_t = nc.dram_tensor("dbg_g", [NTOK, D], F32, kind="ExternalOutput")
        dxf_t = nc.dram_tensor("dbg_xf", [NTOK, D], F32, kind="ExternalOutput")
        dvt_t = nc.dram_tensor("dbg_vt", [D, S], BF16, kind="ExternalOutput")
        dz0_t = nc.dram_tensor("dbg_z0c", [D, S], BF16, kind="ExternalOutput")
        dwt_t = nc.dram_tensor("dbg_wt", [D, S], BF16, kind="ExternalOutput")
        dxa_t = nc.dram_tensor("dbg_xnta", [D, S], BF16, kind="ExternalOutput")
        dsv_t = nc.dram_tensor("dbg_svec", [128, 88], F32, kind="ExternalOutput")
        dwi_t = nc.dram_tensor("dbg_wi", [128, 2048], BF16, kind="ExternalOutput")
        dst_t = nc.dram_tensor("dbg_stage", [128, 2048], F32, kind="ExternalOutput")

    with ExitStack() as stack:
        pool2 = stack.enter_context(nc.sbuf_tensor("pool2", [128, P2_SIZE], BF16))
        pool4 = stack.enter_context(nc.sbuf_tensor("pool4", [128, P4_SIZE], F32))
        b1tl = stack.enter_context(nc.sbuf_tensor("b1tl", [128, 32], F32))
        svtl = stack.enter_context(nc.sbuf_tensor("svtl", [128, 128], F32))
        pball = stack.enter_context(nc.psum_tensor("pball", [128, 4096], F32))
        SH = {n: stack.enter_context(nc.semaphore(n)) for n in SEMS}
        blk = stack.enter_context(nc.Block())

        def p2(off, n, p0=0, np_=128):
            return pool2[p0:p0 + np_, off:off + n]

        def p4(off, n, p0=0, np_=128):
            return pool4[p0:p0 + np_, off:off + n]

        def sv1(off):
            return svtl[:, off:off + 1]

        def bank(k, off=0, n=512):
            return pball[:, k * 512 + off:k * 512 + off + n]

        ONES = pool2[0:1, P2["rows2"]:P2["rows2"] + 128]
        ROWB = pool2[0:1, P2["rows2"] + 128:P2["rows2"] + 128 + D]

        qs = {e: [] for e in ("sync", "tensor", "vector", "scalar", "gpsimd")}
        cnt = {n: 0 for n in SEMS}
        hw = {}

        def wait(eng, sem, val):
            if val is None or val <= 0:
                return
            assert val <= cnt[sem], f"wait {sem}>={val} but only {cnt[sem]} emitted"
            key = (eng, sem)
            if hw.get(key, 0) >= val:
                return
            hw[key] = val
            qs[eng].append(lambda E, sem=sem, val=val: E.wait_ge(SH[sem], val))

        def vop(fn, waits=()):
            for s, v in waits:
                wait("vector", s, v)
            wait("vector", "V", cnt["V"])
            cnt["V"] += 1
            qs["vector"].append(lambda E, fn=fn: fn(E).then_inc(SH["V"], 1))
            return cnt["V"]

        def sop(fn, waits=()):
            for s, v in waits:
                wait("scalar", s, v)
            wait("scalar", "SC", cnt["SC"])
            cnt["SC"] += 1
            qs["scalar"].append(lambda E, fn=fn: fn(E).then_inc(SH["SC"], 1))
            return cnt["SC"]

        def gop(fn, waits=()):
            for s, v in waits:
                wait("gpsimd", s, v)
            wait("gpsimd", "GP", cnt["GP"])
            cnt["GP"] += 1
            qs["gpsimd"].append(lambda E, fn=fn: fn(E).then_inc(SH["GP"], 1))
            return cnt["GP"]

        def top(fn, waits=(), inc=False):
            for s, v in waits:
                wait("tensor", s, v)
            if inc:
                cnt["T"] += 1
                qs["tensor"].append(lambda E, fn=fn: fn(E).then_inc(SH["T"], 1))
                return cnt["T"]
            qs["tensor"].append(lambda E, fn=fn: fn(E))
            return None

        def dma(eng, sem, fn, waits=()):
            for s, v in waits:
                wait(eng, s, v)
            cnt[sem] += 16
            qs[eng].append(lambda E, fn=fn, sem=sem: fn(E).then_inc(SH[sem], 16))
            return cnt[sem]

        AW2 = p4(P4["aw2"], 64, 0, 64)
        AW3 = p4(P4["aw3"], D, 0, 64)
        AB1 = p4(P4["wvec"] + 0, 1, 0, 64)
        AB2 = p4(P4["wvec"] + 1, 1, 0, 64)
        HM1 = p4(P4["wvec"] + 2, 1, 0, 64)
        HM2 = p4(P4["wvec"] + 3, 1, 0, 64)
        HM2P = p4(P4["wvec"] + 4, 1, 0, 64)
        HPAD = p2(P2["hpad"], 1280, 0, 1)
        GT = p2(P2["gt"], D)
        BT = p2(P2["bt"], D)
        TALL = p2(P2["tall"], 1152)

        dma("gpsimd", "dROW", lambda E: E.dma_start(
            out=AB1, in_=ab1_t.ap().rearrange("(a b) -> a b", b=1)))
        dma("gpsimd", "dROW", lambda E: E.dma_start(out=AW2, in_=aw2_t.ap()))
        dma("gpsimd", "dROW", lambda E: E.dma_start(
            out=AB2, in_=ab2_t.ap().rearrange("(a b) -> a b", b=1)))
        dma("gpsimd", "dROW", lambda E: E.dma_start(out=AW3, in_=aw3_t.ap()))
        dma("gpsimd", "dROW", lambda E: E.dma_start(
            out=p2(P2["growb"], D, 0, 1), in_=gam_t.ap()))
        dma("gpsimd", "dROW", lambda E: E.dma_start(
            out=p2(P2["betrb"], D, 0, 1), in_=bet_t.ap()))
        dROW_b2 = dma("gpsimd", "dROW", lambda E: E.dma_start(out=ROWB, in_=b2f_t.ap()))
        for name, src, off in (("bin0", bin0_t, SV["bin0"]), ("bin2", bin2_t, SV["bin2"]),
                               ("bc0", bc0_t, SV["bc0"]), ("bc2", bc2_t, SV["bc2"]),
                               ("fb", fb_t, SV["fb"])):
            for dc in range(8):
                dma("gpsimd", "dVEC", lambda E, src=src, off=off, dc=dc: E.dma_start(
                    out=sv1(off + dc),
                    in_=bass.AP(src, dc * 128, [[1, 128], [1, 1]])))
        for stn, wct in (("t0", wc0_t), ("t2", wc2_t)):
            for k in range(3):
                for dc in range(8):
                    off = SV[stn + str(k)] + dc
                    dma("gpsimd", "dVEC", lambda E, wct=wct, k=k, dc=dc, off=off:
                        E.dma_start(out=sv1(off),
                                    in_=bass.AP(wct, dc * 128 * 3 + k, [[3, 128], [1, 1]]),
                                    allow_slow_non_contiguous=True))
        dvec_total = cnt["dVEC"]
        for fc in range(32):
            dma("gpsimd", "dB1", lambda E, fc=fc: E.dma_start(
                out=b1tl[:, fc:fc + 1],
                in_=bass.AP(b1f_t, fc * 128, [[1, 128], [1, 1]])))

        v_ones = vop(lambda E: E.memset(ONES, 1.0))

        sop(lambda E: E.activation(HM1, AB1, AF.Silu), waits=[("dROW", dROW_b2)])
        top(lambda E: E.matmul(pball[0:64, 0:1], AW2, HM1, start=True, stop=True),
            waits=[("dROW", dROW_b2), ("SC", cnt["SC"])], inc=True)
        vop(lambda E: E.tensor_scalar(HM2P, pball[0:64, 0:1], AB2, None, ALU.add),
            waits=[("T", cnt["T"]), ("dROW", dROW_b2)])
        sop(lambda E: E.activation(HM2, HM2P, AF.Silu), waits=[("V", cnt["V"])])
        top(lambda E: E.matmul(pball[0:1, 512:1024], HM2, AW3[:, 0:512],
                               start=True, stop=True), waits=[("SC", cnt["SC"])])
        t_h = top(lambda E: E.matmul(pball[0:1, 1024:1536], HM2, AW3[:, 512:1024],
                                     start=True, stop=True), inc=True)
        vop(lambda E: E.memset(HPAD, 0.0))
        vop(lambda E: E.tensor_copy(HPAD[0:1, 127:639], pball[0:1, 512:1024]),
            waits=[("T", t_h)])
        v_hp = vop(lambda E: E.tensor_copy(HPAD[0:1, 639:1151], pball[0:1, 1024:1536]))
        dma("gpsimd", "dHP", lambda E: E.dma_start(out=s_hpad.ap(), in_=HPAD),
            waits=[("V", v_hp)])
        top(lambda E: E.matmul(bank(0), ONES, p2(P2["growb"], 512, 0, 1),
                               start=True, stop=True),
            waits=[("dROW", dROW_b2), ("V", max(v_ones, v_hp))])
        t_g = top(lambda E: E.matmul(bank(1), ONES, p2(P2["growb"] + 512, 512, 0, 1),
                                     start=True, stop=True), inc=True)
        vop(lambda E: E.tensor_copy(GT[:, 0:512], bank(0)), waits=[("T", t_g)])
        v_gt = vop(lambda E: E.tensor_copy(GT[:, 512:1024], bank(1)))
        top(lambda E: E.matmul(bank(2), ONES, p2(P2["betrb"], 512, 0, 1),
                               start=True, stop=True), waits=[("V", v_gt)])
        t_b = top(lambda E: E.matmul(bank(3), ONES, p2(P2["betrb"] + 512, 512, 0, 1),
                                     start=True, stop=True), inc=True)
        vop(lambda E: E.tensor_copy(BT[:, 0:512], bank(2)), waits=[("T", t_b)])
        v_bt = vop(lambda E: E.tensor_copy(BT[:, 512:1024], bank(3)))
        v_wend = v_bt

        if not skip_f:
            for g in range(4):
                for dc in range(8):
                    dma("gpsimd", "dW1",
                        lambda E, dc=dc, g=g: E.dma_start(
                            out=p2(P2["W1"] + dc * FF + g * 1024, 1024),
                            in_=bass.AP(w1f_t, dc * 128 * FF + g * 1024,
                                        [[FF, 128], [1, 1024]])),
                        waits=[("dW1", 16 * 8 * g)] if (dc == 0 and g) else ())
                for k in range(8):
                    fc = g * 8 + k
                    dma("gpsimd", "dW2",
                        lambda E, fc=fc: E.dma_start(
                            out=p2(P2["W2"] + fc * D, D),
                            in_=x_slice(w2f_t, fc)),
                        waits=[("dW2", 16 * 8 * g)] if (k == 0 and g) else ())
            wi0_cnt = [None, None]
            for li in range(2):
                src0 = win2_t if (li % 16) // 8 else win0_t
                wi0_cnt[li] = dma(
                    "gpsimd", "dWI%d" % li, lambda E, src0=src0, li=li:
                    E.dma_start(
                        out=p2(P2["wiA"] + li * D, D),
                        in_=bass.AP(src0, (li % 8) * 128,
                                    [[D, 128], [128 * D, 8], [1, 128]])))

        tW = cnt["T"]
        NCH = len(F_CHUNKS)
        relV_eo = [None] * NCH
        t_m4end = [None] * NCH
        sc_h1 = [[None] * 32 for _ in range(NCH)]
        t_m5 = [[None] * 3 for _ in range(NCH)]
        sc_cast = [None] * NBLK
        xfr_rel = [None] * NBLK
        xfr_relv = [None] * NBLK
        xtl_rel = [None] * NBLK
        dscw_cnt = [None] * NBLK
        dx_cnt = [None] * NBLK
        dxt_cnt = [None] * NBLK
        dg_cnt = [None] * NBLK
        sc_silu = [None] * NBLK
        dbgxf_cnt = [None] * NBLK
        dvt_chain = [0]
        mu1v = {}
        r1v = {}

        def front_thunks(ci):
            b0, nb = F_CHUNKS[ci]
            slot = ci % 3
            st_ = {}
            thunks = []

            def blockA(bl):
                beta = b0 + bl
                w = []
                if beta >= 3:
                    w = [("SC", xfr_rel[beta - 3]), ("V", xfr_relv[beta - 3])]
                dx_cnt[beta] = dma("sync", "dX%d" % (beta % 3),
                                   lambda E, beta=beta: E.dma_start(
                    out=p4(P4["xfr"] + (beta % 3) * D, D),
                    in_=x_t.ap()[beta * 128:(beta + 1) * 128, :]), waits=w)
                xfr_relv[beta] = vop(
                    lambda E, beta=beta, bl=bl: E.reduce_sum(
                        p4(P4["stats"] + ST["ssum"] + bl, 1),
                        p4(P4["xfr"] + (beta % 3) * D, D), AX.X),
                    waits=[("dX%d" % (beta % 3), dx_cnt[beta])])
                wsq = [("dX%d" % (beta % 3), dx_cnt[beta])]
                if beta == 0:
                    wsq.append(("T", tW))
                sop(lambda E, beta=beta, bl=bl: E.activation(
                    p4(P4["sqscr"], D), p4(P4["xfr"] + (beta % 3) * D, D),
                    AF.Square, accum_out=p4(P4["stats"] + ST["ssq"] + bl, 1)),
                    waits=wsq)
            for bl in range(nb):
                thunks.append(lambda bl=bl: blockA(bl))

            stv = lambda k, n=nb: p4(P4["stats"] + k, n)

            def rstd_a():
                mu1 = p4(P4["stats"] + ST_MU1(slot), nb)
                mu1v[ci] = mu1
                r1v[ci] = p4(P4["stats"] + ST_R1(slot), nb)
                vop(lambda E, mu1=mu1: E.tensor_scalar(mu1, stv(ST["ssum"]), 1.0 / D,
                                              None, ALU.mult),
                    waits=[("SC", cnt["SC"])])
                vop(lambda E, mu1=mu1: E.tensor_tensor(stv(ST["vtmp"]), mu1, mu1,
                                                       ALU.mult))
                vop(lambda E: E.scalar_tensor_tensor(
                    stv(ST["vtmp"]), stv(ST["ssq"]), 1.0 / D, stv(ST["vtmp"]),
                    ALU.mult, ALU.subtract))
                vop(lambda E: E.tensor_scalar(stv(ST["vtmp"]), stv(ST["vtmp"]),
                                              EPS, None, ALU.add))

            def rstd_b():
                r1 = r1v[ci]
                _rsqrt(vop, stv(ST["vtmp"]), r1, stv(ST["yA"]), stv(ST["yB"]), nb)
                nbias = p4(P4["stats"] + ST_NB(slot), nb)
                vop(lambda E, nbias=nbias, r1=r1: E.scalar_tensor_tensor(
                    nbias, mu1v[ci], -1.0, r1, ALU.mult, ALU.mult))
                st_["v_stats"] = cnt["V"]
            thunks.append(rstd_a)
            thunks.append(rstd_b)

            def blockC(bl):
                beta = b0 + bl
                r1 = r1v[ci]
                w = [("V", st_["v_stats"])]
                if beta >= 2:
                    w.append(("dSCw%d" % (beta % 2), dscw_cnt[beta - 2]))
                sc_cast[beta] = sop(
                    lambda E, beta=beta, bl=bl, r1=r1, slot=slot: E.activation(
                        p2(P2["xnb"] + (beta % 2) * D, D),
                        p4(P4["xfr"] + (beta % 3) * D, D), AF.Identity,
                        bias=p4(P4["stats"] + ST_NB(slot) + bl, 1),
                        scale=r1[:, bl:bl + 1]), waits=w)
                xfr_rel[beta] = sc_cast[beta]
                dscw_cnt[beta] = dma(
                    "scalar", "dSCw%d" % (beta % 2), lambda E, beta=beta: E.dma_start(
                        out=s_xnbf.ap()[beta * 128:(beta + 1) * 128, :],
                        in_=p2(P2["xnb"] + (beta % 2) * D, D)),
                    waits=[("SC", sc_cast[beta])])
            for bl in range(nb):
                thunks.append(lambda bl=bl: blockC(bl))

            def transposes():
                w = [("dSCw%d" % ((b0 + k) % 2), dscw_cnt[b0 + k]) for k in range(nb)]
                if ci == 0:
                    w.append(("dHP", 16))
                if ci >= 2:
                    w.append(("T", t_m4end[ci - 2]))
                for dc in range(8):
                    dma("sync", "dTP%d" % (ci % 2),
                        lambda E, dc=dc, b0=b0, nb=nb, ci=ci: E.dma_start(
                        out=p2(P2["xnt"] + (ci % 2) * 3072 + dc * 384, nb * 128),
                        in_=s_xnbf.ap()[b0 * 128:(b0 + nb) * 128,
                                        dc * 128:(dc + 1) * 128],
                        transpose=True), waits=w if dc == 0 else ())
            thunks.append(transposes)
            return thunks

        def emit_mid(ci, pending=None):
            b0, nb = F_CHUNKS[ci]
            def drain(k=1):
                for _ in range(k):
                    if pending:
                        pending.popleft()()
            dtp_need = 16 * 8 * (ci + 1) + 16 * 128
            def m5p(fc):
                w = [("V", sc_h1[ci][fc])]
                if ci == 0 and fc % 8 == 0:
                    w.append(("dW2", 16 * 8 * (fc // 8 + 1)))
                if fc == 0:
                    w.append(("dROW", dROW_b2))
                    if ci == 0:
                        w.append(("V", v_wend))
                    if ci >= 1 and relV_eo[ci - 1] is not None:
                        w.append(("V", relV_eo[ci - 1]))
                for bl in range(nb):
                    lhsT = p2(P2["h1t"] + (fc % 2) * 384 + bl * 128, 128)
                    for half in range(2):
                        top(lambda E, lhsT=lhsT, fc=fc, bl=bl, half=half: E.matmul(
                            bank(2 + 2 * bl + half),
                            lhsT, p2(P2["W2"] + fc * D + half * 512, 512),
                            start=(fc == 0), stop=False),
                            waits=w if (bl == 0 and half == 0) else ())

            ndtp = 16 * 8 * (ci // 2 + 1)
            for fc in range(32):
                w = [("dTP%d" % (ci % 2), ndtp)]
                if ci == 0 and fc % 8 == 0:
                    w.append(("dW1", 16 * 8 * (fc // 8 + 1)))
                    if fc == 0:
                        w.append(("V", v_wend))
                if fc >= 2:
                    w.append(("V", sc_h1[ci][fc - 2]))
                bk = fc % 2
                tlast = None
                for dc in range(8):
                    tlast = top(
                        lambda E, dc=dc, fc=fc, bk=bk, ci=ci, nb=nb: E.matmul(
                            bank(bk, 0, nb * 128),
                            p2(P2["W1"] + dc * FF + fc * 128, 128),
                            p2(P2["xnt"] + (ci % 2) * 3072 + dc * 384, nb * 128),
                            start=(dc == 0), stop=(dc == 7)),
                        waits=w if dc == 0 else (), inc=(dc == 7))
                t_m4fc = tlast
                sc_h1[ci][fc] = vop(
                    lambda E, fc=fc, bk=bk, nb=nb: E.tensor_scalar(
                        p2(P2["h1t"] + (fc % 2) * 384, nb * 128),
                        bank(bk, 0, nb * 128), b1tl[:, fc:fc + 1], 0.0,
                        ALU.add, ALU.max),
                    waits=[("T", t_m4fc), ("dB1", 16 * 32)])
                if fc >= 1:
                    m5p(fc - 1)
                drain(1)
            t_m4end[ci] = cnt["T"]
            m5p(31)
            for bl in range(nb):
                top(lambda E, bl=bl: E.matmul(
                    bank(2 + 2 * bl), ONES, ROWB[0:1, 0:512],
                    start=False, stop=True))
                t_m5[ci][bl] = top(lambda E, bl=bl: E.matmul(
                    bank(3 + 2 * bl), ONES, ROWB[0:1, 512:1024],
                    start=False, stop=True), inc=True)

        def emit_tail_eo(ci):
            b0, nb = F_CHUNKS[ci]
            for bl in range(nb):
                beta = b0 + bl
                wcp = [("T", t_m5[ci][bl])]
                if debug and bl == 0 and ci >= 1:
                    pb0, pnb = F_CHUNKS[ci - 1]
                    if dbgxf_cnt[pb0 + pnb - 1] is not None:
                        wcp.append(("dVT", dbgxf_cnt[pb0 + pnb - 1]))
                v_xf = vop(lambda E, bl=bl: E.tensor_copy(
                    p4(P4["xfb"] + bl * D, D),
                    pball[:, (2 + 2 * bl) * 512:(2 + 2 * bl) * 512 + 1024]),
                    waits=wcp)
                if debug:
                    dbgxf_cnt[beta] = v_xf
            relV_eo[ci] = cnt["V"]
            for bl in range(min(nb, 2)):
                xtl_load(b0 + bl)

        def xtl_load(beta):
            w = []
            if beta >= 2:
                w = [("V", xtl_rel[beta - 2])]
                assert xtl_rel[beta - 2] is not None
            dxt_cnt[beta] = dma("sync", "dXT%d" % (beta % 2),
                                lambda E, beta=beta: E.dma_start(
                out=p4(P4["xtl"] + (beta % 2) * D, D),
                in_=x_t.ap()[beta * 128:(beta + 1) * 128, :]), waits=w)

        def rest_thunks(ci):
            b0, nb = F_CHUNKS[ci]
            thunks = []

            def blockE1(bl):
                beta = b0 + bl
                mu1 = mu1v[ci]
                r1 = r1v[ci]
                vop(lambda E, beta=beta, bl=bl, mu1=mu1, r1=r1: E.tensor_scalar(
                    p4(P4["t1"], D), p4(P4["xtl"] + (beta % 2) * D, D),
                    mu1[:, bl:bl + 1], r1[:, bl:bl + 1], ALU.subtract, ALU.mult),
                    waits=[("dXT%d" % (beta % 2), dxt_cnt[beta])])
                xtl_rel[beta] = cnt["V"]
                if bl + 2 < nb:
                    xtl_load(b0 + bl + 2)

            def blockE2(bl):
                beta = b0 + bl
                v_xf = vop(lambda E, bl=bl: E.tensor_tensor(
                    p4(P4["xfb"] + bl * D, D), p4(P4["xfb"] + bl * D, D),
                    p4(P4["t1"], D), ALU.add))
                if debug:
                    dbgxf_cnt[beta] = dma(
                        "gpsimd", "dVT", lambda E, beta=beta, bl=bl: E.dma_start(
                            out=dxf_t.ap()[beta * 128:(beta + 1) * 128, :],
                            in_=p4(P4["xfb"] + bl * D, D)),
                        waits=[("V", v_xf), ("dVT", dvt_chain[0])])
                    dvt_chain[0] = dbgxf_cnt[beta]

            def blockE3(bl):
                vop(lambda E, bl=bl: E.reduce_sum(
                    p4(P4["stats"] + ST["s2sum"] + bl, 1),
                    p4(P4["xfb"] + bl * D, D), AX.X))
                sop(lambda E, bl=bl: E.activation(
                    p4(P4["sqscr"], D), p4(P4["xfb"] + bl * D, D), AF.Square,
                    accum_out=p4(P4["stats"] + ST["s2sq"] + bl, 1)),
                    waits=[("V", cnt["V"])])
            for bl in range(nb):
                thunks.append(lambda bl=bl: blockE1(bl))
                thunks.append(lambda bl=bl: blockE2(bl))
                thunks.append(lambda bl=bl: blockE3(bl))

            stv = lambda k, n=nb: p4(P4["stats"] + k, n)

            def rstd2_a():
                vop(lambda E: E.tensor_scalar(stv(ST["mu2"]), stv(ST["s2sum"]),
                                              1.0 / D, None, ALU.mult),
                    waits=[("SC", cnt["SC"])])
                vop(lambda E: E.tensor_tensor(stv(ST["vt2"]), stv(ST["mu2"]),
                                              stv(ST["mu2"]), ALU.mult))
                vop(lambda E: E.scalar_tensor_tensor(
                    stv(ST["vt2"]), stv(ST["s2sq"]), 1.0 / D, stv(ST["vt2"]),
                    ALU.mult, ALU.subtract))
                vop(lambda E: E.tensor_scalar(stv(ST["vt2"]), stv(ST["vt2"]),
                                              EPS, None, ALU.add))

            def rstd2_b():
                _rsqrt(vop, stv(ST["vt2"]), stv(ST["r2"]), stv(ST["yA"]),
                       stv(ST["yB"]), nb)
            thunks.append(rstd2_a)
            thunks.append(rstd2_b)

            def blockG1(bl):
                vop(lambda E, bl=bl: E.scalar_tensor_tensor(
                    p4(P4["t1"], D), p4(P4["xfb"] + bl * D, D),
                    p4(P4["stats"] + ST["mu2"] + bl, 1), GT,
                    ALU.subtract, ALU.mult))

            def blockG2(bl):
                beta = b0 + bl
                wg = []
                if beta >= 1:
                    wg = [("SC", sc_silu[beta - 1])]
                v_gp = vop(lambda E, bl=bl: E.scalar_tensor_tensor(
                    p4(P4["gpre"], D), p4(P4["t1"], D),
                    p4(P4["stats"] + ST["r2"] + bl, 1), BT,
                    ALU.mult, ALU.add), waits=wg)
                w2_ = [("V", v_gp)]
                if beta >= 1:
                    w2_.append(("dG", dg_cnt[beta - 1]))
                sc_silu[beta] = sop(lambda E: E.activation(
                    p4(P4["gbuf"], D), p4(P4["gpre"], D), AF.Silu), waits=w2_)
                dg_cnt[beta] = dma("scalar", "dG", lambda E, beta=beta: E.dma_start(
                    out=s_g.ap()[beta * 128:(beta + 1) * 128, :],
                    in_=p4(P4["gbuf"], D)), waits=[("SC", sc_silu[beta])])
            for bl in range(nb):
                thunks.append(lambda bl=bl: blockG1(bl))
                thunks.append(lambda bl=bl: blockG2(bl))
            return thunks

        from collections import deque
        if not skip_f:
            pending = deque()
            for it in range(NCH + 2):
                if it < NCH:
                    pending.extend(front_thunks(it))
                if 0 <= it - 2 < NCH:
                    pending.extend(rest_thunks(it - 2))
                if 0 <= it - 1 < NCH:
                    emit_mid(it - 1, pending)
                    emit_tail_eo(it - 1)
                else:
                    while pending:
                        pending.popleft()()
            while pending:
                pending.popleft()()

        tEndF = cnt["T"]
        vEndF = cnt["V"]
        scEndF = cnt["SC"]
        gpEndF = cnt["GP"]
        BARRIER = [("T", tEndF), ("V", vEndF), ("SC", scEndF), ("GP", gpEndF)]

        for dc in range(8):
            dma("gpsimd", "dWO", lambda E, dc=dc: E.dma_start(
                out=p2(P2["wout"] + dc * D, D),
                in_=wout_t.ap()[dc * 128:(dc + 1) * 128, :]),
                waits=list(BARRIER) + [("dHP", 16)])
        dma("gpsimd", "dROW", lambda E: E.dma_start(out=ROWB, in_=bout_t.ap()),
            waits=BARRIER)
        dROW_bout = cnt["dROW"]

        t_m1 = [[None] * 32 for _ in range(BC)]
        t_m1end = [None] * BC
        t_m2 = [{} for _ in range(BC)]
        t_m2end = [None] * BC
        t_m3 = [[None] * 16 for _ in range(BC)]
        t_m3end = [None] * BC
        v_zevac = [[None] * 32 for _ in range(BC)]
        v_a4 = [{} for _ in range(BC)]
        v_a4end = [None] * BC
        v_tail = [[None] * 16 for _ in range(BC)]
        dwi_cnt = [[None] * 32 for _ in range(BC)]
        dxta_cnt = [None] * BC
        dvt_cnt = [[None] * 16 for _ in range(BC)]
        dvn_cnt = [None] * BC
        dgi_cnt = [[None] * 16 for _ in range(BC)]
        gp_tail = [[None] * 16 for _ in range(BC)]
        dxi_cnt = [[None] * 16 for _ in range(BC)]
        dy_cnt = [[None] * 16 for _ in range(BC)]
        gp_vt = [[None] * 16 for _ in range(BC)]

        def wi_load(b, li):
            ecg = li % 16
            st2 = ecg // 8
            ec = ecg % 8
            src = win2_t if st2 else win0_t
            w2_ = []
            if li >= 2:
                t_prev = t_m1[b][li - 2]
                assert t_prev is not None
                w2_.append(("T", t_prev))
            elif b >= 1:
                t_prev = t_m1[b - 1][30 + li]
                assert t_prev is not None
                w2_.append(("T", t_prev))
            else:
                dwi_cnt[0][li] = wi0_cnt[li]
                return
            dwi_cnt[b][li] = dma(
                "gpsimd", "dWI%d" % (li % 2),
                lambda E, src=src, ec=ec, li=li: E.dma_start(
                    out=p2(P2["wiA"] + (li % 2) * D, D),
                    in_=bass.AP(src, ec * 128,
                                [[D, 128], [128 * D, 8], [1, 128]])), waits=w2_)

        def emit_prep(b):
            if b == 0:
                w = [("T", tEndF),
                     ("dSCw0", cnt["dSCw0"]), ("dSCw1", cnt["dSCw1"])]
            else:
                w = list(BARRIER) + [("dSCw0", cnt["dSCw0"]),
                                     ("dSCw1", cnt["dSCw1"])]
            if b >= 1:
                w.append(("T", t_m1end[b - 1]))
            for dc in range(8):
                dxta_cnt[b] = dma("sync", "dXTA", lambda E, dc=dc, b=b: E.dma_start(
                    out=p2(P2["xntA"] + dc * S, S),
                    in_=s_xnbf.ap()[b * S:(b + 1) * S, dc * 128:(dc + 1) * 128],
                    transpose=True), waits=w if dc == 0 else ())
            for li in range(2):
                wi_load(b, li)

        rr_cnt = [0]
        rr_evac = {}

        def emit_a2(b, pending):
            if b == 0:
                wm = list(BARRIER) + [("dG", cnt["dG"])]
                if debug and dbgxf_cnt[NBLK - 1] is not None:
                    wm.append(("dVT", dbgxf_cnt[NBLK - 1]))
                vop(lambda E: E.memset(p4(P4["stage"], 2), 0.0), waits=wm)
            last_sct = None
            vfin_last = [None]
            HS = 1024
            for half in range(2):
                for ecg in range(16):
                    st2 = ecg // 8
                    ec = ecg % 8
                    li = half * 16 + ecg
                    assert dwi_cnt[b][li] is not None
                    w = [("dWI%d" % (li % 2), dwi_cnt[b][li]),
                         ("dXTA", dxta_cnt[b])]
                    if b == 0 and li == 0:
                        w.append(("V", relV_eo[NCH - 1]))
                        w.append(("V", relV_eo[NCH - 2]))
                    rr = rr_cnt[0]
                    rr_cnt[0] += 1
                    p = 4 + 2 * (rr % 2)
                    wr = list(w)
                    if rr >= 2:
                        wr.append(("V", rr_evac[rr - 2]))
                    first = True
                    tlast = None
                    for ss in (0, 1):
                        for dc in range(8):
                            tlast = top(
                                lambda E, ss=ss, dc=dc, p=p, half=half, li=li:
                                E.matmul(
                                    bank(p + ss),
                                    p2(P2["wiA"] + (li % 2) * D + dc * 128, 128),
                                    p2(P2["xntA"] + dc * S +
                                       (half * 2 + ss) * 512, 512),
                                    start=(dc == 0), stop=(dc == 7)),
                                waits=wr if first else (),
                                inc=(ss == 1 and dc == 7))
                            first = False
                    bin_off = SV["bin2"] if st2 else SV["bin0"]
                    rr_evac[rr] = vop(
                        lambda E, p=p, half=half, bin_off=bin_off, ec=ec:
                        E.tensor_scalar(
                            p4(P4["stage"] + 2 + half * HS, HS),
                            pball[:, p * 512:p * 512 + 1024],
                            sv1(bin_off + ec), None, ALU.add),
                        waits=[("T", tlast), ("dVEC", dvec_total)] +
                              ([("SC", last_sct)] if last_sct is not None
                               else []))
                    t_m1[b][li] = tlast
                    v_zevac[b][li] = rr_evac[rr]
                    if li + 2 < 32:
                        wi_load(b, li + 2)
                    if half == 0 and ecg < 8:
                        for _ in range(2):
                            if pending:
                                pending.popleft()()
                    bc_off = SV["bc2"] if st2 else SV["bc0"]
                    tp = "t2" if st2 else "t0"
                    if half == 0:
                        sop(lambda E, ecg=ecg: E.activation(
                            p4(P4["carry"] + ecg * 2, 2),
                            p4(P4["stage"] + HS, 2), AF.Identity),
                            waits=[("V", v_zevac[b][li])])
                    else:
                        assert vfin_last[0] is not None
                        sop(lambda E, ecg=ecg: E.activation(
                            p4(P4["stage"] + HS, 2),
                            p4(P4["carry"] + ecg * 2, 2), AF.Identity),
                            waits=[("V", vfin_last[0])])
                    sc_t = sop(lambda E, tp=tp, bc_off=bc_off, ec=ec, half=half:
                               E.activation(
                        p4(P4["tconv"], HS), p4(P4["stage"] + 2 + half * HS, HS),
                        AF.Identity,
                        scale=sv1(SV[tp + "2"] + ec),
                        bias=sv1(bc_off + ec)),
                        waits=[("V", v_zevac[b][li]), ("dVEC", dvec_total)])
                    last_sct = sc_t
                    vop(lambda E, tp=tp, ec=ec, half=half:
                        E.scalar_tensor_tensor(
                        p4(P4["tconv"], HS), p4(P4["stage"] + 1 + half * HS, HS),
                        sv1(SV[tp + "1"] + ec), p4(P4["tconv"], HS),
                        ALU.mult, ALU.add), waits=[("SC", sc_t)])
                    if st2 == 0:
                        v_fin = vop(lambda E, tp=tp, ec=ec, half=half:
                                    E.scalar_tensor_tensor(
                            p2(P2["z0c"] + ec * S + half * HS, HS),
                            p4(P4["stage"] + half * HS, HS),
                            sv1(SV[tp + "0"] + ec), p4(P4["tconv"], HS),
                            ALU.mult, ALU.add))
                        vfin_last[0] = v_fin
                        if debug and b == 3 and half == 1:
                            dvt_chain[0] = dma(
                                "gpsimd", "dVT", lambda E, ec=ec: E.dma_start(
                                    out=dz0_t.ap()[ec * 128:(ec + 1) * 128, :],
                                    in_=p2(P2["z0c"] + ec * S, S)),
                                waits=[("V", v_fin), ("dVT", dvt_chain[0])])
                    else:
                        hx = half * 8 + ec
                        wz = []
                        if ec >= 1 and gp_vt[b][hx - 1] is not None:
                            wz = [("GP", gp_vt[b][hx - 1])]
                        v_fin = vop(lambda E, tp=tp, ec=ec, half=half:
                                    E.scalar_tensor_tensor(
                            p4(P4["z2c"] + half * HS, HS),
                            p4(P4["stage"] + half * HS, HS),
                            sv1(SV[tp + "0"] + ec), p4(P4["tconv"], HS),
                            ALU.mult, ALU.add), waits=wz)
                        vfin_last[0] = v_fin
                        wv = [("V", v_fin)]
                        if b >= 1:
                            wv.append(("T", t_m3end[b - 1]))
                        gp_vt[b][hx] = gop(lambda E, ec=ec, half=half:
                                           E.tensor_tensor(
                            p2(P2["vt"] + ec * S + half * HS, HS),
                            p4(P4["z2c"] + half * HS, HS),
                            p2(P2["z0c"] + ec * S + half * HS, HS),
                            ALU.mult), waits=wv)
                        wv2 = [("GP", gp_vt[b][hx]), ("dVT", dvt_chain[0])]
                        if b >= 1:
                            wv2.append(("dVN", dvn_cnt[b - 1]))
                        dvt_cnt[b][hx] = dma(
                            "gpsimd", "dVT", lambda E, ec=ec, half=half:
                            E.dma_start(
                                out=s_vt.ap()[ec * 128:(ec + 1) * 128,
                                              half * HS:half * HS + HS],
                                in_=p2(P2["vt"] + ec * S + half * HS, HS)),
                            waits=wv2)
                        dvt_chain[0] = dvt_cnt[b][hx]
                        if ec == 7:
                            wt_ = [("dVT", dvt_cnt[b][hx])]
                            if half == 0 and b >= 1:
                                wt_.append(("T", t_m2end[b - 1]))
                            for jj in range(8):
                                j = half * 8 + jj
                                dvn_cnt[b] = dma(
                                    "sync", "dVN", lambda E, j=j: E.dma_start(
                                        out=p2(P2["vnat"] + j * D, D),
                                        in_=s_vt.ap()[:, j * 128:(j + 1) * 128],
                                        transpose=True),
                                    waits=wt_ if jj == 0 else ())
            t_m1end[b] = t_m1[b][31]

        def emit_a4(b):
            started = set()
            seglist = []
            for dc in range(8):
                for j in range(16):
                    for d in range(9):
                        if j + d < 16:
                            seglist.append([dc, j, d, 1, d // 1 * 0 + (j + d) // 4, j + d])
            firsts = {}
            lasts = {}
            for k, (dc, j, d, dlen, G, i0) in enumerate(seglist):
                if (dc, G) not in firsts:
                    firsts[(dc, G)] = k
                lasts[(dc, G)] = k
            for k, (dc, j, d, dlen, G, i0) in enumerate(seglist):
                w = []
                if k == 0:
                    w = [("dTall", 16 * 128), ("dVN", cnt["dVN"])]
                    if b >= 1:
                        assert v_tail[b - 1][15] is not None
                        w.append(("V", v_tail[b - 1][15]))
                if firsts[(dc, G)] == k and dc >= 1:
                    w.append(("V", v_a4[b][(dc - 1, G)]))
                st = (firsts[(dc, G)] == k)
                sp = (lasts[(dc, G)] == k)
                tv = top(lambda E, dc=dc, j=j, d=d, dlen=dlen, G=G, i0=i0, st=st,
                         sp=sp: E.matmul(
                             bank(G, (i0 % 4) * 128, dlen * 128),
                             p2(P2["vnat"] + j * D + dc * 128, 128),
                             p2(P2["tall"] + d * 128, dlen * 128),
                             start=st, stop=sp),
                         waits=w, inc=sp)
                if sp:
                    t_m2[b][(dc, G)] = tv
                    vop(lambda E, dc=dc, G=G: E.scalar_tensor_tensor(
                        p4(P4["vfb"], 512), p2(P2["vt"] + dc * S + G * 512, 512),
                        sv1(SV["fb"] + dc), bank(G),
                        ALU.mult, ALU.add), waits=[("T", tv)])
                    v_a4[b][(dc, G)] = vop(lambda E, dc=dc, G=G: E.tensor_tensor(
                        p2(P2["vt"] + dc * S + G * 512, 512), p4(P4["vfb"], 512),
                        p2(P2["z0c"] + dc * S + G * 512, 512), ALU.mult))
            t_m2end[b] = cnt["T"]
            v_a4end[b] = cnt["V"]

        def a5_unit(b, i):
            par = i % 2
            w = [("V", v_a4[b][(7, i // 4)]),
                 ("V", v_a4[b][(7, par * 2)]),
                 ("V", v_a4[b][(7, par * 2 + 1)]),
                 ("dWO", 16 * 8), ("dROW", dROW_bout)]
            if i >= 2 and v_tail[b][i - 2] is not None:
                w.append(("V", v_tail[b][i - 2]))
            elif i < 2 and b >= 1:
                w.append(("V", v_tail[b - 1][14 + i]))
            if b == 0 and i < 2:
                w.append(("V", vEndF))
            first = True
            for dc in range(8):
                for half in range(2):
                    top(lambda E, dc=dc, half=half, par=par, i=i: E.matmul(
                        bank(par * 2 + half),
                        p2(P2["vt"] + dc * S + i * 128, 128),
                        p2(P2["wout"] + dc * D + half * 512, 512),
                        start=(dc == 0), stop=False),
                        waits=w if first else ())
                    first = False
            top(lambda E, par=par: E.matmul(
                bank(par * 2), ONES, ROWB[0:1, 0:512],
                start=False, stop=True))
            t_m3[b][i] = top(lambda E, par=par: E.matmul(
                bank(par * 2 + 1), ONES, ROWB[0:1, 512:1024],
                start=False, stop=True), inc=True)
            wg = []
            wx = []
            if i >= 2 and v_tail[b][i - 2] is not None:
                wg = [("V", v_tail[b][i - 2])]
                wx = [("GP", gp_tail[b][i - 2])]
            elif i < 2 and b >= 1:
                wg = [("V", v_tail[b - 1][14 + i])]
                wx = [("GP", gp_tail[b - 1][14 + i])]
            row0 = b * S + i * 128
            dgi_cnt[b][i] = dma("sync", "dGI%d" % (i % 2), lambda E, row0=row0, i=i:
                                E.dma_start(out=p4(P4["ginA"] + (i % 2) * D, D),
                                            in_=s_g.ap()[row0:row0 + 128, :]),
                                waits=wg)
            dxi_cnt[b][i] = dma("sync", "dXI%d" % (i % 2), lambda E, row0=row0, i=i:
                                E.dma_start(out=p4(P4["xinA"] + (i % 2) * D, D),
                                            in_=x_t.ap()[row0:row0 + 128, :]),
                                waits=wx)
            wv = [("T", t_m3[b][i]), ("dGI%d" % (i % 2), dgi_cnt[b][i])]
            if (b, i) != (0, 0):
                pi = (b, i - 1) if i >= 1 else (b - 1, 15)
                wv.append(("dY", dy_cnt[pi[0]][pi[1]]))
                wv.append(("GP", gp_tail[pi[0]][pi[1]]))
            v_tail[b][i] = vop(lambda E, i=i, par=par: E.scalar_tensor_tensor(
                p4(P4["obuf"], D), p4(P4["ginA"] + (i % 2) * D, D), 1.0,
                pball[:, (par * 2) * 512:(par * 2) * 512 + 1024],
                ALU.mult, ALU.mult), waits=wv)
            gp_tail[b][i] = gop(lambda E, i=i: E.tensor_tensor(
                p4(P4["obuf"], D), p4(P4["obuf"], D),
                p4(P4["xinA"] + (i % 2) * D, D), ALU.add),
                waits=[("V", v_tail[b][i]), ("dXI%d" % (i % 2), dxi_cnt[b][i])])
            dy_cnt[b][i] = dma("scalar", "dY", lambda E, row0=row0: E.dma_start(
                out=y_t.ap()[row0:row0 + 128, :], in_=p4(P4["obuf"], D)),
                waits=[("GP", gp_tail[b][i])])
            if i == 15:
                t_m3end[b] = t_m3[b][15]

        def a5_thunks(b):
            return [(lambda b=b, i=i: a5_unit(b, i)) for i in range(16)]

        emit_prep(0)
        for b in range(128):
            dma("sync", "dTall", lambda E, b=b: E.dma_start(
                out=TALL[b:b + 1, :],
                in_=bass.AP(s_hpad, 127 - b, [[1, 1], [1, 1152]])),
                waits=[("dHP", 16)] if b == 0 else ())
        pendingA = deque()
        for b in range(BC):
            emit_a2(b, pendingA)
            assert not pendingA
            if b + 1 < BC:
                emit_prep(b + 1)
            emit_a4(b)
            pendingA.extend(a5_thunks(b))
        while pendingA:
            pendingA.popleft()()

        wait("sync", "dY", 16 * NBLK)
        if debug:
            dma("sync", "dDBG", lambda E: E.dma_start(out=dvt_t.ap(), in_=s_vt.ap()),
                waits=[("dY", 16 * NBLK)])
            for dc in range(8):
                dma("sync", "dDBG", lambda E, dc=dc: E.dma_start(
                    out=dwt_t.ap()[dc * 128:(dc + 1) * 128, :],
                    in_=p2(P2["vt"] + dc * S, S)))
            for dc in range(8):
                dma("sync", "dDBG", lambda E, dc=dc: E.dma_start(
                    out=dxa_t.ap()[dc * 128:(dc + 1) * 128, :],
                    in_=p2(P2["xntA"] + dc * S, S)))
            dma("sync", "dDBG", lambda E: E.dma_start(
                out=dsv_t.ap(), in_=svtl[:, 0:88]))
            dma("sync", "dDBG", lambda E: E.dma_start(
                out=dwi_t.ap(), in_=p2(P2["wi"], 2048)))
            dma("sync", "dDBG", lambda E: E.dma_start(out=dxn_t.ap(), in_=s_xnbf.ap()),
                waits=[("dSCw0", cnt["dSCw0"]), ("dSCw1", cnt["dSCw1"])])
            dma("sync", "dDBG", lambda E: E.dma_start(out=dg_t.ap(), in_=s_g.ap()),
                waits=[("dG", cnt["dG"])])
            wait("sync", "dDBG", cnt["dDBG"])

        @blk.sync
        def _(E):
            for f in qs["sync"]:
                f(E)

        @blk.tensor
        def _(E):
            for f in qs["tensor"]:
                f(E)

        @blk.vector
        def _(E):
            for f in qs["vector"]:
                f(E)

        @blk.scalar
        def _(E):
            for f in qs["scalar"]:
                f(E)

        @blk.gpsimd
        def _(E):
            for f in qs["gpsimd"]:
                f(E)

    return nc


def _rsqrt(vop, c_ap, out_ap, ya, yb, nb):
    ci = c_ap.bitcast(I32)
    yi = ya.bitcast(I32)
    vop(lambda E: E.tensor_scalar(yi, ci, 1, None, ALU.arith_shift_right))
    vop(lambda E: E.tensor_scalar(yi, yi, -1, MAGIC, ALU.mult, ALU.add))
    cur = ya
    other = yb
    for _ in range(2):
        vop(lambda E, cur=cur, other=other: E.tensor_tensor(other, cur, cur, ALU.mult))
        vop(lambda E, other=other: E.tensor_tensor(other, other, c_ap, ALU.mult))
        vop(lambda E, other=other: E.tensor_scalar(other, other, -0.5, 1.5,
                                                   ALU.mult, ALU.add))
        vop(lambda E, cur=cur, other=other: E.tensor_tensor(other, cur, other,
                                                            ALU.mult))
        cur, other = other, cur
    vop(lambda E, cur=cur: E.tensor_copy(out_ap, cur))


def x_slice(t, idx):
    return t.ap()[idx * 128:(idx + 1) * 128, :]


_NC_CACHE = {}


def kernel(L0_attn_x, L0_attn_w_in, L0_attn_b_in, L0_attn_w_conv, L0_attn_b_conv,
           L0_attn_w1, L0_attn_b1, L0_attn_w2, L0_attn_b2, L0_attn_w3,
           L0_attn_filt_bias, L0_attn_w_out, L0_attn_b_out, L0_ffn_router_w,
           L0_ffn_router_b, L0_ffn_w1, L0_ffn_b1, L0_ffn_w2, L0_ffn_b2,
           L0_ffn_gamma, L0_ffn_beta):
    if "nc" not in _NC_CACHE:
        _NC_CACHE["nc"] = build()
    nc = _NC_CACHE["nc"]
    A = np.ascontiguousarray
    shared = {
        "win0": A(L0_attn_w_in[:, 0:1024], dtype=np.float32),
        "win2": A(L0_attn_w_in[:, 2048:3072], dtype=np.float32),
        "bin0": A(L0_attn_b_in[0:1024], dtype=np.float32),
        "bin2": A(L0_attn_b_in[2048:3072], dtype=np.float32),
        "wc0": A(L0_attn_w_conv[0:1024, 0, :], dtype=np.float32),
        "wc2": A(L0_attn_w_conv[2048:3072, 0, :], dtype=np.float32),
        "bc0": A(L0_attn_b_conv[0:1024], dtype=np.float32),
        "bc2": A(L0_attn_b_conv[2048:3072], dtype=np.float32),
        "ab1": A(L0_attn_b1, dtype=np.float32),
        "aw2": A(L0_attn_w2, dtype=np.float32),
        "ab2": A(L0_attn_b2, dtype=np.float32),
        "aw3": A(L0_attn_w3, dtype=np.float32),
        "fb": A(L0_attn_filt_bias, dtype=np.float32),
        "wout": A(L0_attn_w_out, dtype=np.float32),
        "bout": A(L0_attn_b_out, dtype=np.float32).reshape(1, 1024),
        "w1f": A(L0_ffn_w1[0], dtype=np.float32),
        "b1f": A(L0_ffn_b1[0], dtype=np.float32),
        "w2f": A(L0_ffn_w2[0], dtype=np.float32),
        "b2f": A(L0_ffn_b2[0], dtype=np.float32).reshape(1, 1024),
        "gam": A(L0_ffn_gamma, dtype=np.float32).reshape(1, 1024),
        "bet": A(L0_ffn_beta, dtype=np.float32).reshape(1, 1024),
    }
    x = np.asarray(L0_attn_x, dtype=np.float32)
    in_maps = []
    for c in range(8):
        m = dict(shared)
        m["x"] = A(x[c * BC:(c + 1) * BC]).reshape(NTOK, D)
        in_maps.append(m)
    res = run_bass_kernel_spmd(nc, in_maps, core_ids=list(range(8)))
    y = np.concatenate([r["y"].reshape(BC, S, D) for r in res.results], axis=0)
    return y


# version 17
# speedup vs baseline: 1.0764x; 1.0764x over previous
import numpy as np
import ml_dtypes
from contextlib import ExitStack

import concourse.bass as bass
import concourse.mybir as mybir
from concourse.bass_utils import run_bass_kernel_spmd

F32 = mybir.dt.float32
BF16 = mybir.dt.bfloat16
I32 = mybir.dt.int32
AF = mybir.ActivationFunctionType
ALU = mybir.AluOpType
AX = mybir.AxisListType

BC = 4
S = 2048
D = 1024
NTOK = BC * S
NBLK = NTOK // 128
FF = 4096
EPS = 1e-5
MAGIC = 0x5F3759DF

F_CHUNKS = [(i * 3, 3) for i in range(21)] + [(63, 1)]

P2 = {}
P2["W1"] = 0
P2["W2"] = 32768
P2["xnt"] = 65536
P2["h1t"] = 71680
P2["xnb"] = 72448
P2["gt"] = 74496
P2["bt"] = 75520
P2["tall"] = 76544
P2["rows2"] = 77696
P2_SIZE = 80896
P2["xntA"] = 0
P2["z0c"] = 16384
P2["vt"] = 32768
P2["vnat"] = 49152
P2["wout"] = 65536
P2["wi"] = 73728
P2["wiA"] = 78848
P2["hpad"] = 65536
P2["growb"] = 66816
P2["betrb"] = 67840

P4 = {}
P4["xfr"] = 0
P4["xtl"] = 3072
P4["xfb"] = 5120
P4["t1"] = 8192
P4["gpre"] = 9216
P4["gbuf"] = 10240
P4["sqscr"] = 11264
P4["stats"] = 12288
P4_F_END = 12416
P4["stage"] = 0
P4["tconv"] = 2052
P4["z2c"] = 4100
P4["xinA"] = 6148
P4["ginA"] = 8196
P4["obuf"] = 10244
P4["vfb"] = 11268
P4["svec"] = 11780
P4["carry"] = 11940
P4_SIZE = 12416
P4["aw3"] = P4["xfb"]
P4["grow"] = P4["t1"]
P4["brow"] = P4["xfb"] + 1024
P4["aw2"] = P4["gpre"]
P4["wvec"] = P4["gbuf"]

SV = {"t00": 0, "t01": 8, "t02": 16, "t20": 24, "t21": 32, "t22": 40,
      "bin0": 48, "bin2": 56, "bc0": 64, "bc2": 72, "fb": 80}

ST_MU1 = lambda slot: 24 * 0 + slot * 8
ST_R1 = lambda slot: slot * 8 + 4
ST_TMP = 24
ST = {"ssum": 24, "ssq": 27, "vtmp": 30, "yA": 36, "yB": 39,
      "mu2": 42, "r2": 45, "s2sum": 48, "s2sq": 51, "vt2": 54}
ST_NB = lambda slot: 57 + slot * 3

SEMS = ["V", "SC", "GP", "T", "dX0", "dX1", "dXT0", "dXT1", "dSCw0", "dSCw1",
        "dX2", "dTP0", "dTP1", "dW1", "dW2", "dWI0", "dWI1", "dWO", "dG", "dGI0",
        "dGI1", "dXI0", "dXI1", "dVT", "dVN", "dY", "dROW", "dTall", "dHP",
        "dB1", "dXTA", "dVEC", "dDBG", "dVN2", "dXQ"]


def build(debug=False):
    import os as _os
    skip_f = bool(_os.environ.get("MKDEV_NOF")) and debug
    nc = bass.Bass("TRN2", target_bir_lowering=False)

    x_t = nc.dram_tensor("x", [NTOK, D], F32, kind="ExternalInput")
    y_t = nc.dram_tensor("y", [NTOK, D], F32, kind="ExternalOutput")
    win0_t = nc.dram_tensor("win0", [D, D], F32, kind="ExternalInput")
    win2_t = nc.dram_tensor("win2", [D, D], F32, kind="ExternalInput")
    bin0_t = nc.dram_tensor("bin0", [D], F32, kind="ExternalInput")
    bin2_t = nc.dram_tensor("bin2", [D], F32, kind="ExternalInput")
    wc0_t = nc.dram_tensor("wc0", [D, 3], F32, kind="ExternalInput")
    wc2_t = nc.dram_tensor("wc2", [D, 3], F32, kind="ExternalInput")
    bc0_t = nc.dram_tensor("bc0", [D], F32, kind="ExternalInput")
    bc2_t = nc.dram_tensor("bc2", [D], F32, kind="ExternalInput")
    ab1_t = nc.dram_tensor("ab1", [64], F32, kind="ExternalInput")
    aw2_t = nc.dram_tensor("aw2", [64, 64], F32, kind="ExternalInput")
    ab2_t = nc.dram_tensor("ab2", [64], F32, kind="ExternalInput")
    aw3_t = nc.dram_tensor("aw3", [64, D], F32, kind="ExternalInput")
    fb_t = nc.dram_tensor("fb", [D], F32, kind="ExternalInput")
    wout_t = nc.dram_tensor("wout", [D, D], F32, kind="ExternalInput")
    bout_t = nc.dram_tensor("bout", [1, D], F32, kind="ExternalInput")
    w1f_t = nc.dram_tensor("w1f", [D, FF], F32, kind="ExternalInput")
    b1f_t = nc.dram_tensor("b1f", [FF], F32, kind="ExternalInput")
    w2f_t = nc.dram_tensor("w2f", [FF, D], F32, kind="ExternalInput")
    b2f_t = nc.dram_tensor("b2f", [1, D], F32, kind="ExternalInput")
    gam_t = nc.dram_tensor("gam", [1, D], F32, kind="ExternalInput")
    bet_t = nc.dram_tensor("bet", [1, D], F32, kind="ExternalInput")

    if skip_f:
        s_xnbf = nc.dram_tensor("xnin", [NTOK, D], BF16, kind="ExternalInput")
    else:
        s_xnbf = nc.dram_tensor("s_xnbf", [NTOK, D], BF16, kind="Internal")
    s_g = nc.dram_tensor("s_g", [NTOK, D], F32, kind="Internal")
    s_vt = nc.dram_tensor("s_vt", [D, S], BF16, kind="Internal")
    s_hpad = nc.dram_tensor("s_hpad", [1, 1280], BF16, kind="Internal")
    if debug:
        dxn_t = nc.dram_tensor("dbg_xn", [NTOK, D], BF16, kind="ExternalOutput")
        dg_t = nc.dram_tensor("dbg_g", [NTOK, D], F32, kind="ExternalOutput")
        dxf_t = nc.dram_tensor("dbg_xf", [NTOK, D], F32, kind="ExternalOutput")
        dvt_t = nc.dram_tensor("dbg_vt", [D, S], BF16, kind="ExternalOutput")
        dz0_t = nc.dram_tensor("dbg_z0c", [D, S], BF16, kind="ExternalOutput")
        dwt_t = nc.dram_tensor("dbg_wt", [D, S], BF16, kind="ExternalOutput")
        dxa_t = nc.dram_tensor("dbg_xnta", [D, S], BF16, kind="ExternalOutput")
        dsv_t = nc.dram_tensor("dbg_svec", [128, 88], F32, kind="ExternalOutput")
        dwi_t = nc.dram_tensor("dbg_wi", [128, 2048], BF16, kind="ExternalOutput")
        dst_t = nc.dram_tensor("dbg_stage", [128, 2048], F32, kind="ExternalOutput")

    with ExitStack() as stack:
        pool2 = stack.enter_context(nc.sbuf_tensor("pool2", [128, P2_SIZE], BF16))
        pool4 = stack.enter_context(nc.sbuf_tensor("pool4", [128, P4_SIZE], F32))
        b1tl = stack.enter_context(nc.sbuf_tensor("b1tl", [128, 32], F32))
        svtl = stack.enter_context(nc.sbuf_tensor("svtl", [128, 128], F32))
        pball = stack.enter_context(nc.psum_tensor("pball", [128, 4096], F32))
        SH = {n: stack.enter_context(nc.semaphore(n)) for n in SEMS}
        blk = stack.enter_context(nc.Block())

        def p2(off, n, p0=0, np_=128):
            return pool2[p0:p0 + np_, off:off + n]

        def p4(off, n, p0=0, np_=128):
            return pool4[p0:p0 + np_, off:off + n]

        def sv1(off):
            return svtl[:, off:off + 1]

        def bank(k, off=0, n=512):
            return pball[:, k * 512 + off:k * 512 + off + n]

        ONES = pool2[0:1, P2["rows2"]:P2["rows2"] + 128]
        ROWB = pool2[0:1, P2["rows2"] + 128:P2["rows2"] + 128 + D]

        qs = {e: [] for e in ("sync", "tensor", "vector", "scalar", "gpsimd")}
        cnt = {n: 0 for n in SEMS}
        hw = {}

        def wait(eng, sem, val):
            if val is None or val <= 0:
                return
            assert val <= cnt[sem], f"wait {sem}>={val} but only {cnt[sem]} emitted"
            key = (eng, sem)
            if hw.get(key, 0) >= val:
                return
            hw[key] = val
            qs[eng].append(lambda E, sem=sem, val=val: E.wait_ge(SH[sem], val))

        def vop(fn, waits=()):
            for s, v in waits:
                wait("vector", s, v)
            wait("vector", "V", cnt["V"])
            cnt["V"] += 1
            qs["vector"].append(lambda E, fn=fn: fn(E).then_inc(SH["V"], 1))
            return cnt["V"]

        def sop(fn, waits=()):
            for s, v in waits:
                wait("scalar", s, v)
            wait("scalar", "SC", cnt["SC"])
            cnt["SC"] += 1
            qs["scalar"].append(lambda E, fn=fn: fn(E).then_inc(SH["SC"], 1))
            return cnt["SC"]

        def gop(fn, waits=()):
            for s, v in waits:
                wait("gpsimd", s, v)
            wait("gpsimd", "GP", cnt["GP"])
            cnt["GP"] += 1
            qs["gpsimd"].append(lambda E, fn=fn: fn(E).then_inc(SH["GP"], 1))
            return cnt["GP"]

        def top(fn, waits=(), inc=False):
            for s, v in waits:
                wait("tensor", s, v)
            if inc:
                cnt["T"] += 1
                qs["tensor"].append(lambda E, fn=fn: fn(E).then_inc(SH["T"], 1))
                return cnt["T"]
            qs["tensor"].append(lambda E, fn=fn: fn(E))
            return None

        def dma(eng, sem, fn, waits=()):
            for s, v in waits:
                wait(eng, s, v)
            cnt[sem] += 16
            qs[eng].append(lambda E, fn=fn, sem=sem: fn(E).then_inc(SH[sem], 16))
            return cnt[sem]

        AW2 = p4(P4["aw2"], 64, 0, 64)
        AW3 = p4(P4["aw3"], D, 0, 64)
        AB1 = p4(P4["wvec"] + 0, 1, 0, 64)
        AB2 = p4(P4["wvec"] + 1, 1, 0, 64)
        HM1 = p4(P4["wvec"] + 2, 1, 0, 64)
        HM2 = p4(P4["wvec"] + 3, 1, 0, 64)
        HM2P = p4(P4["wvec"] + 4, 1, 0, 64)
        HPAD = p2(P2["hpad"], 1280, 0, 1)
        GT = p2(P2["gt"], D)
        BT = p2(P2["bt"], D)
        TALL = p2(P2["tall"], 1152)

        dma("gpsimd", "dROW", lambda E: E.dma_start(
            out=AB1, in_=ab1_t.ap().rearrange("(a b) -> a b", b=1)))
        dma("gpsimd", "dROW", lambda E: E.dma_start(out=AW2, in_=aw2_t.ap()))
        dma("gpsimd", "dROW", lambda E: E.dma_start(
            out=AB2, in_=ab2_t.ap().rearrange("(a b) -> a b", b=1)))
        dma("gpsimd", "dROW", lambda E: E.dma_start(out=AW3, in_=aw3_t.ap()))
        dma("gpsimd", "dROW", lambda E: E.dma_start(
            out=p2(P2["growb"], D, 0, 1), in_=gam_t.ap()))
        dma("gpsimd", "dROW", lambda E: E.dma_start(
            out=p2(P2["betrb"], D, 0, 1), in_=bet_t.ap()))
        dROW_b2 = dma("gpsimd", "dROW", lambda E: E.dma_start(out=ROWB, in_=b2f_t.ap()))
        for name, src, off in (("bin0", bin0_t, SV["bin0"]), ("bin2", bin2_t, SV["bin2"]),
                               ("bc0", bc0_t, SV["bc0"]), ("bc2", bc2_t, SV["bc2"]),
                               ("fb", fb_t, SV["fb"])):
            for dc in range(8):
                dma("gpsimd", "dVEC", lambda E, src=src, off=off, dc=dc: E.dma_start(
                    out=sv1(off + dc),
                    in_=bass.AP(src, dc * 128, [[1, 128], [1, 1]])))
        for stn, wct in (("t0", wc0_t), ("t2", wc2_t)):
            for k in range(3):
                for dc in range(8):
                    off = SV[stn + str(k)] + dc
                    dma("gpsimd", "dVEC", lambda E, wct=wct, k=k, dc=dc, off=off:
                        E.dma_start(out=sv1(off),
                                    in_=bass.AP(wct, dc * 128 * 3 + k, [[3, 128], [1, 1]]),
                                    allow_slow_non_contiguous=True))
        dvec_total = cnt["dVEC"]
        for fc in range(32):
            dma("gpsimd", "dB1", lambda E, fc=fc: E.dma_start(
                out=b1tl[:, fc:fc + 1],
                in_=bass.AP(b1f_t, fc * 128, [[1, 128], [1, 1]])))

        v_ones = vop(lambda E: E.memset(ONES, 1.0))

        sop(lambda E: E.activation(HM1, AB1, AF.Silu), waits=[("dROW", dROW_b2)])
        top(lambda E: E.matmul(pball[0:64, 0:1], AW2, HM1, start=True, stop=True),
            waits=[("dROW", dROW_b2), ("SC", cnt["SC"])], inc=True)
        vop(lambda E: E.tensor_scalar(HM2P, pball[0:64, 0:1], AB2, None, ALU.add),
            waits=[("T", cnt["T"]), ("dROW", dROW_b2)])
        sop(lambda E: E.activation(HM2, HM2P, AF.Silu), waits=[("V", cnt["V"])])
        top(lambda E: E.matmul(pball[0:1, 512:1024], HM2, AW3[:, 0:512],
                               start=True, stop=True), waits=[("SC", cnt["SC"])])
        t_h = top(lambda E: E.matmul(pball[0:1, 1024:1536], HM2, AW3[:, 512:1024],
                                     start=True, stop=True), inc=True)
        vop(lambda E: E.memset(HPAD, 0.0))
        vop(lambda E: E.tensor_copy(HPAD[0:1, 127:639], pball[0:1, 512:1024]),
            waits=[("T", t_h)])
        v_hp = vop(lambda E: E.tensor_copy(HPAD[0:1, 639:1151], pball[0:1, 1024:1536]))
        dma("gpsimd", "dHP", lambda E: E.dma_start(out=s_hpad.ap(), in_=HPAD),
            waits=[("V", v_hp)])
        top(lambda E: E.matmul(bank(0), ONES, p2(P2["growb"], 512, 0, 1),
                               start=True, stop=True),
            waits=[("dROW", dROW_b2), ("V", max(v_ones, v_hp))])
        t_g = top(lambda E: E.matmul(bank(1), ONES, p2(P2["growb"] + 512, 512, 0, 1),
                                     start=True, stop=True), inc=True)
        vop(lambda E: E.tensor_copy(GT[:, 0:512], bank(0)), waits=[("T", t_g)])
        v_gt = vop(lambda E: E.tensor_copy(GT[:, 512:1024], bank(1)))
        top(lambda E: E.matmul(bank(2), ONES, p2(P2["betrb"], 512, 0, 1),
                               start=True, stop=True), waits=[("V", v_gt)])
        t_b = top(lambda E: E.matmul(bank(3), ONES, p2(P2["betrb"] + 512, 512, 0, 1),
                                     start=True, stop=True), inc=True)
        vop(lambda E: E.tensor_copy(BT[:, 0:512], bank(2)), waits=[("T", t_b)])
        v_bt = vop(lambda E: E.tensor_copy(BT[:, 512:1024], bank(3)))
        v_wend = v_bt

        if not skip_f:
            for g in range(4):
                dma("gpsimd", "dW1",
                    lambda E, g=g: E.dma_start(
                        out=bass.AP(pool2, P2["W1"] + g * 1024,
                                    [[P2_SIZE, 128], [FF, 8], [1, 1024]]),
                        in_=bass.AP(w1f_t, g * 1024,
                                    [[FF, 128], [128 * FF, 8], [1, 1024]])),
                    waits=[("dW1", 16 * g)] if g else ())
                dma("gpsimd", "dW2",
                    lambda E, g=g: E.dma_start(
                        out=bass.AP(pool2, P2["W2"] + g * 8 * D,
                                    [[P2_SIZE, 128], [D, 8], [1, D]]),
                        in_=bass.AP(w2f_t, g * 8 * 128 * D,
                                    [[D, 128], [128 * D, 8], [1, D]])),
                    waits=[("dW2", 16 * g)] if g else ())
            wi0_cnt = [None, None]
            for li in range(2):
                src0 = win2_t if (li % 16) // 8 else win0_t
                wi0_cnt[li] = dma(
                    "gpsimd", "dWI%d" % li, lambda E, src0=src0, li=li:
                    E.dma_start(
                        out=p2(P2["wiA"] + li * D, D),
                        in_=bass.AP(src0, (li % 8) * 128,
                                    [[D, 128], [128 * D, 8], [1, 128]])))

        tW = cnt["T"]
        NCH = len(F_CHUNKS)
        relV_eo = [None] * NCH
        t_m4end = [None] * NCH
        sc_h1 = [[None] * 32 for _ in range(NCH)]
        t_m5 = [[None] * 3 for _ in range(NCH)]
        sc_cast = [None] * NBLK
        xfr_rel = [None] * NBLK
        xfr_relv = [None] * NBLK
        xtl_rel = [None] * NBLK
        dscw_cnt = [None] * NBLK
        dx_cnt = [None] * NBLK
        dxt_cnt = [None] * NBLK
        dg_cnt = [None] * NBLK
        sc_silu = [None] * NBLK
        dbgxf_cnt = [None] * NBLK
        dvt_chain = [0]
        mu1v = {}
        r1v = {}

        def front_thunks(ci):
            b0, nb = F_CHUNKS[ci]
            slot = ci % 3
            st_ = {}
            thunks = []

            def blockA(bl):
                beta = b0 + bl
                w = []
                if beta >= 3:
                    w = [("SC", xfr_rel[beta - 3]), ("V", xfr_relv[beta - 3])]
                dx_cnt[beta] = dma("sync", "dX%d" % (beta % 3),
                                   lambda E, beta=beta: E.dma_start(
                    out=p4(P4["xfr"] + (beta % 3) * D, D),
                    in_=x_t.ap()[beta * 128:(beta + 1) * 128, :]), waits=w)
                xfr_relv[beta] = vop(
                    lambda E, beta=beta, bl=bl: E.reduce_sum(
                        p4(P4["stats"] + ST["ssum"] + bl, 1),
                        p4(P4["xfr"] + (beta % 3) * D, D), AX.X),
                    waits=[("dX%d" % (beta % 3), dx_cnt[beta])])
                wsq = [("dX%d" % (beta % 3), dx_cnt[beta])]
                if beta == 0:
                    wsq.append(("T", tW))
                sop(lambda E, beta=beta, bl=bl: E.activation(
                    p4(P4["sqscr"], D), p4(P4["xfr"] + (beta % 3) * D, D),
                    AF.Square, accum_out=p4(P4["stats"] + ST["ssq"] + bl, 1)),
                    waits=wsq)
            for bl in range(nb):
                thunks.append(lambda bl=bl: blockA(bl))

            stv = lambda k, n=nb: p4(P4["stats"] + k, n)

            def rstd_a():
                mu1 = p4(P4["stats"] + ST_MU1(slot), nb)
                mu1v[ci] = mu1
                r1v[ci] = p4(P4["stats"] + ST_R1(slot), nb)
                vop(lambda E, mu1=mu1: E.tensor_scalar(mu1, stv(ST["ssum"]), 1.0 / D,
                                              None, ALU.mult),
                    waits=[("SC", cnt["SC"])])
                vop(lambda E, mu1=mu1: E.tensor_tensor(stv(ST["vtmp"]), mu1, mu1,
                                                       ALU.mult))
                vop(lambda E: E.scalar_tensor_tensor(
                    stv(ST["vtmp"]), stv(ST["ssq"]), 1.0 / D, stv(ST["vtmp"]),
                    ALU.mult, ALU.subtract))
                vop(lambda E: E.tensor_scalar(stv(ST["vtmp"]), stv(ST["vtmp"]),
                                              EPS, None, ALU.add))

            def rstd_b():
                r1 = r1v[ci]
                _rsqrt(vop, stv(ST["vtmp"]), r1, stv(ST["yA"]), stv(ST["yB"]), nb)
                nbias = p4(P4["stats"] + ST_NB(slot), nb)
                vop(lambda E, nbias=nbias, r1=r1: E.scalar_tensor_tensor(
                    nbias, mu1v[ci], -1.0, r1, ALU.mult, ALU.mult))
                st_["v_stats"] = cnt["V"]
            thunks.append(rstd_a)
            thunks.append(rstd_b)

            def blockC(bl):
                beta = b0 + bl
                r1 = r1v[ci]
                w = [("V", st_["v_stats"])]
                if beta >= 2:
                    w.append(("dSCw%d" % (beta % 2), dscw_cnt[beta - 2]))
                sc_cast[beta] = sop(
                    lambda E, beta=beta, bl=bl, r1=r1, slot=slot: E.activation(
                        p2(P2["xnb"] + (beta % 2) * D, D),
                        p4(P4["xfr"] + (beta % 3) * D, D), AF.Identity,
                        bias=p4(P4["stats"] + ST_NB(slot) + bl, 1),
                        scale=r1[:, bl:bl + 1]), waits=w)
                xfr_rel[beta] = sc_cast[beta]
                dscw_cnt[beta] = dma(
                    "scalar", "dSCw%d" % (beta % 2), lambda E, beta=beta: E.dma_start(
                        out=s_xnbf.ap()[beta * 128:(beta + 1) * 128, :],
                        in_=p2(P2["xnb"] + (beta % 2) * D, D)),
                    waits=[("SC", sc_cast[beta])])
            for bl in range(nb):
                thunks.append(lambda bl=bl: blockC(bl))

            def transposes():
                w = [("dSCw%d" % ((b0 + k) % 2), dscw_cnt[b0 + k]) for k in range(nb)]
                if ci == 0:
                    w.append(("dHP", 16))
                if ci >= 2:
                    w.append(("T", t_m4end[ci - 2]))
                for dc in range(8):
                    dma("sync", "dTP%d" % (ci % 2),
                        lambda E, dc=dc, b0=b0, nb=nb, ci=ci: E.dma_start(
                        out=p2(P2["xnt"] + (ci % 2) * 3072 + dc * 384, nb * 128),
                        in_=s_xnbf.ap()[b0 * 128:(b0 + nb) * 128,
                                        dc * 128:(dc + 1) * 128],
                        transpose=True), waits=w if dc == 0 else ())
            thunks.append(transposes)
            return thunks

        def emit_mid(ci, pending=None):
            b0, nb = F_CHUNKS[ci]
            def drain(k=1):
                for _ in range(k):
                    if pending:
                        pending.popleft()()
            dtp_need = 16 * 8 * (ci + 1) + 16 * 128
            def m5p(fc):
                w = [("V", sc_h1[ci][fc])]
                if ci == 0 and fc % 8 == 0:
                    w.append(("dW2", 16 * (fc // 8 + 1)))
                if fc == 0:
                    w.append(("dROW", dROW_b2))
                    if ci == 0:
                        w.append(("V", v_wend))
                    if ci >= 1 and relV_eo[ci - 1] is not None:
                        w.append(("V", relV_eo[ci - 1]))
                for bl in range(nb):
                    lhsT = p2(P2["h1t"] + (fc % 2) * 384 + bl * 128, 128)
                    for half in range(2):
                        top(lambda E, lhsT=lhsT, fc=fc, bl=bl, half=half: E.matmul(
                            bank(2 + 2 * bl + half),
                            lhsT, p2(P2["W2"] + fc * D + half * 512, 512),
                            start=(fc == 0), stop=False),
                            waits=w if (bl == 0 and half == 0) else ())

            ndtp = 16 * 8 * (ci // 2 + 1)
            for fc in range(32):
                w = [("dTP%d" % (ci % 2), ndtp)]
                if ci == 0 and fc % 8 == 0:
                    w.append(("dW1", 16 * (fc // 8 + 1)))
                    if fc == 0:
                        w.append(("V", v_wend))
                if fc >= 2:
                    w.append(("V", sc_h1[ci][fc - 2]))
                bk = fc % 2
                tlast = None
                for dc in range(8):
                    tlast = top(
                        lambda E, dc=dc, fc=fc, bk=bk, ci=ci, nb=nb: E.matmul(
                            bank(bk, 0, nb * 128),
                            p2(P2["W1"] + dc * FF + fc * 128, 128),
                            p2(P2["xnt"] + (ci % 2) * 3072 + dc * 384, nb * 128),
                            start=(dc == 0), stop=(dc == 7)),
                        waits=w if dc == 0 else (), inc=(dc == 7))
                t_m4fc = tlast
                sc_h1[ci][fc] = vop(
                    lambda E, fc=fc, bk=bk, nb=nb: E.tensor_scalar(
                        p2(P2["h1t"] + (fc % 2) * 384, nb * 128),
                        bank(bk, 0, nb * 128), b1tl[:, fc:fc + 1], 0.0,
                        ALU.add, ALU.max),
                    waits=[("T", t_m4fc), ("dB1", 16 * 32)])
                if fc >= 1:
                    m5p(fc - 1)
                drain(1)
            t_m4end[ci] = cnt["T"]
            m5p(31)
            for bl in range(nb):
                top(lambda E, bl=bl: E.matmul(
                    bank(2 + 2 * bl), ONES, ROWB[0:1, 0:512],
                    start=False, stop=True))
                t_m5[ci][bl] = top(lambda E, bl=bl: E.matmul(
                    bank(3 + 2 * bl), ONES, ROWB[0:1, 512:1024],
                    start=False, stop=True), inc=True)

        def emit_tail_eo(ci):
            b0, nb = F_CHUNKS[ci]
            for bl in range(nb):
                beta = b0 + bl
                wcp = [("T", t_m5[ci][bl])]
                if debug and bl == 0 and ci >= 1:
                    pb0, pnb = F_CHUNKS[ci - 1]
                    if dbgxf_cnt[pb0 + pnb - 1] is not None:
                        wcp.append(("dVT", dbgxf_cnt[pb0 + pnb - 1]))
                v_xf = vop(lambda E, bl=bl: E.tensor_copy(
                    p4(P4["xfb"] + bl * D, D),
                    pball[:, (2 + 2 * bl) * 512:(2 + 2 * bl) * 512 + 1024]),
                    waits=wcp)
                if debug:
                    dbgxf_cnt[beta] = v_xf
            relV_eo[ci] = cnt["V"]
            for bl in range(min(nb, 2)):
                xtl_load(b0 + bl)

        def xtl_load(beta):
            w = []
            if beta >= 2:
                w = [("V", xtl_rel[beta - 2])]
                assert xtl_rel[beta - 2] is not None
            dxt_cnt[beta] = dma("sync", "dXT%d" % (beta % 2),
                                lambda E, beta=beta: E.dma_start(
                out=p4(P4["xtl"] + (beta % 2) * D, D),
                in_=x_t.ap()[beta * 128:(beta + 1) * 128, :]), waits=w)

        def rest_thunks(ci):
            b0, nb = F_CHUNKS[ci]
            thunks = []

            def blockE1(bl):
                beta = b0 + bl
                mu1 = mu1v[ci]
                r1 = r1v[ci]
                vop(lambda E, beta=beta, bl=bl, mu1=mu1, r1=r1: E.tensor_scalar(
                    p4(P4["t1"], D), p4(P4["xtl"] + (beta % 2) * D, D),
                    mu1[:, bl:bl + 1], r1[:, bl:bl + 1], ALU.subtract, ALU.mult),
                    waits=[("dXT%d" % (beta % 2), dxt_cnt[beta])])
                xtl_rel[beta] = cnt["V"]
                if bl + 2 < nb:
                    xtl_load(b0 + bl + 2)

            def blockE2(bl):
                beta = b0 + bl
                v_xf = vop(lambda E, bl=bl: E.tensor_tensor(
                    p4(P4["xfb"] + bl * D, D), p4(P4["xfb"] + bl * D, D),
                    p4(P4["t1"], D), ALU.add))
                if debug:
                    dbgxf_cnt[beta] = dma(
                        "gpsimd", "dVT", lambda E, beta=beta, bl=bl: E.dma_start(
                            out=dxf_t.ap()[beta * 128:(beta + 1) * 128, :],
                            in_=p4(P4["xfb"] + bl * D, D)),
                        waits=[("V", v_xf), ("dVT", dvt_chain[0])])
                    dvt_chain[0] = dbgxf_cnt[beta]

            def blockE3(bl):
                vop(lambda E, bl=bl: E.reduce_sum(
                    p4(P4["stats"] + ST["s2sum"] + bl, 1),
                    p4(P4["xfb"] + bl * D, D), AX.X))
                sop(lambda E, bl=bl: E.activation(
                    p4(P4["sqscr"], D), p4(P4["xfb"] + bl * D, D), AF.Square,
                    accum_out=p4(P4["stats"] + ST["s2sq"] + bl, 1)),
                    waits=[("V", cnt["V"])])
            for bl in range(nb):
                thunks.append(lambda bl=bl: blockE1(bl))
                thunks.append(lambda bl=bl: blockE2(bl))
                thunks.append(lambda bl=bl: blockE3(bl))

            stv = lambda k, n=nb: p4(P4["stats"] + k, n)

            def rstd2_a():
                vop(lambda E: E.tensor_scalar(stv(ST["mu2"]), stv(ST["s2sum"]),
                                              1.0 / D, None, ALU.mult),
                    waits=[("SC", cnt["SC"])])
                vop(lambda E: E.tensor_tensor(stv(ST["vt2"]), stv(ST["mu2"]),
                                              stv(ST["mu2"]), ALU.mult))
                vop(lambda E: E.scalar_tensor_tensor(
                    stv(ST["vt2"]), stv(ST["s2sq"]), 1.0 / D, stv(ST["vt2"]),
                    ALU.mult, ALU.subtract))
                vop(lambda E: E.tensor_scalar(stv(ST["vt2"]), stv(ST["vt2"]),
                                              EPS, None, ALU.add))

            def rstd2_b():
                _rsqrt(vop, stv(ST["vt2"]), stv(ST["r2"]), stv(ST["yA"]),
                       stv(ST["yB"]), nb)
            thunks.append(rstd2_a)
            thunks.append(rstd2_b)

            def blockG1(bl):
                vop(lambda E, bl=bl: E.scalar_tensor_tensor(
                    p4(P4["t1"], D), p4(P4["xfb"] + bl * D, D),
                    p4(P4["stats"] + ST["mu2"] + bl, 1), GT,
                    ALU.subtract, ALU.mult))

            def blockG2(bl):
                beta = b0 + bl
                wg = []
                if beta >= 1:
                    wg = [("SC", sc_silu[beta - 1])]
                v_gp = vop(lambda E, bl=bl: E.scalar_tensor_tensor(
                    p4(P4["gpre"], D), p4(P4["t1"], D),
                    p4(P4["stats"] + ST["r2"] + bl, 1), BT,
                    ALU.mult, ALU.add), waits=wg)
                w2_ = [("V", v_gp)]
                if beta >= 1:
                    w2_.append(("dG", dg_cnt[beta - 1]))
                sc_silu[beta] = sop(lambda E: E.activation(
                    p4(P4["gbuf"], D), p4(P4["gpre"], D), AF.Silu), waits=w2_)
                dg_cnt[beta] = dma("scalar", "dG", lambda E, beta=beta: E.dma_start(
                    out=s_g.ap()[beta * 128:(beta + 1) * 128, :],
                    in_=p4(P4["gbuf"], D)), waits=[("SC", sc_silu[beta])])
            for bl in range(nb):
                thunks.append(lambda bl=bl: blockG1(bl))
                thunks.append(lambda bl=bl: blockG2(bl))
            return thunks

        from collections import deque
        if not skip_f:
            pending = deque()
            for it in range(NCH + 2):
                if it < NCH:
                    pending.extend(front_thunks(it))
                if 0 <= it - 2 < NCH:
                    pending.extend(rest_thunks(it - 2))
                if 0 <= it - 1 < NCH:
                    emit_mid(it - 1, pending)
                    emit_tail_eo(it - 1)
                else:
                    while pending:
                        pending.popleft()()
            while pending:
                pending.popleft()()

        tEndF = cnt["T"]
        vEndF = cnt["V"]
        scEndF = cnt["SC"]
        gpEndF = cnt["GP"]
        BARRIER = [("T", tEndF), ("V", vEndF), ("SC", scEndF), ("GP", gpEndF)]

        for dc in range(8):
            dma("gpsimd", "dWO", lambda E, dc=dc: E.dma_start(
                out=p2(P2["wout"] + dc * D, D),
                in_=wout_t.ap()[dc * 128:(dc + 1) * 128, :]),
                waits=list(BARRIER) + [("dHP", 16)])
        dma("gpsimd", "dROW", lambda E: E.dma_start(out=ROWB, in_=bout_t.ap()),
            waits=BARRIER)
        dROW_bout = cnt["dROW"]

        t_m1 = [[None] * 32 for _ in range(BC)]
        t_m1end = [None] * BC
        t_m2 = [{} for _ in range(BC)]
        t_m2end = [None] * BC
        t_m3 = [[None] * 16 for _ in range(BC)]
        t_m3end = [None] * BC
        v_zevac = [[None] * 32 for _ in range(BC)]
        v_a4 = [{} for _ in range(BC)]
        v_a4end = [None] * BC
        v_tail = [[None] * 16 for _ in range(BC)]
        dwi_cnt = [[None] * 32 for _ in range(BC)]
        dxta_cnt = [None] * BC
        dvt_cnt = [[None] * 16 for _ in range(BC)]
        dvn_cnt = [None] * BC
        dgi_cnt = [[None] * 16 for _ in range(BC)]
        gp_tail = [[None] * 16 for _ in range(BC)]
        dxi_cnt = [[None] * 16 for _ in range(BC)]
        dy_cnt = [[None] * 16 for _ in range(BC)]
        gp_vt = [[None] * 16 for _ in range(BC)]

        def wi_load(b, li):
            ecg = li % 16
            st2 = ecg // 8
            ec = ecg % 8
            src = win2_t if st2 else win0_t
            w2_ = []
            if li >= 2:
                t_prev = t_m1[b][li - 2]
                assert t_prev is not None
                w2_.append(("T", t_prev))
            elif b >= 1:
                t_prev = t_m1[b - 1][30 + li]
                assert t_prev is not None
                w2_.append(("T", t_prev))
            else:
                dwi_cnt[0][li] = wi0_cnt[li]
                return
            dwi_cnt[b][li] = dma(
                "gpsimd", "dWI%d" % (li % 2),
                lambda E, src=src, ec=ec, li=li: E.dma_start(
                    out=p2(P2["wiA"] + (li % 2) * D, D),
                    in_=bass.AP(src, ec * 128,
                                [[D, 128], [128 * D, 8], [1, 128]])), waits=w2_)

        def emit_prep(b):
            if b == 0:
                w = [("T", tEndF),
                     ("dSCw0", cnt["dSCw0"]), ("dSCw1", cnt["dSCw1"])]
            else:
                w = list(BARRIER) + [("dSCw0", cnt["dSCw0"]),
                                     ("dSCw1", cnt["dSCw1"])]
            if b >= 1:
                w.append(("T", t_m1end[b - 1]))
            for dc in range(8):
                dxta_cnt[b] = dma("sync", "dXTA", lambda E, dc=dc, b=b: E.dma_start(
                    out=p2(P2["xntA"] + dc * S, S),
                    in_=s_xnbf.ap()[b * S:(b + 1) * S, dc * 128:(dc + 1) * 128],
                    transpose=True), waits=w if dc == 0 else ())
            for li in range(2):
                wi_load(b, li)

        rr_cnt = [0]
        rr_evac = {}

        def emit_a2(b, pending):
            if b == 0:
                wm = list(BARRIER) + [("dG", cnt["dG"])]
                if debug and dbgxf_cnt[NBLK - 1] is not None:
                    wm.append(("dVT", dbgxf_cnt[NBLK - 1]))
                vop(lambda E: E.memset(p4(P4["stage"], 2), 0.0), waits=wm)
            last_sct = None
            vfin_last = [None]
            HS = 1024
            for half in range(2):
                for ecg in range(16):
                    st2 = ecg // 8
                    ec = ecg % 8
                    li = half * 16 + ecg
                    assert dwi_cnt[b][li] is not None
                    w = [("dWI%d" % (li % 2), dwi_cnt[b][li]),
                         ("dXTA", dxta_cnt[b])]
                    if b == 0 and li == 0:
                        w.append(("V", relV_eo[NCH - 1]))
                        w.append(("V", relV_eo[NCH - 2]))
                    rr = rr_cnt[0]
                    rr_cnt[0] += 1
                    p = 4 + 2 * (rr % 2)
                    wr = list(w)
                    if rr >= 2:
                        wr.append(("V", rr_evac[rr - 2]))
                    first = True
                    tlast = None
                    for ss in (0, 1):
                        for dc in range(8):
                            tlast = top(
                                lambda E, ss=ss, dc=dc, p=p, half=half, li=li:
                                E.matmul(
                                    bank(p + ss),
                                    p2(P2["wiA"] + (li % 2) * D + dc * 128, 128),
                                    p2(P2["xntA"] + dc * S +
                                       (half * 2 + ss) * 512, 512),
                                    start=(dc == 0), stop=(dc == 7)),
                                waits=wr if first else (),
                                inc=(ss == 1 and dc == 7))
                            first = False
                    bin_off = SV["bin2"] if st2 else SV["bin0"]
                    rr_evac[rr] = vop(
                        lambda E, p=p, half=half, bin_off=bin_off, ec=ec:
                        E.tensor_scalar(
                            p4(P4["stage"] + 2 + half * HS, HS),
                            pball[:, p * 512:p * 512 + 1024],
                            sv1(bin_off + ec), None, ALU.add),
                        waits=[("T", tlast), ("dVEC", dvec_total)] +
                              ([("SC", last_sct)] if last_sct is not None
                               else []))
                    t_m1[b][li] = tlast
                    v_zevac[b][li] = rr_evac[rr]
                    if li + 2 < 32:
                        wi_load(b, li + 2)
                    if half == 0 and ecg < 8:
                        for _ in range(2):
                            if pending:
                                pending.popleft()()
                    bc_off = SV["bc2"] if st2 else SV["bc0"]
                    tp = "t2" if st2 else "t0"
                    if half == 0:
                        sop(lambda E, ecg=ecg: E.activation(
                            p4(P4["carry"] + ecg * 2, 2),
                            p4(P4["stage"] + HS, 2), AF.Identity),
                            waits=[("V", v_zevac[b][li])])
                    else:
                        assert vfin_last[0] is not None
                        sop(lambda E, ecg=ecg: E.activation(
                            p4(P4["stage"] + HS, 2),
                            p4(P4["carry"] + ecg * 2, 2), AF.Identity),
                            waits=[("V", vfin_last[0])])
                    sc_t = sop(lambda E, tp=tp, bc_off=bc_off, ec=ec, half=half:
                               E.activation(
                        p4(P4["tconv"], HS), p4(P4["stage"] + 2 + half * HS, HS),
                        AF.Identity,
                        scale=sv1(SV[tp + "2"] + ec),
                        bias=sv1(bc_off + ec)),
                        waits=[("V", v_zevac[b][li]), ("dVEC", dvec_total)])
                    last_sct = sc_t
                    vop(lambda E, tp=tp, ec=ec, half=half:
                        E.scalar_tensor_tensor(
                        p4(P4["tconv"], HS), p4(P4["stage"] + 1 + half * HS, HS),
                        sv1(SV[tp + "1"] + ec), p4(P4["tconv"], HS),
                        ALU.mult, ALU.add), waits=[("SC", sc_t)])
                    if st2 == 0:
                        v_fin = vop(lambda E, tp=tp, ec=ec, half=half:
                                    E.scalar_tensor_tensor(
                            p2(P2["z0c"] + ec * S + half * HS, HS),
                            p4(P4["stage"] + half * HS, HS),
                            sv1(SV[tp + "0"] + ec), p4(P4["tconv"], HS),
                            ALU.mult, ALU.add))
                        vfin_last[0] = v_fin
                        if debug and b == 3 and half == 1:
                            dvt_chain[0] = dma(
                                "gpsimd", "dVT", lambda E, ec=ec: E.dma_start(
                                    out=dz0_t.ap()[ec * 128:(ec + 1) * 128, :],
                                    in_=p2(P2["z0c"] + ec * S, S)),
                                waits=[("V", v_fin), ("dVT", dvt_chain[0])])
                    else:
                        hx = half * 8 + ec
                        wz = []
                        if ec >= 1 and gp_vt[b][hx - 1] is not None:
                            wz = [("GP", gp_vt[b][hx - 1])]
                        v_fin = vop(lambda E, tp=tp, ec=ec, half=half:
                                    E.scalar_tensor_tensor(
                            p4(P4["z2c"] + half * HS, HS),
                            p4(P4["stage"] + half * HS, HS),
                            sv1(SV[tp + "0"] + ec), p4(P4["tconv"], HS),
                            ALU.mult, ALU.add), waits=wz)
                        vfin_last[0] = v_fin
                        wv = [("V", v_fin)]
                        if b >= 1:
                            wv.append(("T", t_m3end[b - 1]))
                        gp_vt[b][hx] = gop(lambda E, ec=ec, half=half:
                                           E.tensor_tensor(
                            p2(P2["vt"] + ec * S + half * HS, HS),
                            p4(P4["z2c"] + half * HS, HS),
                            p2(P2["z0c"] + ec * S + half * HS, HS),
                            ALU.mult), waits=wv)
                        wv2 = [("GP", gp_vt[b][hx]), ("dVT", dvt_chain[0])]
                        if b >= 1:
                            wv2.append(("dVN", dvn_cnt[b - 1]))
                        dvt_cnt[b][hx] = dma(
                            "gpsimd", "dVT", lambda E, ec=ec, half=half:
                            E.dma_start(
                                out=s_vt.ap()[ec * 128:(ec + 1) * 128,
                                              half * HS:half * HS + HS],
                                in_=p2(P2["vt"] + ec * S + half * HS, HS)),
                            waits=wv2)
                        dvt_chain[0] = dvt_cnt[b][hx]
                        if ec == 7:
                            wt_ = [("dVT", dvt_cnt[b][hx])]
                            if half == 0 and b >= 1:
                                wt_.append(("T", t_m2end[b - 1]))
                            for jj in range(8):
                                j = half * 8 + jj
                                dvn_cnt[b] = dma(
                                    "sync", "dVN", lambda E, j=j: E.dma_start(
                                        out=p2(P2["vnat"] + j * D, D),
                                        in_=s_vt.ap()[:, j * 128:(j + 1) * 128],
                                        transpose=True),
                                    waits=wt_ if jj == 0 else ())
            t_m1end[b] = t_m1[b][31]

        def emit_a4(b):
            started = set()
            seglist = []
            for dc in range(8):
                for j in range(16):
                    for d in range(9):
                        if j + d < 16:
                            seglist.append([dc, j, d, 1, d // 1 * 0 + (j + d) // 4, j + d])
            firsts = {}
            lasts = {}
            for k, (dc, j, d, dlen, G, i0) in enumerate(seglist):
                if (dc, G) not in firsts:
                    firsts[(dc, G)] = k
                lasts[(dc, G)] = k
            for k, (dc, j, d, dlen, G, i0) in enumerate(seglist):
                w = []
                if k == 0:
                    w = [("dTall", 16 * 128), ("dVN", cnt["dVN"])]
                    if b >= 1:
                        assert v_tail[b - 1][15] is not None
                        w.append(("V", v_tail[b - 1][15]))
                if firsts[(dc, G)] == k and dc >= 1:
                    w.append(("V", v_a4[b][(dc - 1, G)]))
                st = (firsts[(dc, G)] == k)
                sp = (lasts[(dc, G)] == k)
                tv = top(lambda E, dc=dc, j=j, d=d, dlen=dlen, G=G, i0=i0, st=st,
                         sp=sp: E.matmul(
                             bank(G, (i0 % 4) * 128, dlen * 128),
                             p2(P2["vnat"] + j * D + dc * 128, 128),
                             p2(P2["tall"] + d * 128, dlen * 128),
                             start=st, stop=sp),
                         waits=w, inc=sp)
                if sp:
                    t_m2[b][(dc, G)] = tv
                    vop(lambda E, dc=dc, G=G: E.scalar_tensor_tensor(
                        p4(P4["vfb"], 512), p2(P2["vt"] + dc * S + G * 512, 512),
                        sv1(SV["fb"] + dc), bank(G),
                        ALU.mult, ALU.add), waits=[("T", tv)])
                    v_a4[b][(dc, G)] = vop(lambda E, dc=dc, G=G: E.tensor_tensor(
                        p2(P2["vt"] + dc * S + G * 512, 512), p4(P4["vfb"], 512),
                        p2(P2["z0c"] + dc * S + G * 512, 512), ALU.mult))
            t_m2end[b] = cnt["T"]
            v_a4end[b] = cnt["V"]

        def a5_unit(b, i):
            par = i % 2
            w = [("V", v_a4[b][(7, i // 4)]),
                 ("V", v_a4[b][(7, par * 2)]),
                 ("V", v_a4[b][(7, par * 2 + 1)]),
                 ("dWO", 16 * 8), ("dROW", dROW_bout)]
            if i >= 2 and v_tail[b][i - 2] is not None:
                w.append(("V", v_tail[b][i - 2]))
            elif i < 2 and b >= 1:
                w.append(("V", v_tail[b - 1][14 + i]))
            if b == 0 and i < 2:
                w.append(("V", vEndF))
            first = True
            for dc in range(8):
                for half in range(2):
                    top(lambda E, dc=dc, half=half, par=par, i=i: E.matmul(
                        bank(par * 2 + half),
                        p2(P2["vt"] + dc * S + i * 128, 128),
                        p2(P2["wout"] + dc * D + half * 512, 512),
                        start=(dc == 0), stop=False),
                        waits=w if first else ())
                    first = False
            top(lambda E, par=par: E.matmul(
                bank(par * 2), ONES, ROWB[0:1, 0:512],
                start=False, stop=True))
            t_m3[b][i] = top(lambda E, par=par: E.matmul(
                bank(par * 2 + 1), ONES, ROWB[0:1, 512:1024],
                start=False, stop=True), inc=True)
            wg = []
            wx = []
            if i >= 2 and v_tail[b][i - 2] is not None:
                wg = [("V", v_tail[b][i - 2])]
                wx = [("GP", gp_tail[b][i - 2])]
            elif i < 2 and b >= 1:
                wg = [("V", v_tail[b - 1][14 + i])]
                wx = [("GP", gp_tail[b - 1][14 + i])]
            row0 = b * S + i * 128
            dgi_cnt[b][i] = dma("sync", "dGI%d" % (i % 2), lambda E, row0=row0, i=i:
                                E.dma_start(out=p4(P4["ginA"] + (i % 2) * D, D),
                                            in_=s_g.ap()[row0:row0 + 128, :]),
                                waits=wg)
            dxi_cnt[b][i] = dma("sync", "dXI%d" % (i % 2), lambda E, row0=row0, i=i:
                                E.dma_start(out=p4(P4["xinA"] + (i % 2) * D, D),
                                            in_=x_t.ap()[row0:row0 + 128, :]),
                                waits=wx)
            wv = [("T", t_m3[b][i]), ("dGI%d" % (i % 2), dgi_cnt[b][i])]
            if (b, i) != (0, 0):
                pi = (b, i - 1) if i >= 1 else (b - 1, 15)
                wv.append(("dY", dy_cnt[pi[0]][pi[1]]))
                wv.append(("GP", gp_tail[pi[0]][pi[1]]))
            v_tail[b][i] = vop(lambda E, i=i, par=par: E.scalar_tensor_tensor(
                p4(P4["obuf"], D), p4(P4["ginA"] + (i % 2) * D, D), 1.0,
                pball[:, (par * 2) * 512:(par * 2) * 512 + 1024],
                ALU.mult, ALU.mult), waits=wv)
            gp_tail[b][i] = gop(lambda E, i=i: E.tensor_tensor(
                p4(P4["obuf"], D), p4(P4["obuf"], D),
                p4(P4["xinA"] + (i % 2) * D, D), ALU.add),
                waits=[("V", v_tail[b][i]), ("dXI%d" % (i % 2), dxi_cnt[b][i])])
            dy_cnt[b][i] = dma("scalar", "dY", lambda E, row0=row0: E.dma_start(
                out=y_t.ap()[row0:row0 + 128, :], in_=p4(P4["obuf"], D)),
                waits=[("GP", gp_tail[b][i])])
            if i == 15:
                t_m3end[b] = t_m3[b][15]

        def a5_thunks(b):
            return [(lambda b=b, i=i: a5_unit(b, i)) for i in range(16)]

        emit_prep(0)
        for b in range(128):
            dma("sync", "dTall", lambda E, b=b: E.dma_start(
                out=TALL[b:b + 1, :],
                in_=bass.AP(s_hpad, 127 - b, [[1, 1], [1, 1152]])),
                waits=[("dHP", 16)] if b == 0 else ())
        pendingA = deque()
        for b in range(BC):
            emit_a2(b, pendingA)
            assert not pendingA
            if b + 1 < BC:
                emit_prep(b + 1)
            emit_a4(b)
            pendingA.extend(a5_thunks(b))
        while pendingA:
            pendingA.popleft()()

        wait("sync", "dY", 16 * NBLK)
        if debug:
            dma("sync", "dDBG", lambda E: E.dma_start(out=dvt_t.ap(), in_=s_vt.ap()),
                waits=[("dY", 16 * NBLK)])
            for dc in range(8):
                dma("sync", "dDBG", lambda E, dc=dc: E.dma_start(
                    out=dwt_t.ap()[dc * 128:(dc + 1) * 128, :],
                    in_=p2(P2["vt"] + dc * S, S)))
            for dc in range(8):
                dma("sync", "dDBG", lambda E, dc=dc: E.dma_start(
                    out=dxa_t.ap()[dc * 128:(dc + 1) * 128, :],
                    in_=p2(P2["xntA"] + dc * S, S)))
            dma("sync", "dDBG", lambda E: E.dma_start(
                out=dsv_t.ap(), in_=svtl[:, 0:88]))
            dma("sync", "dDBG", lambda E: E.dma_start(
                out=dwi_t.ap(), in_=p2(P2["wi"], 2048)))
            dma("sync", "dDBG", lambda E: E.dma_start(out=dxn_t.ap(), in_=s_xnbf.ap()),
                waits=[("dSCw0", cnt["dSCw0"]), ("dSCw1", cnt["dSCw1"])])
            dma("sync", "dDBG", lambda E: E.dma_start(out=dg_t.ap(), in_=s_g.ap()),
                waits=[("dG", cnt["dG"])])
            wait("sync", "dDBG", cnt["dDBG"])

        @blk.sync
        def _(E):
            for f in qs["sync"]:
                f(E)

        @blk.tensor
        def _(E):
            for f in qs["tensor"]:
                f(E)

        @blk.vector
        def _(E):
            for f in qs["vector"]:
                f(E)

        @blk.scalar
        def _(E):
            for f in qs["scalar"]:
                f(E)

        @blk.gpsimd
        def _(E):
            for f in qs["gpsimd"]:
                f(E)

    return nc


def _rsqrt(vop, c_ap, out_ap, ya, yb, nb):
    ci = c_ap.bitcast(I32)
    yi = ya.bitcast(I32)
    vop(lambda E: E.tensor_scalar(yi, ci, 1, None, ALU.arith_shift_right))
    vop(lambda E: E.tensor_scalar(yi, yi, -1, MAGIC, ALU.mult, ALU.add))
    cur = ya
    other = yb
    for _ in range(2):
        vop(lambda E, cur=cur, other=other: E.tensor_tensor(other, cur, cur, ALU.mult))
        vop(lambda E, other=other: E.tensor_tensor(other, other, c_ap, ALU.mult))
        vop(lambda E, other=other: E.tensor_scalar(other, other, -0.5, 1.5,
                                                   ALU.mult, ALU.add))
        vop(lambda E, cur=cur, other=other: E.tensor_tensor(other, cur, other,
                                                            ALU.mult))
        cur, other = other, cur
    vop(lambda E, cur=cur: E.tensor_copy(out_ap, cur))


def x_slice(t, idx):
    return t.ap()[idx * 128:(idx + 1) * 128, :]


_NC_CACHE = {}


def kernel(L0_attn_x, L0_attn_w_in, L0_attn_b_in, L0_attn_w_conv, L0_attn_b_conv,
           L0_attn_w1, L0_attn_b1, L0_attn_w2, L0_attn_b2, L0_attn_w3,
           L0_attn_filt_bias, L0_attn_w_out, L0_attn_b_out, L0_ffn_router_w,
           L0_ffn_router_b, L0_ffn_w1, L0_ffn_b1, L0_ffn_w2, L0_ffn_b2,
           L0_ffn_gamma, L0_ffn_beta):
    if "nc" not in _NC_CACHE:
        _NC_CACHE["nc"] = build()
    nc = _NC_CACHE["nc"]
    A = np.ascontiguousarray
    shared = {
        "win0": A(L0_attn_w_in[:, 0:1024], dtype=np.float32),
        "win2": A(L0_attn_w_in[:, 2048:3072], dtype=np.float32),
        "bin0": A(L0_attn_b_in[0:1024], dtype=np.float32),
        "bin2": A(L0_attn_b_in[2048:3072], dtype=np.float32),
        "wc0": A(L0_attn_w_conv[0:1024, 0, :], dtype=np.float32),
        "wc2": A(L0_attn_w_conv[2048:3072, 0, :], dtype=np.float32),
        "bc0": A(L0_attn_b_conv[0:1024], dtype=np.float32),
        "bc2": A(L0_attn_b_conv[2048:3072], dtype=np.float32),
        "ab1": A(L0_attn_b1, dtype=np.float32),
        "aw2": A(L0_attn_w2, dtype=np.float32),
        "ab2": A(L0_attn_b2, dtype=np.float32),
        "aw3": A(L0_attn_w3, dtype=np.float32),
        "fb": A(L0_attn_filt_bias, dtype=np.float32),
        "wout": A(L0_attn_w_out, dtype=np.float32),
        "bout": A(L0_attn_b_out, dtype=np.float32).reshape(1, 1024),
        "w1f": A(L0_ffn_w1[0], dtype=np.float32),
        "b1f": A(L0_ffn_b1[0], dtype=np.float32),
        "w2f": A(L0_ffn_w2[0], dtype=np.float32),
        "b2f": A(L0_ffn_b2[0], dtype=np.float32).reshape(1, 1024),
        "gam": A(L0_ffn_gamma, dtype=np.float32).reshape(1, 1024),
        "bet": A(L0_ffn_beta, dtype=np.float32).reshape(1, 1024),
    }
    x = np.asarray(L0_attn_x, dtype=np.float32)
    in_maps = []
    for c in range(8):
        m = dict(shared)
        m["x"] = A(x[c * BC:(c + 1) * BC]).reshape(NTOK, D)
        in_maps.append(m)
    res = run_bass_kernel_spmd(nc, in_maps, core_ids=list(range(8)))
    y = np.concatenate([r["y"].reshape(BC, S, D) for r in res.results], axis=0)
    return y


# version 18
# speedup vs baseline: 1.0794x; 1.0028x over previous
import numpy as np
import ml_dtypes
from contextlib import ExitStack

import concourse.bass as bass
import concourse.mybir as mybir
from concourse.bass_utils import run_bass_kernel_spmd

F32 = mybir.dt.float32
BF16 = mybir.dt.bfloat16
I32 = mybir.dt.int32
AF = mybir.ActivationFunctionType
ALU = mybir.AluOpType
AX = mybir.AxisListType

BC = 4
S = 2048
D = 1024
NTOK = BC * S
NBLK = NTOK // 128
FF = 4096
EPS = 1e-5
MAGIC = 0x5F3759DF

F_CHUNKS = [(i * 3, 3) for i in range(21)] + [(63, 1)]

P2 = {}
P2["W1"] = 0
P2["W2"] = 32768
P2["xnt"] = 65536
P2["h1t"] = 71680
P2["xnb"] = 72448
P2["gt"] = 74496
P2["bt"] = 75520
P2["tall"] = 76544
P2["rows2"] = 77696
P2_SIZE = 80896
P2["xntA"] = 0
P2["z0c"] = 16384
P2["vt"] = 32768
P2["vnat"] = 49152
P2["wout"] = 65536
P2["wi"] = 73728
P2["wiA"] = 78848
P2["hpad"] = 65536
P2["growb"] = 66816
P2["betrb"] = 67840

P4 = {}
P4["xfr"] = 0
P4["xtl"] = 3072
P4["xfb"] = 5120
P4["t1"] = 8192
P4["gpre"] = 9216
P4["gbuf"] = 10240
P4["sqscr"] = 11264
P4["stats"] = 12288
P4_F_END = 12416
P4["stage"] = 0
P4["tconv"] = 2052
P4["z2c"] = 4100
P4["xinA"] = 6148
P4["ginA"] = 8196
P4["obuf"] = 10244
P4["vfb"] = 11268
P4["svec"] = 11780
P4["carry"] = 11940
P4_SIZE = 12416
P4["aw3"] = P4["xfb"]
P4["grow"] = P4["t1"]
P4["brow"] = P4["xfb"] + 1024
P4["aw2"] = P4["gpre"]
P4["wvec"] = P4["gbuf"]

SV = {"t00": 0, "t01": 8, "t02": 16, "t20": 24, "t21": 32, "t22": 40,
      "bin0": 48, "bin2": 56, "bc0": 64, "bc2": 72, "fb": 80}

ST_MU1 = lambda slot: 24 * 0 + slot * 8
ST_R1 = lambda slot: slot * 8 + 4
ST_TMP = 24
ST = {"ssum": 24, "ssq": 27, "vtmp": 30, "yA": 36, "yB": 39,
      "mu2": 42, "r2": 45, "s2sum": 48, "s2sq": 51, "vt2": 54}
ST_NB = lambda slot: 57 + slot * 3

SEMS = ["V", "SC", "GP", "T", "dX0", "dX1", "dXT0", "dXT1", "dSCw0", "dSCw1",
        "dX2", "dTP0", "dTP1", "dW1", "dW2", "dWI0", "dWI1", "dWO", "dG", "dGI0",
        "dGI1", "dXI0", "dXI1", "dVT", "dVN", "dY", "dROW", "dTall", "dHP",
        "dB1", "dXTA", "dVEC", "dDBG", "dVN2", "dXQ"]


def build(debug=False):
    import os as _os
    skip_f = bool(_os.environ.get("MKDEV_NOF")) and debug
    nc = bass.Bass("TRN2", target_bir_lowering=False)

    x_t = nc.dram_tensor("x", [NTOK, D], F32, kind="ExternalInput")
    y_t = nc.dram_tensor("y", [NTOK, D], F32, kind="ExternalOutput")
    win0_t = nc.dram_tensor("win0", [D, D], F32, kind="ExternalInput")
    win2_t = nc.dram_tensor("win2", [D, D], F32, kind="ExternalInput")
    bin0_t = nc.dram_tensor("bin0", [D], F32, kind="ExternalInput")
    bin2_t = nc.dram_tensor("bin2", [D], F32, kind="ExternalInput")
    wc0_t = nc.dram_tensor("wc0", [D, 3], F32, kind="ExternalInput")
    wc2_t = nc.dram_tensor("wc2", [D, 3], F32, kind="ExternalInput")
    bc0_t = nc.dram_tensor("bc0", [D], F32, kind="ExternalInput")
    bc2_t = nc.dram_tensor("bc2", [D], F32, kind="ExternalInput")
    ab1_t = nc.dram_tensor("ab1", [64], F32, kind="ExternalInput")
    aw2_t = nc.dram_tensor("aw2", [64, 64], F32, kind="ExternalInput")
    ab2_t = nc.dram_tensor("ab2", [64], F32, kind="ExternalInput")
    aw3_t = nc.dram_tensor("aw3", [64, D], F32, kind="ExternalInput")
    fb_t = nc.dram_tensor("fb", [D], F32, kind="ExternalInput")
    wout_t = nc.dram_tensor("wout", [D, D], F32, kind="ExternalInput")
    bout_t = nc.dram_tensor("bout", [1, D], F32, kind="ExternalInput")
    w1f_t = nc.dram_tensor("w1f", [D, FF], F32, kind="ExternalInput")
    b1f_t = nc.dram_tensor("b1f", [FF], F32, kind="ExternalInput")
    w2f_t = nc.dram_tensor("w2f", [FF, D], F32, kind="ExternalInput")
    b2f_t = nc.dram_tensor("b2f", [1, D], F32, kind="ExternalInput")
    gam_t = nc.dram_tensor("gam", [1, D], F32, kind="ExternalInput")
    bet_t = nc.dram_tensor("bet", [1, D], F32, kind="ExternalInput")

    if skip_f:
        s_xnbf = nc.dram_tensor("xnin", [NTOK, D], BF16, kind="ExternalInput")
    else:
        s_xnbf = nc.dram_tensor("s_xnbf", [NTOK, D], BF16, kind="Internal")
    s_g = nc.dram_tensor("s_g", [NTOK, D], F32, kind="Internal")
    s_vt = nc.dram_tensor("s_vt", [D, S], BF16, kind="Internal")
    s_hpad = nc.dram_tensor("s_hpad", [1, 1280], BF16, kind="Internal")
    if debug:
        dxn_t = nc.dram_tensor("dbg_xn", [NTOK, D], BF16, kind="ExternalOutput")
        dg_t = nc.dram_tensor("dbg_g", [NTOK, D], F32, kind="ExternalOutput")
        dxf_t = nc.dram_tensor("dbg_xf", [NTOK, D], F32, kind="ExternalOutput")
        dvt_t = nc.dram_tensor("dbg_vt", [D, S], BF16, kind="ExternalOutput")
        dz0_t = nc.dram_tensor("dbg_z0c", [D, S], BF16, kind="ExternalOutput")
        dwt_t = nc.dram_tensor("dbg_wt", [D, S], BF16, kind="ExternalOutput")
        dxa_t = nc.dram_tensor("dbg_xnta", [D, S], BF16, kind="ExternalOutput")
        dsv_t = nc.dram_tensor("dbg_svec", [128, 88], F32, kind="ExternalOutput")
        dwi_t = nc.dram_tensor("dbg_wi", [128, 2048], BF16, kind="ExternalOutput")
        dst_t = nc.dram_tensor("dbg_stage", [128, 2048], F32, kind="ExternalOutput")

    with ExitStack() as stack:
        pool2 = stack.enter_context(nc.sbuf_tensor("pool2", [128, P2_SIZE], BF16))
        pool4 = stack.enter_context(nc.sbuf_tensor("pool4", [128, P4_SIZE], F32))
        b1tl = stack.enter_context(nc.sbuf_tensor("b1tl", [128, 32], F32))
        svtl = stack.enter_context(nc.sbuf_tensor("svtl", [128, 128], F32))
        pball = stack.enter_context(nc.psum_tensor("pball", [128, 4096], F32))
        SH = {n: stack.enter_context(nc.semaphore(n)) for n in SEMS}
        blk = stack.enter_context(nc.Block())

        def p2(off, n, p0=0, np_=128):
            return pool2[p0:p0 + np_, off:off + n]

        def p4(off, n, p0=0, np_=128):
            return pool4[p0:p0 + np_, off:off + n]

        def sv1(off):
            return svtl[:, off:off + 1]

        def bank(k, off=0, n=512):
            return pball[:, k * 512 + off:k * 512 + off + n]

        ONES = pool2[0:1, P2["rows2"]:P2["rows2"] + 128]
        ROWB = pool2[0:1, P2["rows2"] + 128:P2["rows2"] + 128 + D]

        qs = {e: [] for e in ("sync", "tensor", "vector", "scalar", "gpsimd")}
        cnt = {n: 0 for n in SEMS}
        hw = {}

        def wait(eng, sem, val, future=False):
            if val is None or val <= 0:
                return
            assert future or val <= cnt[sem], \
                f"wait {sem}>={val} but only {cnt[sem]} emitted"
            key = (eng, sem)
            if hw.get(key, 0) >= val:
                return
            hw[key] = val
            qs[eng].append(lambda E, sem=sem, val=val: E.wait_ge(SH[sem], val))

        def vop(fn, waits=()):
            for s, v in waits:
                wait("vector", s, v)
            wait("vector", "V", cnt["V"])
            cnt["V"] += 1
            qs["vector"].append(lambda E, fn=fn: fn(E).then_inc(SH["V"], 1))
            return cnt["V"]

        def sop(fn, waits=()):
            for s, v in waits:
                wait("scalar", s, v)
            wait("scalar", "SC", cnt["SC"])
            cnt["SC"] += 1
            qs["scalar"].append(lambda E, fn=fn: fn(E).then_inc(SH["SC"], 1))
            return cnt["SC"]

        def gop(fn, waits=()):
            for s, v in waits:
                wait("gpsimd", s, v)
            wait("gpsimd", "GP", cnt["GP"])
            cnt["GP"] += 1
            qs["gpsimd"].append(lambda E, fn=fn: fn(E).then_inc(SH["GP"], 1))
            return cnt["GP"]

        def top(fn, waits=(), inc=False):
            for s, v in waits:
                wait("tensor", s, v)
            if inc:
                cnt["T"] += 1
                qs["tensor"].append(lambda E, fn=fn: fn(E).then_inc(SH["T"], 1))
                return cnt["T"]
            qs["tensor"].append(lambda E, fn=fn: fn(E))
            return None

        def dma(eng, sem, fn, waits=()):
            for s, v in waits:
                wait(eng, s, v)
            cnt[sem] += 16
            qs[eng].append(lambda E, fn=fn, sem=sem: fn(E).then_inc(SH[sem], 16))
            return cnt[sem]

        AW2 = p4(P4["aw2"], 64, 0, 64)
        AW3 = p4(P4["aw3"], D, 0, 64)
        AB1 = p4(P4["wvec"] + 0, 1, 0, 64)
        AB2 = p4(P4["wvec"] + 1, 1, 0, 64)
        HM1 = p4(P4["wvec"] + 2, 1, 0, 64)
        HM2 = p4(P4["wvec"] + 3, 1, 0, 64)
        HM2P = p4(P4["wvec"] + 4, 1, 0, 64)
        HPAD = p2(P2["hpad"], 1280, 0, 1)
        GT = p2(P2["gt"], D)
        BT = p2(P2["bt"], D)
        TALL = p2(P2["tall"], 1152)

        dma("gpsimd", "dROW", lambda E: E.dma_start(
            out=AB1, in_=ab1_t.ap().rearrange("(a b) -> a b", b=1)))
        dma("gpsimd", "dROW", lambda E: E.dma_start(out=AW2, in_=aw2_t.ap()))
        dma("gpsimd", "dROW", lambda E: E.dma_start(
            out=AB2, in_=ab2_t.ap().rearrange("(a b) -> a b", b=1)))
        dma("gpsimd", "dROW", lambda E: E.dma_start(out=AW3, in_=aw3_t.ap()))
        dma("gpsimd", "dROW", lambda E: E.dma_start(
            out=p2(P2["growb"], D, 0, 1), in_=gam_t.ap()))
        dma("gpsimd", "dROW", lambda E: E.dma_start(
            out=p2(P2["betrb"], D, 0, 1), in_=bet_t.ap()))
        dROW_b2 = dma("gpsimd", "dROW", lambda E: E.dma_start(out=ROWB, in_=b2f_t.ap()))
        for name, src, off in (("bin0", bin0_t, SV["bin0"]), ("bin2", bin2_t, SV["bin2"]),
                               ("bc0", bc0_t, SV["bc0"]), ("bc2", bc2_t, SV["bc2"]),
                               ("fb", fb_t, SV["fb"])):
            for dc in range(8):
                dma("gpsimd", "dVEC", lambda E, src=src, off=off, dc=dc: E.dma_start(
                    out=sv1(off + dc),
                    in_=bass.AP(src, dc * 128, [[1, 128], [1, 1]])))
        for stn, wct in (("t0", wc0_t), ("t2", wc2_t)):
            for k in range(3):
                for dc in range(8):
                    off = SV[stn + str(k)] + dc
                    dma("gpsimd", "dVEC", lambda E, wct=wct, k=k, dc=dc, off=off:
                        E.dma_start(out=sv1(off),
                                    in_=bass.AP(wct, dc * 128 * 3 + k, [[3, 128], [1, 1]]),
                                    allow_slow_non_contiguous=True))
        dvec_total = cnt["dVEC"]
        for fc in range(32):
            dma("gpsimd", "dB1", lambda E, fc=fc: E.dma_start(
                out=b1tl[:, fc:fc + 1],
                in_=bass.AP(b1f_t, fc * 128, [[1, 128], [1, 1]])))

        v_ones = vop(lambda E: E.memset(ONES, 1.0))

        sop(lambda E: E.activation(HM1, AB1, AF.Silu), waits=[("dROW", dROW_b2)])
        top(lambda E: E.matmul(pball[0:64, 0:1], AW2, HM1, start=True, stop=True),
            waits=[("dROW", dROW_b2), ("SC", cnt["SC"])], inc=True)
        vop(lambda E: E.tensor_scalar(HM2P, pball[0:64, 0:1], AB2, None, ALU.add),
            waits=[("T", cnt["T"]), ("dROW", dROW_b2)])
        sop(lambda E: E.activation(HM2, HM2P, AF.Silu), waits=[("V", cnt["V"])])
        top(lambda E: E.matmul(pball[0:1, 512:1024], HM2, AW3[:, 0:512],
                               start=True, stop=True), waits=[("SC", cnt["SC"])])
        t_h = top(lambda E: E.matmul(pball[0:1, 1024:1536], HM2, AW3[:, 512:1024],
                                     start=True, stop=True), inc=True)
        vop(lambda E: E.memset(HPAD, 0.0))
        vop(lambda E: E.tensor_copy(HPAD[0:1, 127:639], pball[0:1, 512:1024]),
            waits=[("T", t_h)])
        v_hp = vop(lambda E: E.tensor_copy(HPAD[0:1, 639:1151], pball[0:1, 1024:1536]))
        dma("gpsimd", "dHP", lambda E: E.dma_start(out=s_hpad.ap(), in_=HPAD),
            waits=[("V", v_hp)])
        top(lambda E: E.matmul(bank(0), ONES, p2(P2["growb"], 512, 0, 1),
                               start=True, stop=True),
            waits=[("dROW", dROW_b2), ("V", max(v_ones, v_hp))])
        t_g = top(lambda E: E.matmul(bank(1), ONES, p2(P2["growb"] + 512, 512, 0, 1),
                                     start=True, stop=True), inc=True)
        vop(lambda E: E.tensor_copy(GT[:, 0:512], bank(0)), waits=[("T", t_g)])
        v_gt = vop(lambda E: E.tensor_copy(GT[:, 512:1024], bank(1)))
        top(lambda E: E.matmul(bank(2), ONES, p2(P2["betrb"], 512, 0, 1),
                               start=True, stop=True), waits=[("V", v_gt)])
        t_b = top(lambda E: E.matmul(bank(3), ONES, p2(P2["betrb"] + 512, 512, 0, 1),
                                     start=True, stop=True), inc=True)
        vop(lambda E: E.tensor_copy(BT[:, 0:512], bank(2)), waits=[("T", t_b)])
        v_bt = vop(lambda E: E.tensor_copy(BT[:, 512:1024], bank(3)))
        v_wend = v_bt

        if not skip_f:
            for g in range(4):
                w1w = [("dW1", 16 * g)] if g else []
                if g == 1:
                    wait("gpsimd", "dTP0", 16 * 8, future=True)
                dma("gpsimd", "dW1",
                    lambda E, g=g: E.dma_start(
                        out=bass.AP(pool2, P2["W1"] + g * 1024,
                                    [[P2_SIZE, 128], [FF, 8], [1, 1024]]),
                        in_=bass.AP(w1f_t, g * 1024,
                                    [[FF, 128], [128 * FF, 8], [1, 1024]])),
                    waits=w1w)
                dma("gpsimd", "dW2",
                    lambda E, g=g: E.dma_start(
                        out=bass.AP(pool2, P2["W2"] + g * 8 * D,
                                    [[P2_SIZE, 128], [D, 8], [1, D]]),
                        in_=bass.AP(w2f_t, g * 8 * 128 * D,
                                    [[D, 128], [128 * D, 8], [1, D]])),
                    waits=[("dW2", 16 * g)] if g else ())
            wi0_cnt = [None, None]
            for li in range(2):
                src0 = win2_t if (li % 16) // 8 else win0_t
                wi0_cnt[li] = dma(
                    "gpsimd", "dWI%d" % li, lambda E, src0=src0, li=li:
                    E.dma_start(
                        out=p2(P2["wiA"] + li * D, D),
                        in_=bass.AP(src0, (li % 8) * 128,
                                    [[D, 128], [128 * D, 8], [1, 128]])))

        tW = cnt["T"]
        NCH = len(F_CHUNKS)
        relV_eo = [None] * NCH
        t_m4end = [None] * NCH
        sc_h1 = [[None] * 32 for _ in range(NCH)]
        t_m5 = [[None] * 3 for _ in range(NCH)]
        sc_cast = [None] * NBLK
        xfr_rel = [None] * NBLK
        xfr_relv = [None] * NBLK
        xtl_rel = [None] * NBLK
        dscw_cnt = [None] * NBLK
        dx_cnt = [None] * NBLK
        dxt_cnt = [None] * NBLK
        dg_cnt = [None] * NBLK
        sc_silu = [None] * NBLK
        dbgxf_cnt = [None] * NBLK
        dvt_chain = [0]
        mu1v = {}
        r1v = {}

        def front_thunks(ci):
            b0, nb = F_CHUNKS[ci]
            slot = ci % 3
            st_ = {}
            thunks = []

            def blockA(bl):
                beta = b0 + bl
                w = []
                if beta >= 3:
                    w = [("SC", xfr_rel[beta - 3]), ("V", xfr_relv[beta - 3])]
                dx_cnt[beta] = dma("sync", "dX%d" % (beta % 3),
                                   lambda E, beta=beta: E.dma_start(
                    out=p4(P4["xfr"] + (beta % 3) * D, D),
                    in_=x_t.ap()[beta * 128:(beta + 1) * 128, :]), waits=w)
                xfr_relv[beta] = vop(
                    lambda E, beta=beta, bl=bl: E.reduce_sum(
                        p4(P4["stats"] + ST["ssum"] + bl, 1),
                        p4(P4["xfr"] + (beta % 3) * D, D), AX.X),
                    waits=[("dX%d" % (beta % 3), dx_cnt[beta])])
                wsq = [("dX%d" % (beta % 3), dx_cnt[beta])]
                if beta == 0:
                    wsq.append(("T", tW))
                sop(lambda E, beta=beta, bl=bl: E.activation(
                    p4(P4["sqscr"], D), p4(P4["xfr"] + (beta % 3) * D, D),
                    AF.Square, accum_out=p4(P4["stats"] + ST["ssq"] + bl, 1)),
                    waits=wsq)
            for bl in range(nb):
                thunks.append(lambda bl=bl: blockA(bl))

            stv = lambda k, n=nb: p4(P4["stats"] + k, n)

            def rstd_a():
                mu1 = p4(P4["stats"] + ST_MU1(slot), nb)
                mu1v[ci] = mu1
                r1v[ci] = p4(P4["stats"] + ST_R1(slot), nb)
                vop(lambda E, mu1=mu1: E.tensor_scalar(mu1, stv(ST["ssum"]), 1.0 / D,
                                              None, ALU.mult),
                    waits=[("SC", cnt["SC"])])
                vop(lambda E, mu1=mu1: E.tensor_tensor(stv(ST["vtmp"]), mu1, mu1,
                                                       ALU.mult))
                vop(lambda E: E.scalar_tensor_tensor(
                    stv(ST["vtmp"]), stv(ST["ssq"]), 1.0 / D, stv(ST["vtmp"]),
                    ALU.mult, ALU.subtract))
                vop(lambda E: E.tensor_scalar(stv(ST["vtmp"]), stv(ST["vtmp"]),
                                              EPS, None, ALU.add))

            def rstd_b():
                r1 = r1v[ci]
                _rsqrt(vop, stv(ST["vtmp"]), r1, stv(ST["yA"]), stv(ST["yB"]), nb)
                nbias = p4(P4["stats"] + ST_NB(slot), nb)
                vop(lambda E, nbias=nbias, r1=r1: E.scalar_tensor_tensor(
                    nbias, mu1v[ci], -1.0, r1, ALU.mult, ALU.mult))
                st_["v_stats"] = cnt["V"]
            thunks.append(rstd_a)
            thunks.append(rstd_b)

            def blockC(bl):
                beta = b0 + bl
                r1 = r1v[ci]
                w = [("V", st_["v_stats"])]
                if beta >= 2:
                    w.append(("dSCw%d" % (beta % 2), dscw_cnt[beta - 2]))
                sc_cast[beta] = sop(
                    lambda E, beta=beta, bl=bl, r1=r1, slot=slot: E.activation(
                        p2(P2["xnb"] + (beta % 2) * D, D),
                        p4(P4["xfr"] + (beta % 3) * D, D), AF.Identity,
                        bias=p4(P4["stats"] + ST_NB(slot) + bl, 1),
                        scale=r1[:, bl:bl + 1]), waits=w)
                xfr_rel[beta] = sc_cast[beta]
                dscw_cnt[beta] = dma(
                    "scalar", "dSCw%d" % (beta % 2), lambda E, beta=beta: E.dma_start(
                        out=s_xnbf.ap()[beta * 128:(beta + 1) * 128, :],
                        in_=p2(P2["xnb"] + (beta % 2) * D, D)),
                    waits=[("SC", sc_cast[beta])])
            for bl in range(nb):
                thunks.append(lambda bl=bl: blockC(bl))

            def transposes():
                w = [("dSCw%d" % ((b0 + k) % 2), dscw_cnt[b0 + k]) for k in range(nb)]
                if ci == 0:
                    w.append(("dHP", 16))
                if ci >= 2:
                    w.append(("T", t_m4end[ci - 2]))
                for dc in range(8):
                    dma("sync", "dTP%d" % (ci % 2),
                        lambda E, dc=dc, b0=b0, nb=nb, ci=ci: E.dma_start(
                        out=p2(P2["xnt"] + (ci % 2) * 3072 + dc * 384, nb * 128),
                        in_=s_xnbf.ap()[b0 * 128:(b0 + nb) * 128,
                                        dc * 128:(dc + 1) * 128],
                        transpose=True), waits=w if dc == 0 else ())
            thunks.append(transposes)
            return thunks

        def emit_mid(ci, pending=None):
            b0, nb = F_CHUNKS[ci]
            def drain(k=1):
                for _ in range(k):
                    if pending:
                        pending.popleft()()
            dtp_need = 16 * 8 * (ci + 1) + 16 * 128
            def m5p(fc):
                w = [("V", sc_h1[ci][fc])]
                if ci == 0 and fc % 8 == 0:
                    w.append(("dW2", 16 * (fc // 8 + 1)))
                if fc == 0:
                    w.append(("dROW", dROW_b2))
                    if ci == 0:
                        w.append(("V", v_wend))
                    if ci >= 1 and relV_eo[ci - 1] is not None:
                        w.append(("V", relV_eo[ci - 1]))
                for bl in range(nb):
                    lhsT = p2(P2["h1t"] + (fc % 2) * 384 + bl * 128, 128)
                    for half in range(2):
                        top(lambda E, lhsT=lhsT, fc=fc, bl=bl, half=half: E.matmul(
                            bank(2 + 2 * bl + half),
                            lhsT, p2(P2["W2"] + fc * D + half * 512, 512),
                            start=(fc == 0), stop=False),
                            waits=w if (bl == 0 and half == 0) else ())

            ndtp = 16 * 8 * (ci // 2 + 1)
            for fc in range(32):
                w = [("dTP%d" % (ci % 2), ndtp)]
                if ci == 0 and fc % 8 == 0:
                    w.append(("dW1", 16 * (fc // 8 + 1)))
                    if fc == 0:
                        w.append(("V", v_wend))
                if fc >= 2:
                    w.append(("V", sc_h1[ci][fc - 2]))
                bk = fc % 2
                tlast = None
                for dc in range(8):
                    tlast = top(
                        lambda E, dc=dc, fc=fc, bk=bk, ci=ci, nb=nb: E.matmul(
                            bank(bk, 0, nb * 128),
                            p2(P2["W1"] + dc * FF + fc * 128, 128),
                            p2(P2["xnt"] + (ci % 2) * 3072 + dc * 384, nb * 128),
                            start=(dc == 0), stop=(dc == 7)),
                        waits=w if dc == 0 else (), inc=(dc == 7))
                t_m4fc = tlast
                sc_h1[ci][fc] = vop(
                    lambda E, fc=fc, bk=bk, nb=nb: E.tensor_scalar(
                        p2(P2["h1t"] + (fc % 2) * 384, nb * 128),
                        bank(bk, 0, nb * 128), b1tl[:, fc:fc + 1], 0.0,
                        ALU.add, ALU.max),
                    waits=[("T", t_m4fc), ("dB1", 16 * 32)])
                if fc >= 1:
                    m5p(fc - 1)
                drain(1)
            t_m4end[ci] = cnt["T"]
            m5p(31)
            for bl in range(nb):
                top(lambda E, bl=bl: E.matmul(
                    bank(2 + 2 * bl), ONES, ROWB[0:1, 0:512],
                    start=False, stop=True))
                t_m5[ci][bl] = top(lambda E, bl=bl: E.matmul(
                    bank(3 + 2 * bl), ONES, ROWB[0:1, 512:1024],
                    start=False, stop=True), inc=True)

        def emit_tail_eo(ci):
            b0, nb = F_CHUNKS[ci]
            for bl in range(nb):
                beta = b0 + bl
                wcp = [("T", t_m5[ci][bl])]
                if debug and bl == 0 and ci >= 1:
                    pb0, pnb = F_CHUNKS[ci - 1]
                    if dbgxf_cnt[pb0 + pnb - 1] is not None:
                        wcp.append(("dVT", dbgxf_cnt[pb0 + pnb - 1]))
                v_xf = vop(lambda E, bl=bl: E.tensor_copy(
                    p4(P4["xfb"] + bl * D, D),
                    pball[:, (2 + 2 * bl) * 512:(2 + 2 * bl) * 512 + 1024]),
                    waits=wcp)
                if debug:
                    dbgxf_cnt[beta] = v_xf
            relV_eo[ci] = cnt["V"]
            for bl in range(min(nb, 2)):
                xtl_load(b0 + bl)

        def xtl_load(beta):
            w = []
            if beta >= 2:
                w = [("V", xtl_rel[beta - 2])]
                assert xtl_rel[beta - 2] is not None
            dxt_cnt[beta] = dma("sync", "dXT%d" % (beta % 2),
                                lambda E, beta=beta: E.dma_start(
                out=p4(P4["xtl"] + (beta % 2) * D, D),
                in_=x_t.ap()[beta * 128:(beta + 1) * 128, :]), waits=w)

        def rest_thunks(ci):
            b0, nb = F_CHUNKS[ci]
            thunks = []

            def blockE1(bl):
                beta = b0 + bl
                mu1 = mu1v[ci]
                r1 = r1v[ci]
                vop(lambda E, beta=beta, bl=bl, mu1=mu1, r1=r1: E.tensor_scalar(
                    p4(P4["t1"], D), p4(P4["xtl"] + (beta % 2) * D, D),
                    mu1[:, bl:bl + 1], r1[:, bl:bl + 1], ALU.subtract, ALU.mult),
                    waits=[("dXT%d" % (beta % 2), dxt_cnt[beta])])
                xtl_rel[beta] = cnt["V"]
                if bl + 2 < nb:
                    xtl_load(b0 + bl + 2)

            def blockE2(bl):
                beta = b0 + bl
                v_xf = vop(lambda E, bl=bl: E.tensor_tensor(
                    p4(P4["xfb"] + bl * D, D), p4(P4["xfb"] + bl * D, D),
                    p4(P4["t1"], D), ALU.add))
                if debug:
                    dbgxf_cnt[beta] = dma(
                        "gpsimd", "dVT", lambda E, beta=beta, bl=bl: E.dma_start(
                            out=dxf_t.ap()[beta * 128:(beta + 1) * 128, :],
                            in_=p4(P4["xfb"] + bl * D, D)),
                        waits=[("V", v_xf), ("dVT", dvt_chain[0])])
                    dvt_chain[0] = dbgxf_cnt[beta]

            def blockE3(bl):
                vop(lambda E, bl=bl: E.reduce_sum(
                    p4(P4["stats"] + ST["s2sum"] + bl, 1),
                    p4(P4["xfb"] + bl * D, D), AX.X))
                sop(lambda E, bl=bl: E.activation(
                    p4(P4["sqscr"], D), p4(P4["xfb"] + bl * D, D), AF.Square,
                    accum_out=p4(P4["stats"] + ST["s2sq"] + bl, 1)),
                    waits=[("V", cnt["V"])])
            for bl in range(nb):
                thunks.append(lambda bl=bl: blockE1(bl))
                thunks.append(lambda bl=bl: blockE2(bl))
                thunks.append(lambda bl=bl: blockE3(bl))

            stv = lambda k, n=nb: p4(P4["stats"] + k, n)

            def rstd2_a():
                vop(lambda E: E.tensor_scalar(stv(ST["mu2"]), stv(ST["s2sum"]),
                                              1.0 / D, None, ALU.mult),
                    waits=[("SC", cnt["SC"])])
                vop(lambda E: E.tensor_tensor(stv(ST["vt2"]), stv(ST["mu2"]),
                                              stv(ST["mu2"]), ALU.mult))
                vop(lambda E: E.scalar_tensor_tensor(
                    stv(ST["vt2"]), stv(ST["s2sq"]), 1.0 / D, stv(ST["vt2"]),
                    ALU.mult, ALU.subtract))
                vop(lambda E: E.tensor_scalar(stv(ST["vt2"]), stv(ST["vt2"]),
                                              EPS, None, ALU.add))

            def rstd2_b():
                _rsqrt(vop, stv(ST["vt2"]), stv(ST["r2"]), stv(ST["yA"]),
                       stv(ST["yB"]), nb)
            thunks.append(rstd2_a)
            thunks.append(rstd2_b)

            def blockG1(bl):
                vop(lambda E, bl=bl: E.scalar_tensor_tensor(
                    p4(P4["t1"], D), p4(P4["xfb"] + bl * D, D),
                    p4(P4["stats"] + ST["mu2"] + bl, 1), GT,
                    ALU.subtract, ALU.mult))

            def blockG2(bl):
                beta = b0 + bl
                wg = []
                if beta >= 1:
                    wg = [("SC", sc_silu[beta - 1])]
                v_gp = vop(lambda E, bl=bl: E.scalar_tensor_tensor(
                    p4(P4["gpre"], D), p4(P4["t1"], D),
                    p4(P4["stats"] + ST["r2"] + bl, 1), BT,
                    ALU.mult, ALU.add), waits=wg)
                w2_ = [("V", v_gp)]
                if beta >= 1:
                    w2_.append(("dG", dg_cnt[beta - 1]))
                sc_silu[beta] = sop(lambda E: E.activation(
                    p4(P4["gbuf"], D), p4(P4["gpre"], D), AF.Silu), waits=w2_)
                dg_cnt[beta] = dma("scalar", "dG", lambda E, beta=beta: E.dma_start(
                    out=s_g.ap()[beta * 128:(beta + 1) * 128, :],
                    in_=p4(P4["gbuf"], D)), waits=[("SC", sc_silu[beta])])
            for bl in range(nb):
                thunks.append(lambda bl=bl: blockG1(bl))
                thunks.append(lambda bl=bl: blockG2(bl))
            return thunks

        from collections import deque
        if not skip_f:
            pending = deque()
            for it in range(NCH + 2):
                if it < NCH:
                    pending.extend(front_thunks(it))
                if 0 <= it - 2 < NCH:
                    pending.extend(rest_thunks(it - 2))
                if 0 <= it - 1 < NCH:
                    emit_mid(it - 1, pending)
                    emit_tail_eo(it - 1)
                else:
                    while pending:
                        pending.popleft()()
            while pending:
                pending.popleft()()

        tEndF = cnt["T"]
        vEndF = cnt["V"]
        scEndF = cnt["SC"]
        gpEndF = cnt["GP"]
        BARRIER = [("T", tEndF), ("V", vEndF), ("SC", scEndF), ("GP", gpEndF)]

        for dc in range(8):
            dma("gpsimd", "dWO", lambda E, dc=dc: E.dma_start(
                out=p2(P2["wout"] + dc * D, D),
                in_=wout_t.ap()[dc * 128:(dc + 1) * 128, :]),
                waits=list(BARRIER) + [("dHP", 16)])
        dma("gpsimd", "dROW", lambda E: E.dma_start(out=ROWB, in_=bout_t.ap()),
            waits=BARRIER)
        dROW_bout = cnt["dROW"]

        t_m1 = [[None] * 32 for _ in range(BC)]
        t_m1end = [None] * BC
        t_m2 = [{} for _ in range(BC)]
        t_m2end = [None] * BC
        t_m3 = [[None] * 16 for _ in range(BC)]
        t_m3end = [None] * BC
        v_zevac = [[None] * 32 for _ in range(BC)]
        v_a4 = [{} for _ in range(BC)]
        v_a4end = [None] * BC
        v_tail = [[None] * 16 for _ in range(BC)]
        dwi_cnt = [[None] * 32 for _ in range(BC)]
        dxta_cnt = [None] * BC
        dvt_cnt = [[None] * 16 for _ in range(BC)]
        dvn_cnt = [None] * BC
        dgi_cnt = [[None] * 16 for _ in range(BC)]
        gp_tail = [[None] * 16 for _ in range(BC)]
        dxi_cnt = [[None] * 16 for _ in range(BC)]
        dy_cnt = [[None] * 16 for _ in range(BC)]
        gp_vt = [[None] * 16 for _ in range(BC)]

        def wi_load(b, li):
            ecg = li % 16
            st2 = ecg // 8
            ec = ecg % 8
            src = win2_t if st2 else win0_t
            w2_ = []
            if li >= 2:
                t_prev = t_m1[b][li - 2]
                assert t_prev is not None
                w2_.append(("T", t_prev))
            elif b >= 1:
                t_prev = t_m1[b - 1][30 + li]
                assert t_prev is not None
                w2_.append(("T", t_prev))
            else:
                dwi_cnt[0][li] = wi0_cnt[li]
                return
            dwi_cnt[b][li] = dma(
                "gpsimd", "dWI%d" % (li % 2),
                lambda E, src=src, ec=ec, li=li: E.dma_start(
                    out=p2(P2["wiA"] + (li % 2) * D, D),
                    in_=bass.AP(src, ec * 128,
                                [[D, 128], [128 * D, 8], [1, 128]])), waits=w2_)

        def emit_prep(b):
            if b == 0:
                w = [("T", tEndF),
                     ("dSCw0", cnt["dSCw0"]), ("dSCw1", cnt["dSCw1"])]
            else:
                w = list(BARRIER) + [("dSCw0", cnt["dSCw0"]),
                                     ("dSCw1", cnt["dSCw1"])]
            if b >= 1:
                w.append(("T", t_m1end[b - 1]))
            for dc in range(8):
                dxta_cnt[b] = dma("sync", "dXTA", lambda E, dc=dc, b=b: E.dma_start(
                    out=p2(P2["xntA"] + dc * S, S),
                    in_=s_xnbf.ap()[b * S:(b + 1) * S, dc * 128:(dc + 1) * 128],
                    transpose=True), waits=w if dc == 0 else ())
            for li in range(2):
                wi_load(b, li)

        rr_cnt = [0]
        rr_evac = {}

        def emit_a2(b, pending):
            if b == 0:
                wm = list(BARRIER) + [("dG", cnt["dG"])]
                if debug and dbgxf_cnt[NBLK - 1] is not None:
                    wm.append(("dVT", dbgxf_cnt[NBLK - 1]))
                vop(lambda E: E.memset(p4(P4["stage"], 2), 0.0), waits=wm)
            last_sct = None
            vfin_last = [None]
            HS = 1024
            for half in range(2):
                for ecg in range(16):
                    st2 = ecg // 8
                    ec = ecg % 8
                    li = half * 16 + ecg
                    assert dwi_cnt[b][li] is not None
                    w = [("dWI%d" % (li % 2), dwi_cnt[b][li]),
                         ("dXTA", dxta_cnt[b])]
                    if b == 0 and li == 0:
                        w.append(("V", relV_eo[NCH - 1]))
                        w.append(("V", relV_eo[NCH - 2]))
                    rr = rr_cnt[0]
                    rr_cnt[0] += 1
                    p = 4 + 2 * (rr % 2)
                    wr = list(w)
                    if rr >= 2:
                        wr.append(("V", rr_evac[rr - 2]))
                    first = True
                    tlast = None
                    for ss in (0, 1):
                        for dc in range(8):
                            tlast = top(
                                lambda E, ss=ss, dc=dc, p=p, half=half, li=li:
                                E.matmul(
                                    bank(p + ss),
                                    p2(P2["wiA"] + (li % 2) * D + dc * 128, 128),
                                    p2(P2["xntA"] + dc * S +
                                       (half * 2 + ss) * 512, 512),
                                    start=(dc == 0), stop=(dc == 7)),
                                waits=wr if first else (),
                                inc=(ss == 1 and dc == 7))
                            first = False
                    bin_off = SV["bin2"] if st2 else SV["bin0"]
                    rr_evac[rr] = vop(
                        lambda E, p=p, half=half, bin_off=bin_off, ec=ec:
                        E.tensor_scalar(
                            p4(P4["stage"] + 2 + half * HS, HS),
                            pball[:, p * 512:p * 512 + 1024],
                            sv1(bin_off + ec), None, ALU.add),
                        waits=[("T", tlast), ("dVEC", dvec_total)] +
                              ([("SC", last_sct)] if last_sct is not None
                               else []))
                    t_m1[b][li] = tlast
                    v_zevac[b][li] = rr_evac[rr]
                    if li + 2 < 32:
                        wi_load(b, li + 2)
                    if half == 0 and ecg < 8:
                        for _ in range(2):
                            if pending:
                                pending.popleft()()
                    bc_off = SV["bc2"] if st2 else SV["bc0"]
                    tp = "t2" if st2 else "t0"
                    if half == 0:
                        sop(lambda E, ecg=ecg: E.activation(
                            p4(P4["carry"] + ecg * 2, 2),
                            p4(P4["stage"] + HS, 2), AF.Identity),
                            waits=[("V", v_zevac[b][li])])
                    else:
                        assert vfin_last[0] is not None
                        sop(lambda E, ecg=ecg: E.activation(
                            p4(P4["stage"] + HS, 2),
                            p4(P4["carry"] + ecg * 2, 2), AF.Identity),
                            waits=[("V", vfin_last[0])])
                    sc_t = sop(lambda E, tp=tp, bc_off=bc_off, ec=ec, half=half:
                               E.activation(
                        p4(P4["tconv"], HS), p4(P4["stage"] + 2 + half * HS, HS),
                        AF.Identity,
                        scale=sv1(SV[tp + "2"] + ec),
                        bias=sv1(bc_off + ec)),
                        waits=[("V", v_zevac[b][li]), ("dVEC", dvec_total)])
                    last_sct = sc_t
                    vop(lambda E, tp=tp, ec=ec, half=half:
                        E.scalar_tensor_tensor(
                        p4(P4["tconv"], HS), p4(P4["stage"] + 1 + half * HS, HS),
                        sv1(SV[tp + "1"] + ec), p4(P4["tconv"], HS),
                        ALU.mult, ALU.add), waits=[("SC", sc_t)])
                    if st2 == 0:
                        v_fin = vop(lambda E, tp=tp, ec=ec, half=half:
                                    E.scalar_tensor_tensor(
                            p2(P2["z0c"] + ec * S + half * HS, HS),
                            p4(P4["stage"] + half * HS, HS),
                            sv1(SV[tp + "0"] + ec), p4(P4["tconv"], HS),
                            ALU.mult, ALU.add))
                        vfin_last[0] = v_fin
                        if debug and b == 3 and half == 1:
                            dvt_chain[0] = dma(
                                "gpsimd", "dVT", lambda E, ec=ec: E.dma_start(
                                    out=dz0_t.ap()[ec * 128:(ec + 1) * 128, :],
                                    in_=p2(P2["z0c"] + ec * S, S)),
                                waits=[("V", v_fin), ("dVT", dvt_chain[0])])
                    else:
                        hx = half * 8 + ec
                        wz = []
                        if ec >= 1 and gp_vt[b][hx - 1] is not None:
                            wz = [("GP", gp_vt[b][hx - 1])]
                        v_fin = vop(lambda E, tp=tp, ec=ec, half=half:
                                    E.scalar_tensor_tensor(
                            p4(P4["z2c"] + half * HS, HS),
                            p4(P4["stage"] + half * HS, HS),
                            sv1(SV[tp + "0"] + ec), p4(P4["tconv"], HS),
                            ALU.mult, ALU.add), waits=wz)
                        vfin_last[0] = v_fin
                        wv = [("V", v_fin)]
                        if b >= 1:
                            wv.append(("T", t_m3end[b - 1]))
                        gp_vt[b][hx] = gop(lambda E, ec=ec, half=half:
                                           E.tensor_tensor(
                            p2(P2["vt"] + ec * S + half * HS, HS),
                            p4(P4["z2c"] + half * HS, HS),
                            p2(P2["z0c"] + ec * S + half * HS, HS),
                            ALU.mult), waits=wv)
                        wv2 = [("GP", gp_vt[b][hx]), ("dVT", dvt_chain[0])]
                        if b >= 1:
                            wv2.append(("dVN", dvn_cnt[b - 1]))
                        dvt_cnt[b][hx] = dma(
                            "gpsimd", "dVT", lambda E, ec=ec, half=half:
                            E.dma_start(
                                out=s_vt.ap()[ec * 128:(ec + 1) * 128,
                                              half * HS:half * HS + HS],
                                in_=p2(P2["vt"] + ec * S + half * HS, HS)),
                            waits=wv2)
                        dvt_chain[0] = dvt_cnt[b][hx]
                        if ec == 7:
                            wt_ = [("dVT", dvt_cnt[b][hx])]
                            if half == 0 and b >= 1:
                                wt_.append(("T", t_m2end[b - 1]))
                            for jj in range(8):
                                j = half * 8 + jj
                                dvn_cnt[b] = dma(
                                    "sync", "dVN", lambda E, j=j: E.dma_start(
                                        out=p2(P2["vnat"] + j * D, D),
                                        in_=s_vt.ap()[:, j * 128:(j + 1) * 128],
                                        transpose=True),
                                    waits=wt_ if jj == 0 else ())
            t_m1end[b] = t_m1[b][31]

        def emit_a4(b):
            started = set()
            seglist = []
            for dc in range(8):
                for j in range(16):
                    for d in range(9):
                        if j + d < 16:
                            seglist.append([dc, j, d, 1, d // 1 * 0 + (j + d) // 4, j + d])
            firsts = {}
            lasts = {}
            for k, (dc, j, d, dlen, G, i0) in enumerate(seglist):
                if (dc, G) not in firsts:
                    firsts[(dc, G)] = k
                lasts[(dc, G)] = k
            for k, (dc, j, d, dlen, G, i0) in enumerate(seglist):
                w = []
                if k == 0:
                    w = [("dTall", 16 * 128), ("dVN", cnt["dVN"])]
                    if b >= 1:
                        assert v_tail[b - 1][15] is not None
                        w.append(("V", v_tail[b - 1][15]))
                if firsts[(dc, G)] == k and dc >= 1:
                    w.append(("V", v_a4[b][(dc - 1, G)]))
                st = (firsts[(dc, G)] == k)
                sp = (lasts[(dc, G)] == k)
                tv = top(lambda E, dc=dc, j=j, d=d, dlen=dlen, G=G, i0=i0, st=st,
                         sp=sp: E.matmul(
                             bank(G, (i0 % 4) * 128, dlen * 128),
                             p2(P2["vnat"] + j * D + dc * 128, 128),
                             p2(P2["tall"] + d * 128, dlen * 128),
                             start=st, stop=sp),
                         waits=w, inc=sp)
                if sp:
                    t_m2[b][(dc, G)] = tv
                    vop(lambda E, dc=dc, G=G: E.scalar_tensor_tensor(
                        p4(P4["vfb"], 512), p2(P2["vt"] + dc * S + G * 512, 512),
                        sv1(SV["fb"] + dc), bank(G),
                        ALU.mult, ALU.add), waits=[("T", tv)])
                    v_a4[b][(dc, G)] = vop(lambda E, dc=dc, G=G: E.tensor_tensor(
                        p2(P2["vt"] + dc * S + G * 512, 512), p4(P4["vfb"], 512),
                        p2(P2["z0c"] + dc * S + G * 512, 512), ALU.mult))
            t_m2end[b] = cnt["T"]
            v_a4end[b] = cnt["V"]

        def a5_unit(b, i):
            par = i % 2
            w = [("V", v_a4[b][(7, i // 4)]),
                 ("V", v_a4[b][(7, par * 2)]),
                 ("V", v_a4[b][(7, par * 2 + 1)]),
                 ("dWO", 16 * 8), ("dROW", dROW_bout)]
            if i >= 2 and v_tail[b][i - 2] is not None:
                w.append(("V", v_tail[b][i - 2]))
            elif i < 2 and b >= 1:
                w.append(("V", v_tail[b - 1][14 + i]))
            if b == 0 and i < 2:
                w.append(("V", vEndF))
            first = True
            for dc in range(8):
                for half in range(2):
                    top(lambda E, dc=dc, half=half, par=par, i=i: E.matmul(
                        bank(par * 2 + half),
                        p2(P2["vt"] + dc * S + i * 128, 128),
                        p2(P2["wout"] + dc * D + half * 512, 512),
                        start=(dc == 0), stop=False),
                        waits=w if first else ())
                    first = False
            top(lambda E, par=par: E.matmul(
                bank(par * 2), ONES, ROWB[0:1, 0:512],
                start=False, stop=True))
            t_m3[b][i] = top(lambda E, par=par: E.matmul(
                bank(par * 2 + 1), ONES, ROWB[0:1, 512:1024],
                start=False, stop=True), inc=True)
            wg = []
            wx = []
            if i >= 2 and v_tail[b][i - 2] is not None:
                wg = [("V", v_tail[b][i - 2])]
                wx = [("GP", gp_tail[b][i - 2])]
            elif i < 2 and b >= 1:
                wg = [("V", v_tail[b - 1][14 + i])]
                wx = [("GP", gp_tail[b - 1][14 + i])]
            row0 = b * S + i * 128
            dgi_cnt[b][i] = dma("sync", "dGI%d" % (i % 2), lambda E, row0=row0, i=i:
                                E.dma_start(out=p4(P4["ginA"] + (i % 2) * D, D),
                                            in_=s_g.ap()[row0:row0 + 128, :]),
                                waits=wg)
            dxi_cnt[b][i] = dma("sync", "dXI%d" % (i % 2), lambda E, row0=row0, i=i:
                                E.dma_start(out=p4(P4["xinA"] + (i % 2) * D, D),
                                            in_=x_t.ap()[row0:row0 + 128, :]),
                                waits=wx)
            wv = [("T", t_m3[b][i]), ("dGI%d" % (i % 2), dgi_cnt[b][i])]
            if (b, i) != (0, 0):
                pi = (b, i - 1) if i >= 1 else (b - 1, 15)
                wv.append(("dY", dy_cnt[pi[0]][pi[1]]))
                wv.append(("GP", gp_tail[pi[0]][pi[1]]))
            v_tail[b][i] = vop(lambda E, i=i, par=par: E.scalar_tensor_tensor(
                p4(P4["obuf"], D), p4(P4["ginA"] + (i % 2) * D, D), 1.0,
                pball[:, (par * 2) * 512:(par * 2) * 512 + 1024],
                ALU.mult, ALU.mult), waits=wv)
            gp_tail[b][i] = gop(lambda E, i=i: E.tensor_tensor(
                p4(P4["obuf"], D), p4(P4["obuf"], D),
                p4(P4["xinA"] + (i % 2) * D, D), ALU.add),
                waits=[("V", v_tail[b][i]), ("dXI%d" % (i % 2), dxi_cnt[b][i])])
            dy_cnt[b][i] = dma("scalar", "dY", lambda E, row0=row0: E.dma_start(
                out=y_t.ap()[row0:row0 + 128, :], in_=p4(P4["obuf"], D)),
                waits=[("GP", gp_tail[b][i])])
            if i == 15:
                t_m3end[b] = t_m3[b][15]

        def a5_thunks(b):
            return [(lambda b=b, i=i: a5_unit(b, i)) for i in range(16)]

        emit_prep(0)
        for b in range(128):
            dma("sync", "dTall", lambda E, b=b: E.dma_start(
                out=TALL[b:b + 1, :],
                in_=bass.AP(s_hpad, 127 - b, [[1, 1], [1, 1152]])),
                waits=[("dHP", 16)] if b == 0 else ())
        pendingA = deque()
        for b in range(BC):
            emit_a2(b, pendingA)
            assert not pendingA
            if b + 1 < BC:
                emit_prep(b + 1)
            emit_a4(b)
            pendingA.extend(a5_thunks(b))
        while pendingA:
            pendingA.popleft()()

        wait("sync", "dY", 16 * NBLK)
        if debug:
            dma("sync", "dDBG", lambda E: E.dma_start(out=dvt_t.ap(), in_=s_vt.ap()),
                waits=[("dY", 16 * NBLK)])
            for dc in range(8):
                dma("sync", "dDBG", lambda E, dc=dc: E.dma_start(
                    out=dwt_t.ap()[dc * 128:(dc + 1) * 128, :],
                    in_=p2(P2["vt"] + dc * S, S)))
            for dc in range(8):
                dma("sync", "dDBG", lambda E, dc=dc: E.dma_start(
                    out=dxa_t.ap()[dc * 128:(dc + 1) * 128, :],
                    in_=p2(P2["xntA"] + dc * S, S)))
            dma("sync", "dDBG", lambda E: E.dma_start(
                out=dsv_t.ap(), in_=svtl[:, 0:88]))
            dma("sync", "dDBG", lambda E: E.dma_start(
                out=dwi_t.ap(), in_=p2(P2["wi"], 2048)))
            dma("sync", "dDBG", lambda E: E.dma_start(out=dxn_t.ap(), in_=s_xnbf.ap()),
                waits=[("dSCw0", cnt["dSCw0"]), ("dSCw1", cnt["dSCw1"])])
            dma("sync", "dDBG", lambda E: E.dma_start(out=dg_t.ap(), in_=s_g.ap()),
                waits=[("dG", cnt["dG"])])
            wait("sync", "dDBG", cnt["dDBG"])

        @blk.sync
        def _(E):
            for f in qs["sync"]:
                f(E)

        @blk.tensor
        def _(E):
            for f in qs["tensor"]:
                f(E)

        @blk.vector
        def _(E):
            for f in qs["vector"]:
                f(E)

        @blk.scalar
        def _(E):
            for f in qs["scalar"]:
                f(E)

        @blk.gpsimd
        def _(E):
            for f in qs["gpsimd"]:
                f(E)

    return nc


def _rsqrt(vop, c_ap, out_ap, ya, yb, nb):
    ci = c_ap.bitcast(I32)
    yi = ya.bitcast(I32)
    vop(lambda E: E.tensor_scalar(yi, ci, 1, None, ALU.arith_shift_right))
    vop(lambda E: E.tensor_scalar(yi, yi, -1, MAGIC, ALU.mult, ALU.add))
    cur = ya
    other = yb
    for _ in range(2):
        vop(lambda E, cur=cur, other=other: E.tensor_tensor(other, cur, cur, ALU.mult))
        vop(lambda E, other=other: E.tensor_tensor(other, other, c_ap, ALU.mult))
        vop(lambda E, other=other: E.tensor_scalar(other, other, -0.5, 1.5,
                                                   ALU.mult, ALU.add))
        vop(lambda E, cur=cur, other=other: E.tensor_tensor(other, cur, other,
                                                            ALU.mult))
        cur, other = other, cur
    vop(lambda E, cur=cur: E.tensor_copy(out_ap, cur))


def x_slice(t, idx):
    return t.ap()[idx * 128:(idx + 1) * 128, :]


_NC_CACHE = {}


def kernel(L0_attn_x, L0_attn_w_in, L0_attn_b_in, L0_attn_w_conv, L0_attn_b_conv,
           L0_attn_w1, L0_attn_b1, L0_attn_w2, L0_attn_b2, L0_attn_w3,
           L0_attn_filt_bias, L0_attn_w_out, L0_attn_b_out, L0_ffn_router_w,
           L0_ffn_router_b, L0_ffn_w1, L0_ffn_b1, L0_ffn_w2, L0_ffn_b2,
           L0_ffn_gamma, L0_ffn_beta):
    if "nc" not in _NC_CACHE:
        _NC_CACHE["nc"] = build()
    nc = _NC_CACHE["nc"]
    A = np.ascontiguousarray
    shared = {
        "win0": A(L0_attn_w_in[:, 0:1024], dtype=np.float32),
        "win2": A(L0_attn_w_in[:, 2048:3072], dtype=np.float32),
        "bin0": A(L0_attn_b_in[0:1024], dtype=np.float32),
        "bin2": A(L0_attn_b_in[2048:3072], dtype=np.float32),
        "wc0": A(L0_attn_w_conv[0:1024, 0, :], dtype=np.float32),
        "wc2": A(L0_attn_w_conv[2048:3072, 0, :], dtype=np.float32),
        "bc0": A(L0_attn_b_conv[0:1024], dtype=np.float32),
        "bc2": A(L0_attn_b_conv[2048:3072], dtype=np.float32),
        "ab1": A(L0_attn_b1, dtype=np.float32),
        "aw2": A(L0_attn_w2, dtype=np.float32),
        "ab2": A(L0_attn_b2, dtype=np.float32),
        "aw3": A(L0_attn_w3, dtype=np.float32),
        "fb": A(L0_attn_filt_bias, dtype=np.float32),
        "wout": A(L0_attn_w_out, dtype=np.float32),
        "bout": A(L0_attn_b_out, dtype=np.float32).reshape(1, 1024),
        "w1f": A(L0_ffn_w1[0], dtype=np.float32),
        "b1f": A(L0_ffn_b1[0], dtype=np.float32),
        "w2f": A(L0_ffn_w2[0], dtype=np.float32),
        "b2f": A(L0_ffn_b2[0], dtype=np.float32).reshape(1, 1024),
        "gam": A(L0_ffn_gamma, dtype=np.float32).reshape(1, 1024),
        "bet": A(L0_ffn_beta, dtype=np.float32).reshape(1, 1024),
    }
    x = np.asarray(L0_attn_x, dtype=np.float32)
    in_maps = []
    for c in range(8):
        m = dict(shared)
        m["x"] = A(x[c * BC:(c + 1) * BC]).reshape(NTOK, D)
        in_maps.append(m)
    res = run_bass_kernel_spmd(nc, in_maps, core_ids=list(range(8)))
    y = np.concatenate([r["y"].reshape(BC, S, D) for r in res.results], axis=0)
    return y


# version 19
# speedup vs baseline: 1.1015x; 1.0205x over previous
import numpy as np
import ml_dtypes
from contextlib import ExitStack

import concourse.bass as bass
import concourse.mybir as mybir
from concourse.bass_utils import run_bass_kernel_spmd

F32 = mybir.dt.float32
BF16 = mybir.dt.bfloat16
I32 = mybir.dt.int32
AF = mybir.ActivationFunctionType
ALU = mybir.AluOpType
AX = mybir.AxisListType

BC = 4
S = 2048
D = 1024
NTOK = BC * S
NBLK = NTOK // 128
FF = 4096
EPS = 1e-5
MAGIC = 0x5F3759DF

F_CHUNKS = [(i * 3, 3) for i in range(21)] + [(63, 1)]

P2 = {}
P2["W1"] = 0
P2["W2"] = 32768
P2["xnt"] = 65536
P2["h1t"] = 71680
P2["xnb"] = 72448
P2["gt"] = 74496
P2["bt"] = 75520
P2["tall"] = 76544
P2["rows2"] = 77696
P2_SIZE = 80896
P2["xntA"] = 0
P2["z0c"] = 16384
P2["vt"] = 32768
P2["vnat"] = 49152
P2["wout"] = 65536
P2["wi"] = 73728
P2["wiA"] = 78848
P2["hpad"] = 65536
P2["growb"] = 66816
P2["betrb"] = 67840

P4 = {}
P4["xfr"] = 0
P4["xtl"] = 3072
P4["xfb"] = 5120
P4["t1"] = 8192
P4["gpre"] = 9216
P4["gbuf"] = 10240
P4["sqscr"] = 11264
P4["stats"] = 12288
P4_F_END = 12416
P4["stage"] = 0
P4["tconv"] = 2052
P4["z2c"] = 4100
P4["xinA"] = 6148
P4["ginA"] = 8196
P4["obuf"] = 10244
P4["vfb"] = 11268
P4["svec"] = 11780
P4["carry"] = 11940
P4_SIZE = 12416
P4["aw3"] = P4["xfb"]
P4["grow"] = P4["t1"]
P4["brow"] = P4["xfb"] + 1024
P4["aw2"] = P4["gpre"]
P4["wvec"] = P4["gbuf"]

SV = {"t00": 0, "t01": 8, "t02": 16, "t20": 24, "t21": 32, "t22": 40,
      "bin0": 48, "bin2": 56, "bc0": 64, "bc2": 72, "fb": 80}

ST_MU1 = lambda slot: 24 * 0 + slot * 8
ST_R1 = lambda slot: slot * 8 + 4
ST_TMP = 24
ST = {"ssum": 24, "ssq": 27, "vtmp": 30, "yA": 36, "yB": 39,
      "mu2": 42, "r2": 45, "s2sum": 48, "s2sq": 51, "vt2": 54}
ST_NB = lambda slot: 57 + slot * 3

SEMS = ["V", "SC", "GP", "T", "dX0", "dX1", "dXT0", "dXT1", "dSCw0", "dSCw1",
        "dX2", "dTP0", "dTP1", "dW1", "dW2", "dWI0", "dWI1", "dWO", "dG", "dGI0",
        "dGI1", "dXI0", "dXI1", "dVT", "dVN", "dY", "dROW", "dTall", "dHP",
        "dB1", "dXTA", "dVEC", "dDBG", "dVN2", "dXQ"]


def build(debug=False):
    import os as _os
    skip_f = bool(_os.environ.get("MKDEV_NOF")) and debug
    nc = bass.Bass("TRN2", target_bir_lowering=False)

    x_t = nc.dram_tensor("x", [NTOK, D], F32, kind="ExternalInput")
    y_t = nc.dram_tensor("y", [NTOK, D], F32, kind="ExternalOutput")
    win0_t = nc.dram_tensor("win0", [D, D], F32, kind="ExternalInput")
    win2_t = nc.dram_tensor("win2", [D, D], F32, kind="ExternalInput")
    bin0_t = nc.dram_tensor("bin0", [D], F32, kind="ExternalInput")
    bin2_t = nc.dram_tensor("bin2", [D], F32, kind="ExternalInput")
    wc0_t = nc.dram_tensor("wc0", [D, 3], F32, kind="ExternalInput")
    wc2_t = nc.dram_tensor("wc2", [D, 3], F32, kind="ExternalInput")
    bc0_t = nc.dram_tensor("bc0", [D], F32, kind="ExternalInput")
    bc2_t = nc.dram_tensor("bc2", [D], F32, kind="ExternalInput")
    ab1_t = nc.dram_tensor("ab1", [64], F32, kind="ExternalInput")
    aw2_t = nc.dram_tensor("aw2", [64, 64], F32, kind="ExternalInput")
    ab2_t = nc.dram_tensor("ab2", [64], F32, kind="ExternalInput")
    aw3_t = nc.dram_tensor("aw3", [64, D], F32, kind="ExternalInput")
    fb_t = nc.dram_tensor("fb", [D], F32, kind="ExternalInput")
    wout_t = nc.dram_tensor("wout", [D, D], F32, kind="ExternalInput")
    bout_t = nc.dram_tensor("bout", [1, D], F32, kind="ExternalInput")
    w1f_t = nc.dram_tensor("w1f", [D, FF], F32, kind="ExternalInput")
    b1f_t = nc.dram_tensor("b1f", [FF], F32, kind="ExternalInput")
    w2f_t = nc.dram_tensor("w2f", [FF, D], F32, kind="ExternalInput")
    b2f_t = nc.dram_tensor("b2f", [1, D], F32, kind="ExternalInput")
    gam_t = nc.dram_tensor("gam", [1, D], F32, kind="ExternalInput")
    bet_t = nc.dram_tensor("bet", [1, D], F32, kind="ExternalInput")

    if skip_f:
        s_xnbf = nc.dram_tensor("xnin", [NTOK, D], BF16, kind="ExternalInput")
    else:
        s_xnbf = nc.dram_tensor("s_xnbf", [NTOK, D], BF16, kind="Internal")
    s_g = nc.dram_tensor("s_g", [NTOK, D], F32, kind="Internal")
    s_vt = nc.dram_tensor("s_vt", [D, S], BF16, kind="Internal")
    s_hpad = nc.dram_tensor("s_hpad", [1, 1280], BF16, kind="Internal")
    if debug:
        dxn_t = nc.dram_tensor("dbg_xn", [NTOK, D], BF16, kind="ExternalOutput")
        dg_t = nc.dram_tensor("dbg_g", [NTOK, D], F32, kind="ExternalOutput")
        dxf_t = nc.dram_tensor("dbg_xf", [NTOK, D], F32, kind="ExternalOutput")
        dvt_t = nc.dram_tensor("dbg_vt", [D, S], BF16, kind="ExternalOutput")
        dz0_t = nc.dram_tensor("dbg_z0c", [D, S], BF16, kind="ExternalOutput")
        dwt_t = nc.dram_tensor("dbg_wt", [D, S], BF16, kind="ExternalOutput")
        dxa_t = nc.dram_tensor("dbg_xnta", [D, S], BF16, kind="ExternalOutput")
        dsv_t = nc.dram_tensor("dbg_svec", [128, 88], F32, kind="ExternalOutput")
        dwi_t = nc.dram_tensor("dbg_wi", [128, 2048], BF16, kind="ExternalOutput")
        dst_t = nc.dram_tensor("dbg_stage", [128, 2048], F32, kind="ExternalOutput")

    with ExitStack() as stack:
        pool2 = stack.enter_context(nc.sbuf_tensor("pool2", [128, P2_SIZE], BF16))
        pool4 = stack.enter_context(nc.sbuf_tensor("pool4", [128, P4_SIZE], F32))
        b1tl = stack.enter_context(nc.sbuf_tensor("b1tl", [128, 32], F32))
        svtl = stack.enter_context(nc.sbuf_tensor("svtl", [128, 128], F32))
        pball = stack.enter_context(nc.psum_tensor("pball", [128, 4096], F32))
        SH = {n: stack.enter_context(nc.semaphore(n)) for n in SEMS}
        blk = stack.enter_context(nc.Block())

        def p2(off, n, p0=0, np_=128):
            return pool2[p0:p0 + np_, off:off + n]

        def p4(off, n, p0=0, np_=128):
            return pool4[p0:p0 + np_, off:off + n]

        def sv1(off):
            return svtl[:, off:off + 1]

        def bank(k, off=0, n=512):
            return pball[:, k * 512 + off:k * 512 + off + n]

        ONES = pool2[0:1, P2["rows2"]:P2["rows2"] + 128]
        ROWB = pool2[0:1, P2["rows2"] + 128:P2["rows2"] + 128 + D]

        qs = {e: [] for e in ("sync", "tensor", "vector", "scalar", "gpsimd")}
        cnt = {n: 0 for n in SEMS}
        hw = {}

        def wait(eng, sem, val, future=False):
            if val is None or val <= 0:
                return
            assert future or val <= cnt[sem], \
                f"wait {sem}>={val} but only {cnt[sem]} emitted"
            key = (eng, sem)
            if hw.get(key, 0) >= val:
                return
            hw[key] = val
            qs[eng].append(lambda E, sem=sem, val=val: E.wait_ge(SH[sem], val))

        def vop(fn, waits=()):
            for s, v in waits:
                wait("vector", s, v)
            wait("vector", "V", cnt["V"])
            cnt["V"] += 1
            qs["vector"].append(lambda E, fn=fn: fn(E).then_inc(SH["V"], 1))
            return cnt["V"]

        def sop(fn, waits=()):
            for s, v in waits:
                wait("scalar", s, v)
            wait("scalar", "SC", cnt["SC"])
            cnt["SC"] += 1
            qs["scalar"].append(lambda E, fn=fn: fn(E).then_inc(SH["SC"], 1))
            return cnt["SC"]

        def gop(fn, waits=()):
            for s, v in waits:
                wait("gpsimd", s, v)
            wait("gpsimd", "GP", cnt["GP"])
            cnt["GP"] += 1
            qs["gpsimd"].append(lambda E, fn=fn: fn(E).then_inc(SH["GP"], 1))
            return cnt["GP"]

        def top(fn, waits=(), inc=False):
            for s, v in waits:
                wait("tensor", s, v)
            if inc:
                cnt["T"] += 1
                qs["tensor"].append(lambda E, fn=fn: fn(E).then_inc(SH["T"], 1))
                return cnt["T"]
            qs["tensor"].append(lambda E, fn=fn: fn(E))
            return None

        def dma(eng, sem, fn, waits=()):
            for s, v in waits:
                wait(eng, s, v)
            cnt[sem] += 16
            qs[eng].append(lambda E, fn=fn, sem=sem: fn(E).then_inc(SH[sem], 16))
            return cnt[sem]

        AW2 = p4(P4["aw2"], 64, 0, 64)
        AW3 = p4(P4["aw3"], D, 0, 64)
        AB1 = p4(P4["wvec"] + 0, 1, 0, 64)
        AB2 = p4(P4["wvec"] + 1, 1, 0, 64)
        HM1 = p4(P4["wvec"] + 2, 1, 0, 64)
        HM2 = p4(P4["wvec"] + 3, 1, 0, 64)
        HM2P = p4(P4["wvec"] + 4, 1, 0, 64)
        HPAD = p2(P2["hpad"], 1280, 0, 1)
        GT = p2(P2["gt"], D)
        BT = p2(P2["bt"], D)
        TALL = p2(P2["tall"], 1152)

        dma("gpsimd", "dROW", lambda E: E.dma_start(
            out=AB1, in_=ab1_t.ap().rearrange("(a b) -> a b", b=1)))
        dma("gpsimd", "dROW", lambda E: E.dma_start(out=AW2, in_=aw2_t.ap()))
        dma("gpsimd", "dROW", lambda E: E.dma_start(
            out=AB2, in_=ab2_t.ap().rearrange("(a b) -> a b", b=1)))
        dma("gpsimd", "dROW", lambda E: E.dma_start(out=AW3, in_=aw3_t.ap()))
        dma("gpsimd", "dROW", lambda E: E.dma_start(
            out=p2(P2["growb"], D, 0, 1), in_=gam_t.ap()))
        dma("gpsimd", "dROW", lambda E: E.dma_start(
            out=p2(P2["betrb"], D, 0, 1), in_=bet_t.ap()))
        dROW_b2 = dma("gpsimd", "dROW", lambda E: E.dma_start(out=ROWB, in_=b2f_t.ap()))
        for name, src, off in (("bin0", bin0_t, SV["bin0"]), ("bin2", bin2_t, SV["bin2"]),
                               ("bc0", bc0_t, SV["bc0"]), ("bc2", bc2_t, SV["bc2"]),
                               ("fb", fb_t, SV["fb"])):
            dma("gpsimd", "dVEC", lambda E, src=src, off=off: E.dma_start(
                out=svtl[:, off:off + 8],
                in_=bass.AP(src, 0, [[1, 128], [128, 8]]),
                allow_slow_non_contiguous=True))
        for stn, wct in (("t0", wc0_t), ("t2", wc2_t)):
            for k in range(3):
                off = SV[stn + str(k)]
                dma("gpsimd", "dVEC", lambda E, wct=wct, k=k, off=off:
                    E.dma_start(out=svtl[:, off:off + 8],
                                in_=bass.AP(wct, k, [[3, 128], [384, 8]]),
                                allow_slow_non_contiguous=True))
        dvec_total = cnt["dVEC"]
        dma("gpsimd", "dB1", lambda E: E.dma_start(
            out=b1tl[:, 0:32],
            in_=bass.AP(b1f_t, 0, [[1, 128], [128, 32]]),
            allow_slow_non_contiguous=True))

        v_ones = vop(lambda E: E.memset(ONES, 1.0))

        sop(lambda E: E.activation(HM1, AB1, AF.Silu), waits=[("dROW", dROW_b2)])
        top(lambda E: E.matmul(pball[0:64, 0:1], AW2, HM1, start=True, stop=True),
            waits=[("dROW", dROW_b2), ("SC", cnt["SC"])], inc=True)
        vop(lambda E: E.tensor_scalar(HM2P, pball[0:64, 0:1], AB2, None, ALU.add),
            waits=[("T", cnt["T"]), ("dROW", dROW_b2)])
        sop(lambda E: E.activation(HM2, HM2P, AF.Silu), waits=[("V", cnt["V"])])
        top(lambda E: E.matmul(pball[0:1, 512:1024], HM2, AW3[:, 0:512],
                               start=True, stop=True), waits=[("SC", cnt["SC"])])
        t_h = top(lambda E: E.matmul(pball[0:1, 1024:1536], HM2, AW3[:, 512:1024],
                                     start=True, stop=True), inc=True)
        vop(lambda E: E.memset(HPAD, 0.0))
        vop(lambda E: E.tensor_copy(HPAD[0:1, 127:639], pball[0:1, 512:1024]),
            waits=[("T", t_h)])
        v_hp = vop(lambda E: E.tensor_copy(HPAD[0:1, 639:1151], pball[0:1, 1024:1536]))
        dma("gpsimd", "dHP", lambda E: E.dma_start(out=s_hpad.ap(), in_=HPAD),
            waits=[("V", v_hp)])
        top(lambda E: E.matmul(bank(0), ONES, p2(P2["growb"], 512, 0, 1),
                               start=True, stop=True),
            waits=[("dROW", dROW_b2), ("V", max(v_ones, v_hp))])
        t_g = top(lambda E: E.matmul(bank(1), ONES, p2(P2["growb"] + 512, 512, 0, 1),
                                     start=True, stop=True), inc=True)
        vop(lambda E: E.tensor_copy(GT[:, 0:512], bank(0)), waits=[("T", t_g)])
        v_gt = vop(lambda E: E.tensor_copy(GT[:, 512:1024], bank(1)))
        top(lambda E: E.matmul(bank(2), ONES, p2(P2["betrb"], 512, 0, 1),
                               start=True, stop=True), waits=[("V", v_gt)])
        t_b = top(lambda E: E.matmul(bank(3), ONES, p2(P2["betrb"] + 512, 512, 0, 1),
                                     start=True, stop=True), inc=True)
        vop(lambda E: E.tensor_copy(BT[:, 0:512], bank(2)), waits=[("T", t_b)])
        v_bt = vop(lambda E: E.tensor_copy(BT[:, 512:1024], bank(3)))
        v_wend = v_bt

        if not skip_f:
            for g in range(4):
                w1w = [("dW1", 16 * g)] if g else []
                if g == 1:
                    wait("gpsimd", "dTP0", 16 * 8, future=True)
                dma("gpsimd", "dW1",
                    lambda E, g=g: E.dma_start(
                        out=bass.AP(pool2, P2["W1"] + g * 1024,
                                    [[P2_SIZE, 128], [FF, 8], [1, 1024]]),
                        in_=bass.AP(w1f_t, g * 1024,
                                    [[FF, 128], [128 * FF, 8], [1, 1024]])),
                    waits=w1w)
                dma("gpsimd", "dW2",
                    lambda E, g=g: E.dma_start(
                        out=bass.AP(pool2, P2["W2"] + g * 8 * D,
                                    [[P2_SIZE, 128], [D, 8], [1, D]]),
                        in_=bass.AP(w2f_t, g * 8 * 128 * D,
                                    [[D, 128], [128 * D, 8], [1, D]])),
                    waits=[("dW2", 16 * g)] if g else ())
            wi0_cnt = [None, None]
            for li in range(2):
                src0 = win2_t if (li % 16) // 8 else win0_t
                wi0_cnt[li] = dma(
                    "gpsimd", "dWI%d" % li, lambda E, src0=src0, li=li:
                    E.dma_start(
                        out=p2(P2["wiA"] + li * D, D),
                        in_=bass.AP(src0, (li % 8) * 128,
                                    [[D, 128], [128 * D, 8], [1, 128]])))

        tW = cnt["T"]
        NCH = len(F_CHUNKS)
        relV_eo = [None] * NCH
        t_m4end = [None] * NCH
        sc_h1 = [[None] * 32 for _ in range(NCH)]
        t_m5 = [[None] * 3 for _ in range(NCH)]
        sc_cast = [None] * NBLK
        xfr_rel = [None] * NBLK
        xfr_relv = [None] * NBLK
        xtl_rel = [None] * NBLK
        dscw_cnt = [None] * NBLK
        dx_cnt = [None] * NBLK
        dxt_cnt = [None] * NBLK
        dg_cnt = [None] * NBLK
        sc_silu = [None] * NBLK
        dbgxf_cnt = [None] * NBLK
        dvt_chain = [0]
        mu1v = {}
        r1v = {}

        def front_thunks(ci):
            b0, nb = F_CHUNKS[ci]
            slot = ci % 3
            st_ = {}
            thunks = []

            def blockA(bl):
                beta = b0 + bl
                w = []
                if beta >= 3:
                    w = [("SC", xfr_rel[beta - 3]), ("V", xfr_relv[beta - 3])]
                dx_cnt[beta] = dma("sync", "dX%d" % (beta % 3),
                                   lambda E, beta=beta: E.dma_start(
                    out=p4(P4["xfr"] + (beta % 3) * D, D),
                    in_=x_t.ap()[beta * 128:(beta + 1) * 128, :]), waits=w)
                xfr_relv[beta] = vop(
                    lambda E, beta=beta, bl=bl: E.reduce_sum(
                        p4(P4["stats"] + ST["ssum"] + bl, 1),
                        p4(P4["xfr"] + (beta % 3) * D, D), AX.X),
                    waits=[("dX%d" % (beta % 3), dx_cnt[beta])])
                wsq = [("dX%d" % (beta % 3), dx_cnt[beta])]
                if beta == 0:
                    wsq.append(("T", tW))
                sop(lambda E, beta=beta, bl=bl: E.activation(
                    p4(P4["sqscr"], D), p4(P4["xfr"] + (beta % 3) * D, D),
                    AF.Square, accum_out=p4(P4["stats"] + ST["ssq"] + bl, 1)),
                    waits=wsq)
            for bl in range(nb):
                thunks.append(lambda bl=bl: blockA(bl))

            stv = lambda k, n=nb: p4(P4["stats"] + k, n)

            def rstd_a():
                mu1 = p4(P4["stats"] + ST_MU1(slot), nb)
                mu1v[ci] = mu1
                r1v[ci] = p4(P4["stats"] + ST_R1(slot), nb)
                vop(lambda E, mu1=mu1: E.tensor_scalar(mu1, stv(ST["ssum"]), 1.0 / D,
                                              None, ALU.mult),
                    waits=[("SC", cnt["SC"])])
                vop(lambda E, mu1=mu1: E.tensor_tensor(stv(ST["vtmp"]), mu1, mu1,
                                                       ALU.mult))
                vop(lambda E: E.scalar_tensor_tensor(
                    stv(ST["vtmp"]), stv(ST["ssq"]), 1.0 / D, stv(ST["vtmp"]),
                    ALU.mult, ALU.subtract))
                vop(lambda E: E.tensor_scalar(stv(ST["vtmp"]), stv(ST["vtmp"]),
                                              EPS, None, ALU.add))

            def rstd_b():
                r1 = r1v[ci]
                _rsqrt(vop, stv(ST["vtmp"]), r1, stv(ST["yA"]), stv(ST["yB"]), nb)
                nbias = p4(P4["stats"] + ST_NB(slot), nb)
                vop(lambda E, nbias=nbias, r1=r1: E.scalar_tensor_tensor(
                    nbias, mu1v[ci], -1.0, r1, ALU.mult, ALU.mult))
                st_["v_stats"] = cnt["V"]
            thunks.append(rstd_a)
            thunks.append(rstd_b)

            def blockC(bl):
                beta = b0 + bl
                r1 = r1v[ci]
                w = [("V", st_["v_stats"])]
                if beta >= 2:
                    w.append(("dSCw%d" % (beta % 2), dscw_cnt[beta - 2]))
                sc_cast[beta] = sop(
                    lambda E, beta=beta, bl=bl, r1=r1, slot=slot: E.activation(
                        p2(P2["xnb"] + (beta % 2) * D, D),
                        p4(P4["xfr"] + (beta % 3) * D, D), AF.Identity,
                        bias=p4(P4["stats"] + ST_NB(slot) + bl, 1),
                        scale=r1[:, bl:bl + 1]), waits=w)
                xfr_rel[beta] = sc_cast[beta]
                dscw_cnt[beta] = dma(
                    "scalar", "dSCw%d" % (beta % 2), lambda E, beta=beta: E.dma_start(
                        out=s_xnbf.ap()[beta * 128:(beta + 1) * 128, :],
                        in_=p2(P2["xnb"] + (beta % 2) * D, D)),
                    waits=[("SC", sc_cast[beta])])
            for bl in range(nb):
                thunks.append(lambda bl=bl: blockC(bl))

            def transposes():
                w = [("dSCw%d" % ((b0 + k) % 2), dscw_cnt[b0 + k]) for k in range(nb)]
                if ci == 0:
                    w.append(("dHP", 16))
                if ci >= 2:
                    w.append(("T", t_m4end[ci - 2]))
                for dc in range(8):
                    dma("sync", "dTP%d" % (ci % 2),
                        lambda E, dc=dc, b0=b0, nb=nb, ci=ci: E.dma_start(
                        out=p2(P2["xnt"] + (ci % 2) * 3072 + dc * 384, nb * 128),
                        in_=s_xnbf.ap()[b0 * 128:(b0 + nb) * 128,
                                        dc * 128:(dc + 1) * 128],
                        transpose=True), waits=w if dc == 0 else ())
            thunks.append(transposes)
            return thunks

        def emit_mid(ci, pending=None):
            b0, nb = F_CHUNKS[ci]
            def drain(k=1):
                for _ in range(k):
                    if pending:
                        pending.popleft()()
            dtp_need = 16 * 8 * (ci + 1) + 16 * 128
            def m5p(fc):
                w = [("V", sc_h1[ci][fc])]
                if ci == 0 and fc % 8 == 0:
                    w.append(("dW2", 16 * (fc // 8 + 1)))
                if fc == 0:
                    w.append(("dROW", dROW_b2))
                    if ci == 0:
                        w.append(("V", v_wend))
                    if ci >= 1 and relV_eo[ci - 1] is not None:
                        w.append(("V", relV_eo[ci - 1]))
                for bl in range(nb):
                    lhsT = p2(P2["h1t"] + (fc % 2) * 384 + bl * 128, 128)
                    for half in range(2):
                        top(lambda E, lhsT=lhsT, fc=fc, bl=bl, half=half: E.matmul(
                            bank(2 + 2 * bl + half),
                            lhsT, p2(P2["W2"] + fc * D + half * 512, 512),
                            start=(fc == 0), stop=False),
                            waits=w if (bl == 0 and half == 0) else ())

            ndtp = 16 * 8 * (ci // 2 + 1)
            for fc in range(32):
                w = [("dTP%d" % (ci % 2), ndtp)]
                if ci == 0 and fc % 8 == 0:
                    w.append(("dW1", 16 * (fc // 8 + 1)))
                    if fc == 0:
                        w.append(("V", v_wend))
                if fc >= 2:
                    w.append(("V", sc_h1[ci][fc - 2]))
                bk = fc % 2
                tlast = None
                for dc in range(8):
                    tlast = top(
                        lambda E, dc=dc, fc=fc, bk=bk, ci=ci, nb=nb: E.matmul(
                            bank(bk, 0, nb * 128),
                            p2(P2["W1"] + dc * FF + fc * 128, 128),
                            p2(P2["xnt"] + (ci % 2) * 3072 + dc * 384, nb * 128),
                            start=(dc == 0), stop=(dc == 7)),
                        waits=w if dc == 0 else (), inc=(dc == 7))
                t_m4fc = tlast
                sc_h1[ci][fc] = vop(
                    lambda E, fc=fc, bk=bk, nb=nb: E.tensor_scalar(
                        p2(P2["h1t"] + (fc % 2) * 384, nb * 128),
                        bank(bk, 0, nb * 128), b1tl[:, fc:fc + 1], 0.0,
                        ALU.add, ALU.max),
                    waits=[("T", t_m4fc), ("dB1", 16)])
                if fc >= 1:
                    m5p(fc - 1)
                drain(1)
            t_m4end[ci] = cnt["T"]
            m5p(31)
            for bl in range(nb):
                top(lambda E, bl=bl: E.matmul(
                    bank(2 + 2 * bl), ONES, ROWB[0:1, 0:512],
                    start=False, stop=True))
                t_m5[ci][bl] = top(lambda E, bl=bl: E.matmul(
                    bank(3 + 2 * bl), ONES, ROWB[0:1, 512:1024],
                    start=False, stop=True), inc=True)

        def emit_tail_eo(ci):
            b0, nb = F_CHUNKS[ci]
            for bl in range(nb):
                beta = b0 + bl
                wcp = [("T", t_m5[ci][bl])]
                if debug and bl == 0 and ci >= 1:
                    pb0, pnb = F_CHUNKS[ci - 1]
                    if dbgxf_cnt[pb0 + pnb - 1] is not None:
                        wcp.append(("dVT", dbgxf_cnt[pb0 + pnb - 1]))
                v_xf = vop(lambda E, bl=bl: E.tensor_copy(
                    p4(P4["xfb"] + bl * D, D),
                    pball[:, (2 + 2 * bl) * 512:(2 + 2 * bl) * 512 + 1024]),
                    waits=wcp)
                if debug:
                    dbgxf_cnt[beta] = v_xf
            relV_eo[ci] = cnt["V"]
            for bl in range(min(nb, 2)):
                xtl_load(b0 + bl)

        def xtl_load(beta):
            w = []
            if beta >= 2:
                w = [("V", xtl_rel[beta - 2])]
                assert xtl_rel[beta - 2] is not None
            dxt_cnt[beta] = dma("sync", "dXT%d" % (beta % 2),
                                lambda E, beta=beta: E.dma_start(
                out=p4(P4["xtl"] + (beta % 2) * D, D),
                in_=x_t.ap()[beta * 128:(beta + 1) * 128, :]), waits=w)

        def rest_thunks(ci):
            b0, nb = F_CHUNKS[ci]
            thunks = []

            def blockE1(bl):
                beta = b0 + bl
                mu1 = mu1v[ci]
                r1 = r1v[ci]
                vop(lambda E, beta=beta, bl=bl, mu1=mu1, r1=r1: E.tensor_scalar(
                    p4(P4["t1"], D), p4(P4["xtl"] + (beta % 2) * D, D),
                    mu1[:, bl:bl + 1], r1[:, bl:bl + 1], ALU.subtract, ALU.mult),
                    waits=[("dXT%d" % (beta % 2), dxt_cnt[beta])])
                xtl_rel[beta] = cnt["V"]
                if bl + 2 < nb:
                    xtl_load(b0 + bl + 2)

            def blockE2(bl):
                beta = b0 + bl
                v_xf = vop(lambda E, bl=bl: E.tensor_tensor(
                    p4(P4["xfb"] + bl * D, D), p4(P4["xfb"] + bl * D, D),
                    p4(P4["t1"], D), ALU.add))
                if debug:
                    dbgxf_cnt[beta] = dma(
                        "gpsimd", "dVT", lambda E, beta=beta, bl=bl: E.dma_start(
                            out=dxf_t.ap()[beta * 128:(beta + 1) * 128, :],
                            in_=p4(P4["xfb"] + bl * D, D)),
                        waits=[("V", v_xf), ("dVT", dvt_chain[0])])
                    dvt_chain[0] = dbgxf_cnt[beta]

            def blockE3(bl):
                vop(lambda E, bl=bl: E.reduce_sum(
                    p4(P4["stats"] + ST["s2sum"] + bl, 1),
                    p4(P4["xfb"] + bl * D, D), AX.X))
                sop(lambda E, bl=bl: E.activation(
                    p4(P4["sqscr"], D), p4(P4["xfb"] + bl * D, D), AF.Square,
                    accum_out=p4(P4["stats"] + ST["s2sq"] + bl, 1)),
                    waits=[("V", cnt["V"])])
            for bl in range(nb):
                thunks.append(lambda bl=bl: blockE1(bl))
                thunks.append(lambda bl=bl: blockE2(bl))
                thunks.append(lambda bl=bl: blockE3(bl))

            stv = lambda k, n=nb: p4(P4["stats"] + k, n)

            def rstd2_a():
                vop(lambda E: E.tensor_scalar(stv(ST["mu2"]), stv(ST["s2sum"]),
                                              1.0 / D, None, ALU.mult),
                    waits=[("SC", cnt["SC"])])
                vop(lambda E: E.tensor_tensor(stv(ST["vt2"]), stv(ST["mu2"]),
                                              stv(ST["mu2"]), ALU.mult))
                vop(lambda E: E.scalar_tensor_tensor(
                    stv(ST["vt2"]), stv(ST["s2sq"]), 1.0 / D, stv(ST["vt2"]),
                    ALU.mult, ALU.subtract))
                vop(lambda E: E.tensor_scalar(stv(ST["vt2"]), stv(ST["vt2"]),
                                              EPS, None, ALU.add))

            def rstd2_b():
                _rsqrt(vop, stv(ST["vt2"]), stv(ST["r2"]), stv(ST["yA"]),
                       stv(ST["yB"]), nb)
            thunks.append(rstd2_a)
            thunks.append(rstd2_b)

            def blockG1(bl):
                vop(lambda E, bl=bl: E.scalar_tensor_tensor(
                    p4(P4["t1"], D), p4(P4["xfb"] + bl * D, D),
                    p4(P4["stats"] + ST["mu2"] + bl, 1), GT,
                    ALU.subtract, ALU.mult))

            def blockG2(bl):
                beta = b0 + bl
                wg = []
                if beta >= 1:
                    wg = [("SC", sc_silu[beta - 1])]
                v_gp = vop(lambda E, bl=bl: E.scalar_tensor_tensor(
                    p4(P4["gpre"], D), p4(P4["t1"], D),
                    p4(P4["stats"] + ST["r2"] + bl, 1), BT,
                    ALU.mult, ALU.add), waits=wg)
                w2_ = [("V", v_gp)]
                if beta >= 1:
                    w2_.append(("dG", dg_cnt[beta - 1]))
                sc_silu[beta] = sop(lambda E: E.activation(
                    p4(P4["gbuf"], D), p4(P4["gpre"], D), AF.Silu), waits=w2_)
                dg_cnt[beta] = dma("scalar", "dG", lambda E, beta=beta: E.dma_start(
                    out=s_g.ap()[beta * 128:(beta + 1) * 128, :],
                    in_=p4(P4["gbuf"], D)), waits=[("SC", sc_silu[beta])])
            for bl in range(nb):
                thunks.append(lambda bl=bl: blockG1(bl))
                thunks.append(lambda bl=bl: blockG2(bl))
            return thunks

        from collections import deque
        if not skip_f:
            pending = deque()
            for it in range(NCH + 2):
                if it < NCH:
                    pending.extend(front_thunks(it))
                if 0 <= it - 2 < NCH:
                    pending.extend(rest_thunks(it - 2))
                if 0 <= it - 1 < NCH:
                    emit_mid(it - 1, pending)
                    emit_tail_eo(it - 1)
                else:
                    while pending:
                        pending.popleft()()
            while pending:
                pending.popleft()()

        tEndF = cnt["T"]
        vEndF = cnt["V"]
        scEndF = cnt["SC"]
        gpEndF = cnt["GP"]
        BARRIER = [("T", tEndF), ("V", vEndF), ("SC", scEndF), ("GP", gpEndF)]

        for dc in range(8):
            dma("gpsimd", "dWO", lambda E, dc=dc: E.dma_start(
                out=p2(P2["wout"] + dc * D, D),
                in_=wout_t.ap()[dc * 128:(dc + 1) * 128, :]),
                waits=list(BARRIER) + [("dHP", 16)])
        dma("gpsimd", "dROW", lambda E: E.dma_start(out=ROWB, in_=bout_t.ap()),
            waits=BARRIER)
        dROW_bout = cnt["dROW"]

        t_m1 = [[None] * 32 for _ in range(BC)]
        t_m1end = [None] * BC
        t_m2 = [{} for _ in range(BC)]
        t_m2end = [None] * BC
        t_m3 = [[None] * 16 for _ in range(BC)]
        t_m3end = [None] * BC
        v_zevac = [[None] * 32 for _ in range(BC)]
        v_a4 = [{} for _ in range(BC)]
        v_a4end = [None] * BC
        v_tail = [[None] * 16 for _ in range(BC)]
        dwi_cnt = [[None] * 32 for _ in range(BC)]
        dxta_cnt = [None] * BC
        dvt_cnt = [[None] * 16 for _ in range(BC)]
        dvn_cnt = [None] * BC
        dgi_cnt = [[None] * 16 for _ in range(BC)]
        gp_tail = [[None] * 16 for _ in range(BC)]
        dxi_cnt = [[None] * 16 for _ in range(BC)]
        dy_cnt = [[None] * 16 for _ in range(BC)]
        gp_vt = [[None] * 16 for _ in range(BC)]

        def wi_load(b, li):
            ecg = li % 16
            st2 = ecg // 8
            ec = ecg % 8
            src = win2_t if st2 else win0_t
            w2_ = []
            if li >= 2:
                t_prev = t_m1[b][li - 2]
                assert t_prev is not None
                w2_.append(("T", t_prev))
            elif b >= 1:
                t_prev = t_m1[b - 1][30 + li]
                assert t_prev is not None
                w2_.append(("T", t_prev))
            else:
                dwi_cnt[0][li] = wi0_cnt[li]
                return
            dwi_cnt[b][li] = dma(
                "gpsimd", "dWI%d" % (li % 2),
                lambda E, src=src, ec=ec, li=li: E.dma_start(
                    out=p2(P2["wiA"] + (li % 2) * D, D),
                    in_=bass.AP(src, ec * 128,
                                [[D, 128], [128 * D, 8], [1, 128]])), waits=w2_)

        def emit_prep(b):
            if b == 0:
                w = [("T", tEndF),
                     ("dSCw0", cnt["dSCw0"]), ("dSCw1", cnt["dSCw1"])]
            else:
                w = list(BARRIER) + [("dSCw0", cnt["dSCw0"]),
                                     ("dSCw1", cnt["dSCw1"])]
            if b >= 1:
                w.append(("T", t_m1end[b - 1]))
            for dc in range(8):
                dxta_cnt[b] = dma("sync", "dXTA", lambda E, dc=dc, b=b: E.dma_start(
                    out=p2(P2["xntA"] + dc * S, S),
                    in_=s_xnbf.ap()[b * S:(b + 1) * S, dc * 128:(dc + 1) * 128],
                    transpose=True), waits=w if dc == 0 else ())
            for li in range(2):
                wi_load(b, li)

        rr_cnt = [0]
        rr_evac = {}

        def emit_a2(b, pending):
            if b == 0:
                wm = list(BARRIER) + [("dG", cnt["dG"])]
                if debug and dbgxf_cnt[NBLK - 1] is not None:
                    wm.append(("dVT", dbgxf_cnt[NBLK - 1]))
                vop(lambda E: E.memset(p4(P4["stage"], 2), 0.0), waits=wm)
            last_sct = None
            vfin_last = [None]
            HS = 1024
            for half in range(2):
                for ecg in range(16):
                    st2 = ecg // 8
                    ec = ecg % 8
                    li = half * 16 + ecg
                    assert dwi_cnt[b][li] is not None
                    w = [("dWI%d" % (li % 2), dwi_cnt[b][li]),
                         ("dXTA", dxta_cnt[b])]
                    if b == 0 and li == 0:
                        w.append(("V", relV_eo[NCH - 1]))
                        w.append(("V", relV_eo[NCH - 2]))
                    rr = rr_cnt[0]
                    rr_cnt[0] += 1
                    p = 4 + 2 * (rr % 2)
                    wr = list(w)
                    if rr >= 2:
                        wr.append(("V", rr_evac[rr - 2]))
                    first = True
                    tlast = None
                    for ss in (0, 1):
                        for dc in range(8):
                            tlast = top(
                                lambda E, ss=ss, dc=dc, p=p, half=half, li=li:
                                E.matmul(
                                    bank(p + ss),
                                    p2(P2["wiA"] + (li % 2) * D + dc * 128, 128),
                                    p2(P2["xntA"] + dc * S +
                                       (half * 2 + ss) * 512, 512),
                                    start=(dc == 0), stop=(dc == 7)),
                                waits=wr if first else (),
                                inc=(ss == 1 and dc == 7))
                            first = False
                    bin_off = SV["bin2"] if st2 else SV["bin0"]
                    rr_evac[rr] = vop(
                        lambda E, p=p, half=half, bin_off=bin_off, ec=ec:
                        E.tensor_scalar(
                            p4(P4["stage"] + 2 + half * HS, HS),
                            pball[:, p * 512:p * 512 + 1024],
                            sv1(bin_off + ec), None, ALU.add),
                        waits=[("T", tlast), ("dVEC", dvec_total)] +
                              ([("SC", last_sct)] if last_sct is not None
                               else []))
                    t_m1[b][li] = tlast
                    v_zevac[b][li] = rr_evac[rr]
                    if li + 2 < 32:
                        wi_load(b, li + 2)
                    if half == 0 and ecg < 8:
                        for _ in range(2):
                            if pending:
                                pending.popleft()()
                    bc_off = SV["bc2"] if st2 else SV["bc0"]
                    tp = "t2" if st2 else "t0"
                    if half == 0:
                        sop(lambda E, ecg=ecg: E.activation(
                            p4(P4["carry"] + ecg * 2, 2),
                            p4(P4["stage"] + HS, 2), AF.Identity),
                            waits=[("V", v_zevac[b][li])])
                    else:
                        assert vfin_last[0] is not None
                        sop(lambda E, ecg=ecg: E.activation(
                            p4(P4["stage"] + HS, 2),
                            p4(P4["carry"] + ecg * 2, 2), AF.Identity),
                            waits=[("V", vfin_last[0])])
                    sc_t = sop(lambda E, tp=tp, bc_off=bc_off, ec=ec, half=half:
                               E.activation(
                        p4(P4["tconv"], HS), p4(P4["stage"] + 2 + half * HS, HS),
                        AF.Identity,
                        scale=sv1(SV[tp + "2"] + ec),
                        bias=sv1(bc_off + ec)),
                        waits=[("V", v_zevac[b][li]), ("dVEC", dvec_total)])
                    last_sct = sc_t
                    vop(lambda E, tp=tp, ec=ec, half=half:
                        E.scalar_tensor_tensor(
                        p4(P4["tconv"], HS), p4(P4["stage"] + 1 + half * HS, HS),
                        sv1(SV[tp + "1"] + ec), p4(P4["tconv"], HS),
                        ALU.mult, ALU.add), waits=[("SC", sc_t)])
                    if st2 == 0:
                        v_fin = vop(lambda E, tp=tp, ec=ec, half=half:
                                    E.scalar_tensor_tensor(
                            p2(P2["z0c"] + ec * S + half * HS, HS),
                            p4(P4["stage"] + half * HS, HS),
                            sv1(SV[tp + "0"] + ec), p4(P4["tconv"], HS),
                            ALU.mult, ALU.add))
                        vfin_last[0] = v_fin
                        if debug and b == 3 and half == 1:
                            dvt_chain[0] = dma(
                                "gpsimd", "dVT", lambda E, ec=ec: E.dma_start(
                                    out=dz0_t.ap()[ec * 128:(ec + 1) * 128, :],
                                    in_=p2(P2["z0c"] + ec * S, S)),
                                waits=[("V", v_fin), ("dVT", dvt_chain[0])])
                    else:
                        hx = half * 8 + ec
                        wz = []
                        if ec >= 1 and gp_vt[b][hx - 1] is not None:
                            wz = [("GP", gp_vt[b][hx - 1])]
                        v_fin = vop(lambda E, tp=tp, ec=ec, half=half:
                                    E.scalar_tensor_tensor(
                            p4(P4["z2c"] + half * HS, HS),
                            p4(P4["stage"] + half * HS, HS),
                            sv1(SV[tp + "0"] + ec), p4(P4["tconv"], HS),
                            ALU.mult, ALU.add), waits=wz)
                        vfin_last[0] = v_fin
                        wv = [("V", v_fin)]
                        if b >= 1:
                            wv.append(("T", t_m3end[b - 1]))
                        gp_vt[b][hx] = gop(lambda E, ec=ec, half=half:
                                           E.tensor_tensor(
                            p2(P2["vt"] + ec * S + half * HS, HS),
                            p4(P4["z2c"] + half * HS, HS),
                            p2(P2["z0c"] + ec * S + half * HS, HS),
                            ALU.mult), waits=wv)
                        wv2 = [("GP", gp_vt[b][hx]), ("dVT", dvt_chain[0])]
                        if b >= 1:
                            wv2.append(("dVN", dvn_cnt[b - 1]))
                        dvt_cnt[b][hx] = dma(
                            "gpsimd", "dVT", lambda E, ec=ec, half=half:
                            E.dma_start(
                                out=s_vt.ap()[ec * 128:(ec + 1) * 128,
                                              half * HS:half * HS + HS],
                                in_=p2(P2["vt"] + ec * S + half * HS, HS)),
                            waits=wv2)
                        dvt_chain[0] = dvt_cnt[b][hx]
                        if ec == 7:
                            wt_ = [("dVT", dvt_cnt[b][hx])]
                            if half == 0 and b >= 1:
                                wt_.append(("T", t_m2end[b - 1]))
                            for jj in range(8):
                                j = half * 8 + jj
                                dvn_cnt[b] = dma(
                                    "sync", "dVN", lambda E, j=j: E.dma_start(
                                        out=p2(P2["vnat"] + j * D, D),
                                        in_=s_vt.ap()[:, j * 128:(j + 1) * 128],
                                        transpose=True),
                                    waits=wt_ if jj == 0 else ())
            t_m1end[b] = t_m1[b][31]

        def emit_a4(b):
            started = set()
            seglist = []
            for dc in range(8):
                for j in range(16):
                    for d in range(9):
                        if j + d < 16:
                            seglist.append([dc, j, d, 1, d // 1 * 0 + (j + d) // 4, j + d])
            firsts = {}
            lasts = {}
            for k, (dc, j, d, dlen, G, i0) in enumerate(seglist):
                if (dc, G) not in firsts:
                    firsts[(dc, G)] = k
                lasts[(dc, G)] = k
            for k, (dc, j, d, dlen, G, i0) in enumerate(seglist):
                w = []
                if k == 0:
                    w = [("dTall", 16 * 128), ("dVN", cnt["dVN"])]
                    if b >= 1:
                        assert v_tail[b - 1][15] is not None
                        w.append(("V", v_tail[b - 1][15]))
                if firsts[(dc, G)] == k and dc >= 1:
                    w.append(("V", v_a4[b][(dc - 1, G)]))
                st = (firsts[(dc, G)] == k)
                sp = (lasts[(dc, G)] == k)
                tv = top(lambda E, dc=dc, j=j, d=d, dlen=dlen, G=G, i0=i0, st=st,
                         sp=sp: E.matmul(
                             bank(G, (i0 % 4) * 128, dlen * 128),
                             p2(P2["vnat"] + j * D + dc * 128, 128),
                             p2(P2["tall"] + d * 128, dlen * 128),
                             start=st, stop=sp),
                         waits=w, inc=sp)
                if sp:
                    t_m2[b][(dc, G)] = tv
                    vop(lambda E, dc=dc, G=G: E.scalar_tensor_tensor(
                        p4(P4["vfb"], 512), p2(P2["vt"] + dc * S + G * 512, 512),
                        sv1(SV["fb"] + dc), bank(G),
                        ALU.mult, ALU.add), waits=[("T", tv)])
                    v_a4[b][(dc, G)] = vop(lambda E, dc=dc, G=G: E.tensor_tensor(
                        p2(P2["vt"] + dc * S + G * 512, 512), p4(P4["vfb"], 512),
                        p2(P2["z0c"] + dc * S + G * 512, 512), ALU.mult))
            t_m2end[b] = cnt["T"]
            v_a4end[b] = cnt["V"]

        def a5_unit(b, i):
            par = i % 2
            w = [("V", v_a4[b][(7, i // 4)]),
                 ("V", v_a4[b][(7, par * 2)]),
                 ("V", v_a4[b][(7, par * 2 + 1)]),
                 ("dWO", 16 * 8), ("dROW", dROW_bout)]
            if i >= 2 and v_tail[b][i - 2] is not None:
                w.append(("V", v_tail[b][i - 2]))
            elif i < 2 and b >= 1:
                w.append(("V", v_tail[b - 1][14 + i]))
            if b == 0 and i < 2:
                w.append(("V", vEndF))
            first = True
            for dc in range(8):
                for half in range(2):
                    top(lambda E, dc=dc, half=half, par=par, i=i: E.matmul(
                        bank(par * 2 + half),
                        p2(P2["vt"] + dc * S + i * 128, 128),
                        p2(P2["wout"] + dc * D + half * 512, 512),
                        start=(dc == 0), stop=False),
                        waits=w if first else ())
                    first = False
            top(lambda E, par=par: E.matmul(
                bank(par * 2), ONES, ROWB[0:1, 0:512],
                start=False, stop=True))
            t_m3[b][i] = top(lambda E, par=par: E.matmul(
                bank(par * 2 + 1), ONES, ROWB[0:1, 512:1024],
                start=False, stop=True), inc=True)
            wg = []
            wx = []
            if i >= 2 and v_tail[b][i - 2] is not None:
                wg = [("V", v_tail[b][i - 2])]
                wx = [("GP", gp_tail[b][i - 2])]
            elif i < 2 and b >= 1:
                wg = [("V", v_tail[b - 1][14 + i])]
                wx = [("GP", gp_tail[b - 1][14 + i])]
            row0 = b * S + i * 128
            dgi_cnt[b][i] = dma("sync", "dGI%d" % (i % 2), lambda E, row0=row0, i=i:
                                E.dma_start(out=p4(P4["ginA"] + (i % 2) * D, D),
                                            in_=s_g.ap()[row0:row0 + 128, :]),
                                waits=wg)
            dxi_cnt[b][i] = dma("sync", "dXI%d" % (i % 2), lambda E, row0=row0, i=i:
                                E.dma_start(out=p4(P4["xinA"] + (i % 2) * D, D),
                                            in_=x_t.ap()[row0:row0 + 128, :]),
                                waits=wx)
            wv = [("T", t_m3[b][i]), ("dGI%d" % (i % 2), dgi_cnt[b][i])]
            if (b, i) != (0, 0):
                pi = (b, i - 1) if i >= 1 else (b - 1, 15)
                wv.append(("dY", dy_cnt[pi[0]][pi[1]]))
                wv.append(("GP", gp_tail[pi[0]][pi[1]]))
            v_tail[b][i] = vop(lambda E, i=i, par=par: E.scalar_tensor_tensor(
                p4(P4["obuf"], D), p4(P4["ginA"] + (i % 2) * D, D), 1.0,
                pball[:, (par * 2) * 512:(par * 2) * 512 + 1024],
                ALU.mult, ALU.mult), waits=wv)
            gp_tail[b][i] = gop(lambda E, i=i: E.tensor_tensor(
                p4(P4["obuf"], D), p4(P4["obuf"], D),
                p4(P4["xinA"] + (i % 2) * D, D), ALU.add),
                waits=[("V", v_tail[b][i]), ("dXI%d" % (i % 2), dxi_cnt[b][i])])
            dy_cnt[b][i] = dma("scalar", "dY", lambda E, row0=row0: E.dma_start(
                out=y_t.ap()[row0:row0 + 128, :], in_=p4(P4["obuf"], D)),
                waits=[("GP", gp_tail[b][i])])
            if i == 15:
                t_m3end[b] = t_m3[b][15]

        def a5_thunks(b):
            return [(lambda b=b, i=i: a5_unit(b, i)) for i in range(16)]

        emit_prep(0)
        for b in range(128):
            dma("sync", "dTall", lambda E, b=b: E.dma_start(
                out=TALL[b:b + 1, :],
                in_=bass.AP(s_hpad, 127 - b, [[1, 1], [1, 1152]])),
                waits=[("dHP", 16)] if b == 0 else ())
        pendingA = deque()
        for b in range(BC):
            emit_a2(b, pendingA)
            assert not pendingA
            if b + 1 < BC:
                emit_prep(b + 1)
            emit_a4(b)
            pendingA.extend(a5_thunks(b))
        while pendingA:
            pendingA.popleft()()

        wait("sync", "dY", 16 * NBLK)
        if debug:
            dma("sync", "dDBG", lambda E: E.dma_start(out=dvt_t.ap(), in_=s_vt.ap()),
                waits=[("dY", 16 * NBLK)])
            for dc in range(8):
                dma("sync", "dDBG", lambda E, dc=dc: E.dma_start(
                    out=dwt_t.ap()[dc * 128:(dc + 1) * 128, :],
                    in_=p2(P2["vt"] + dc * S, S)))
            for dc in range(8):
                dma("sync", "dDBG", lambda E, dc=dc: E.dma_start(
                    out=dxa_t.ap()[dc * 128:(dc + 1) * 128, :],
                    in_=p2(P2["xntA"] + dc * S, S)))
            dma("sync", "dDBG", lambda E: E.dma_start(
                out=dsv_t.ap(), in_=svtl[:, 0:88]))
            dma("sync", "dDBG", lambda E: E.dma_start(
                out=dwi_t.ap(), in_=p2(P2["wi"], 2048)))
            dma("sync", "dDBG", lambda E: E.dma_start(out=dxn_t.ap(), in_=s_xnbf.ap()),
                waits=[("dSCw0", cnt["dSCw0"]), ("dSCw1", cnt["dSCw1"])])
            dma("sync", "dDBG", lambda E: E.dma_start(out=dg_t.ap(), in_=s_g.ap()),
                waits=[("dG", cnt["dG"])])
            wait("sync", "dDBG", cnt["dDBG"])

        @blk.sync
        def _(E):
            for f in qs["sync"]:
                f(E)

        @blk.tensor
        def _(E):
            for f in qs["tensor"]:
                f(E)

        @blk.vector
        def _(E):
            for f in qs["vector"]:
                f(E)

        @blk.scalar
        def _(E):
            for f in qs["scalar"]:
                f(E)

        @blk.gpsimd
        def _(E):
            for f in qs["gpsimd"]:
                f(E)

    return nc


def _rsqrt(vop, c_ap, out_ap, ya, yb, nb):
    ci = c_ap.bitcast(I32)
    yi = ya.bitcast(I32)
    vop(lambda E: E.tensor_scalar(yi, ci, 1, None, ALU.arith_shift_right))
    vop(lambda E: E.tensor_scalar(yi, yi, -1, MAGIC, ALU.mult, ALU.add))
    cur = ya
    other = yb
    for _ in range(2):
        vop(lambda E, cur=cur, other=other: E.tensor_tensor(other, cur, cur, ALU.mult))
        vop(lambda E, other=other: E.tensor_tensor(other, other, c_ap, ALU.mult))
        vop(lambda E, other=other: E.tensor_scalar(other, other, -0.5, 1.5,
                                                   ALU.mult, ALU.add))
        vop(lambda E, cur=cur, other=other: E.tensor_tensor(other, cur, other,
                                                            ALU.mult))
        cur, other = other, cur
    vop(lambda E, cur=cur: E.tensor_copy(out_ap, cur))


def x_slice(t, idx):
    return t.ap()[idx * 128:(idx + 1) * 128, :]


_NC_CACHE = {}


def kernel(L0_attn_x, L0_attn_w_in, L0_attn_b_in, L0_attn_w_conv, L0_attn_b_conv,
           L0_attn_w1, L0_attn_b1, L0_attn_w2, L0_attn_b2, L0_attn_w3,
           L0_attn_filt_bias, L0_attn_w_out, L0_attn_b_out, L0_ffn_router_w,
           L0_ffn_router_b, L0_ffn_w1, L0_ffn_b1, L0_ffn_w2, L0_ffn_b2,
           L0_ffn_gamma, L0_ffn_beta):
    if "nc" not in _NC_CACHE:
        _NC_CACHE["nc"] = build()
    nc = _NC_CACHE["nc"]
    A = np.ascontiguousarray
    shared = {
        "win0": A(L0_attn_w_in[:, 0:1024], dtype=np.float32),
        "win2": A(L0_attn_w_in[:, 2048:3072], dtype=np.float32),
        "bin0": A(L0_attn_b_in[0:1024], dtype=np.float32),
        "bin2": A(L0_attn_b_in[2048:3072], dtype=np.float32),
        "wc0": A(L0_attn_w_conv[0:1024, 0, :], dtype=np.float32),
        "wc2": A(L0_attn_w_conv[2048:3072, 0, :], dtype=np.float32),
        "bc0": A(L0_attn_b_conv[0:1024], dtype=np.float32),
        "bc2": A(L0_attn_b_conv[2048:3072], dtype=np.float32),
        "ab1": A(L0_attn_b1, dtype=np.float32),
        "aw2": A(L0_attn_w2, dtype=np.float32),
        "ab2": A(L0_attn_b2, dtype=np.float32),
        "aw3": A(L0_attn_w3, dtype=np.float32),
        "fb": A(L0_attn_filt_bias, dtype=np.float32),
        "wout": A(L0_attn_w_out, dtype=np.float32),
        "bout": A(L0_attn_b_out, dtype=np.float32).reshape(1, 1024),
        "w1f": A(L0_ffn_w1[0], dtype=np.float32),
        "b1f": A(L0_ffn_b1[0], dtype=np.float32),
        "w2f": A(L0_ffn_w2[0], dtype=np.float32),
        "b2f": A(L0_ffn_b2[0], dtype=np.float32).reshape(1, 1024),
        "gam": A(L0_ffn_gamma, dtype=np.float32).reshape(1, 1024),
        "bet": A(L0_ffn_beta, dtype=np.float32).reshape(1, 1024),
    }
    x = np.asarray(L0_attn_x, dtype=np.float32)
    in_maps = []
    for c in range(8):
        m = dict(shared)
        m["x"] = A(x[c * BC:(c + 1) * BC]).reshape(NTOK, D)
        in_maps.append(m)
    res = run_bass_kernel_spmd(nc, in_maps, core_ids=list(range(8)))
    y = np.concatenate([r["y"].reshape(BC, S, D) for r in res.results], axis=0)
    return y


# version 20
# speedup vs baseline: 1.1146x; 1.0119x over previous
import numpy as np
import ml_dtypes
from contextlib import ExitStack

import concourse.bass as bass
import concourse.mybir as mybir
from concourse.bass_utils import run_bass_kernel_spmd

F32 = mybir.dt.float32
BF16 = mybir.dt.bfloat16
I32 = mybir.dt.int32
AF = mybir.ActivationFunctionType
ALU = mybir.AluOpType
AX = mybir.AxisListType

BC = 4
S = 2048
D = 1024
NTOK = BC * S
NBLK = NTOK // 128
FF = 4096
EPS = 1e-5
MAGIC = 0x5F3759DF

F_CHUNKS = [(i * 3, 3) for i in range(21)] + [(63, 1)]

P2 = {}
P2["W1"] = 0
P2["W2"] = 32768
P2["xnt"] = 65536
P2["h1t"] = 71680
P2["xnb"] = 72448
P2["gt"] = 74496
P2["bt"] = 75520
P2["tall"] = 76544
P2["rows2"] = 77696
P2_SIZE = 80896
P2["xntA"] = 0
P2["z0c"] = 16384
P2["vt"] = 32768
P2["vnat"] = 49152
P2["wout"] = 65536
P2["wi"] = 73728
P2["wiA"] = 78848
P2["hpad"] = 65536
P2["growb"] = 66816
P2["betrb"] = 67840

P4 = {}
P4["xfr"] = 0
P4["xtl"] = 3072
P4["xfb"] = 5120
P4["t1"] = 8192
P4["gpre"] = 9216
P4["gbuf"] = 10240
P4["sqscr"] = 11264
P4["stats"] = 12288
P4_F_END = 12416
P4["stage"] = 0
P4["tconv"] = 2052
P4["z2c"] = 4100
P4["xinA"] = 6148
P4["ginA"] = 8196
P4["obuf"] = 10244
P4["vfb"] = 11268
P4["svec"] = 11780
P4["carry"] = 11940
P4_SIZE = 12416
P4["aw3"] = P4["xfb"]
P4["grow"] = P4["t1"]
P4["brow"] = P4["xfb"] + 1024
P4["aw2"] = P4["gpre"]
P4["wvec"] = P4["gbuf"]

SV = {"t00": 0, "t01": 8, "t02": 16, "t20": 24, "t21": 32, "t22": 40,
      "bin0": 48, "bin2": 56, "bc0": 64, "bc2": 72, "fb": 80}

ST_MU1 = lambda slot: 24 * 0 + slot * 8
ST_R1 = lambda slot: slot * 8 + 4
ST_TMP = 24
ST = {"ssum": 24, "ssq": 27, "vtmp": 30, "yA": 36, "yB": 39,
      "mu2": 42, "r2": 45, "s2sum": 48, "s2sq": 51, "vt2": 54}
ST_NB = lambda slot: 57 + slot * 3

SEMS = ["V", "SC", "GP", "T", "dX0", "dX1", "dXT0", "dXT1", "dSCw0", "dSCw1",
        "dX2", "dTP0", "dTP1", "dW1", "dW2", "dWI0", "dWI1", "dWO", "dG", "dGI0",
        "dGI1", "dXI0", "dXI1", "dVT", "dVN", "dY", "dROW", "dTall", "dHP",
        "dB1", "dXTA", "dVEC", "dDBG", "dVN2", "dXQ"]


def build(debug=False):
    import os as _os
    skip_f = bool(_os.environ.get("MKDEV_NOF")) and debug
    nc = bass.Bass("TRN2", target_bir_lowering=False)

    x_t = nc.dram_tensor("x", [NTOK, D], F32, kind="ExternalInput")
    y_t = nc.dram_tensor("y", [NTOK, D], F32, kind="ExternalOutput")
    win0_t = nc.dram_tensor("win0", [D, D], F32, kind="ExternalInput")
    win2_t = nc.dram_tensor("win2", [D, D], F32, kind="ExternalInput")
    bin0_t = nc.dram_tensor("bin0", [D], F32, kind="ExternalInput")
    bin2_t = nc.dram_tensor("bin2", [D], F32, kind="ExternalInput")
    wc0_t = nc.dram_tensor("wc0", [D, 3], F32, kind="ExternalInput")
    wc2_t = nc.dram_tensor("wc2", [D, 3], F32, kind="ExternalInput")
    bc0_t = nc.dram_tensor("bc0", [D], F32, kind="ExternalInput")
    bc2_t = nc.dram_tensor("bc2", [D], F32, kind="ExternalInput")
    ab1_t = nc.dram_tensor("ab1", [64], F32, kind="ExternalInput")
    aw2_t = nc.dram_tensor("aw2", [64, 64], F32, kind="ExternalInput")
    ab2_t = nc.dram_tensor("ab2", [64], F32, kind="ExternalInput")
    aw3_t = nc.dram_tensor("aw3", [64, D], F32, kind="ExternalInput")
    fb_t = nc.dram_tensor("fb", [D], F32, kind="ExternalInput")
    wout_t = nc.dram_tensor("wout", [D, D], F32, kind="ExternalInput")
    bout_t = nc.dram_tensor("bout", [1, D], F32, kind="ExternalInput")
    w1f_t = nc.dram_tensor("w1f", [D, FF], F32, kind="ExternalInput")
    b1f_t = nc.dram_tensor("b1f", [FF], F32, kind="ExternalInput")
    w2f_t = nc.dram_tensor("w2f", [FF, D], F32, kind="ExternalInput")
    b2f_t = nc.dram_tensor("b2f", [1, D], F32, kind="ExternalInput")
    gam_t = nc.dram_tensor("gam", [1, D], F32, kind="ExternalInput")
    bet_t = nc.dram_tensor("bet", [1, D], F32, kind="ExternalInput")

    if skip_f:
        s_xnbf = nc.dram_tensor("xnin", [NTOK, D], BF16, kind="ExternalInput")
    else:
        s_xnbf = nc.dram_tensor("s_xnbf", [NTOK, D], BF16, kind="Internal")
    s_g = nc.dram_tensor("s_g", [NTOK, D], F32, kind="Internal")
    s_vt = nc.dram_tensor("s_vt", [D, S], BF16, kind="Internal")
    s_hpad = nc.dram_tensor("s_hpad", [1, 1280], BF16, kind="Internal")
    if debug:
        dxn_t = nc.dram_tensor("dbg_xn", [NTOK, D], BF16, kind="ExternalOutput")
        dg_t = nc.dram_tensor("dbg_g", [NTOK, D], F32, kind="ExternalOutput")
        dxf_t = nc.dram_tensor("dbg_xf", [NTOK, D], F32, kind="ExternalOutput")
        dvt_t = nc.dram_tensor("dbg_vt", [D, S], BF16, kind="ExternalOutput")
        dz0_t = nc.dram_tensor("dbg_z0c", [D, S], BF16, kind="ExternalOutput")
        dwt_t = nc.dram_tensor("dbg_wt", [D, S], BF16, kind="ExternalOutput")
        dxa_t = nc.dram_tensor("dbg_xnta", [D, S], BF16, kind="ExternalOutput")
        dsv_t = nc.dram_tensor("dbg_svec", [128, 88], F32, kind="ExternalOutput")
        dwi_t = nc.dram_tensor("dbg_wi", [128, 2048], BF16, kind="ExternalOutput")
        dst_t = nc.dram_tensor("dbg_stage", [128, 2048], F32, kind="ExternalOutput")

    with ExitStack() as stack:
        pool2 = stack.enter_context(nc.sbuf_tensor("pool2", [128, P2_SIZE], BF16))
        pool4 = stack.enter_context(nc.sbuf_tensor("pool4", [128, P4_SIZE], F32))
        b1tl = stack.enter_context(nc.sbuf_tensor("b1tl", [128, 32], F32))
        svtl = stack.enter_context(nc.sbuf_tensor("svtl", [128, 128], F32))
        pball = stack.enter_context(nc.psum_tensor("pball", [128, 4096], F32))
        SH = {n: stack.enter_context(nc.semaphore(n)) for n in SEMS}
        blk = stack.enter_context(nc.Block())

        def p2(off, n, p0=0, np_=128):
            return pool2[p0:p0 + np_, off:off + n]

        def p4(off, n, p0=0, np_=128):
            return pool4[p0:p0 + np_, off:off + n]

        def sv1(off):
            return svtl[:, off:off + 1]

        def bank(k, off=0, n=512):
            return pball[:, k * 512 + off:k * 512 + off + n]

        ONES = pool2[0:1, P2["rows2"]:P2["rows2"] + 128]
        ROWB = pool2[0:1, P2["rows2"] + 128:P2["rows2"] + 128 + D]

        qs = {e: [] for e in ("sync", "tensor", "vector", "scalar", "gpsimd")}
        cnt = {n: 0 for n in SEMS}
        hw = {}

        def wait(eng, sem, val, future=False):
            if val is None or val <= 0:
                return
            assert future or val <= cnt[sem], \
                f"wait {sem}>={val} but only {cnt[sem]} emitted"
            key = (eng, sem)
            if hw.get(key, 0) >= val:
                return
            hw[key] = val
            qs[eng].append(lambda E, sem=sem, val=val: E.wait_ge(SH[sem], val))

        def vop(fn, waits=()):
            for s, v in waits:
                wait("vector", s, v)
            wait("vector", "V", cnt["V"])
            cnt["V"] += 1
            qs["vector"].append(lambda E, fn=fn: fn(E).then_inc(SH["V"], 1))
            return cnt["V"]

        def sop(fn, waits=()):
            for s, v in waits:
                wait("scalar", s, v)
            wait("scalar", "SC", cnt["SC"])
            cnt["SC"] += 1
            qs["scalar"].append(lambda E, fn=fn: fn(E).then_inc(SH["SC"], 1))
            return cnt["SC"]

        def gop(fn, waits=()):
            for s, v in waits:
                wait("gpsimd", s, v)
            wait("gpsimd", "GP", cnt["GP"])
            cnt["GP"] += 1
            qs["gpsimd"].append(lambda E, fn=fn: fn(E).then_inc(SH["GP"], 1))
            return cnt["GP"]

        def top(fn, waits=(), inc=False):
            for s, v in waits:
                wait("tensor", s, v)
            if inc:
                cnt["T"] += 1
                qs["tensor"].append(lambda E, fn=fn: fn(E).then_inc(SH["T"], 1))
                return cnt["T"]
            qs["tensor"].append(lambda E, fn=fn: fn(E))
            return None

        def dma(eng, sem, fn, waits=()):
            for s, v in waits:
                wait(eng, s, v)
            cnt[sem] += 16
            qs[eng].append(lambda E, fn=fn, sem=sem: fn(E).then_inc(SH[sem], 16))
            return cnt[sem]

        AW2 = p4(P4["aw2"], 64, 0, 64)
        AW3 = p4(P4["aw3"], D, 0, 64)
        AB1 = p4(P4["wvec"] + 0, 1, 0, 64)
        AB2 = p4(P4["wvec"] + 1, 1, 0, 64)
        HM1 = p4(P4["wvec"] + 2, 1, 0, 64)
        HM2 = p4(P4["wvec"] + 3, 1, 0, 64)
        HM2P = p4(P4["wvec"] + 4, 1, 0, 64)
        HPAD = p2(P2["hpad"], 1280, 0, 1)
        GT = p2(P2["gt"], D)
        BT = p2(P2["bt"], D)
        TALL = p2(P2["tall"], 1152)

        dma("gpsimd", "dROW", lambda E: E.dma_start(
            out=AB1, in_=ab1_t.ap().rearrange("(a b) -> a b", b=1)))
        dma("gpsimd", "dROW", lambda E: E.dma_start(out=AW2, in_=aw2_t.ap()))
        dma("gpsimd", "dROW", lambda E: E.dma_start(
            out=AB2, in_=ab2_t.ap().rearrange("(a b) -> a b", b=1)))
        dma("gpsimd", "dROW", lambda E: E.dma_start(out=AW3, in_=aw3_t.ap()))
        dma("gpsimd", "dROW", lambda E: E.dma_start(
            out=p2(P2["growb"], D, 0, 1), in_=gam_t.ap()))
        dma("gpsimd", "dROW", lambda E: E.dma_start(
            out=p2(P2["betrb"], D, 0, 1), in_=bet_t.ap()))
        dROW_b2 = dma("gpsimd", "dROW", lambda E: E.dma_start(out=ROWB, in_=b2f_t.ap()))
        for name, src, off in (("bin0", bin0_t, SV["bin0"]), ("bin2", bin2_t, SV["bin2"]),
                               ("bc0", bc0_t, SV["bc0"]), ("bc2", bc2_t, SV["bc2"]),
                               ("fb", fb_t, SV["fb"])):
            dma("gpsimd", "dVEC", lambda E, src=src, off=off: E.dma_start(
                out=svtl[:, off:off + 8],
                in_=bass.AP(src, 0, [[1, 128], [128, 8]]),
                allow_slow_non_contiguous=True))
        for stn, wct in (("t0", wc0_t), ("t2", wc2_t)):
            for k in range(3):
                off = SV[stn + str(k)]
                dma("gpsimd", "dVEC", lambda E, wct=wct, k=k, off=off:
                    E.dma_start(out=svtl[:, off:off + 8],
                                in_=bass.AP(wct, k, [[3, 128], [384, 8]]),
                                allow_slow_non_contiguous=True))
        dvec_total = cnt["dVEC"]
        dma("gpsimd", "dB1", lambda E: E.dma_start(
            out=b1tl[:, 0:32],
            in_=bass.AP(b1f_t, 0, [[1, 128], [128, 32]]),
            allow_slow_non_contiguous=True))

        v_ones = vop(lambda E: E.memset(ONES, 1.0))

        sop(lambda E: E.activation(HM1, AB1, AF.Silu), waits=[("dROW", dROW_b2)])
        top(lambda E: E.matmul(pball[0:64, 0:1], AW2, HM1, start=True, stop=True),
            waits=[("dROW", dROW_b2), ("SC", cnt["SC"])], inc=True)
        vop(lambda E: E.tensor_scalar(HM2P, pball[0:64, 0:1], AB2, None, ALU.add),
            waits=[("T", cnt["T"]), ("dROW", dROW_b2)])
        sop(lambda E: E.activation(HM2, HM2P, AF.Silu), waits=[("V", cnt["V"])])
        top(lambda E: E.matmul(pball[0:1, 512:1024], HM2, AW3[:, 0:512],
                               start=True, stop=True), waits=[("SC", cnt["SC"])])
        t_h = top(lambda E: E.matmul(pball[0:1, 1024:1536], HM2, AW3[:, 512:1024],
                                     start=True, stop=True), inc=True)
        vop(lambda E: E.memset(HPAD, 0.0))
        vop(lambda E: E.tensor_copy(HPAD[0:1, 127:639], pball[0:1, 512:1024]),
            waits=[("T", t_h)])
        v_hp = vop(lambda E: E.tensor_copy(HPAD[0:1, 639:1151], pball[0:1, 1024:1536]))
        dma("gpsimd", "dHP", lambda E: E.dma_start(out=s_hpad.ap(), in_=HPAD),
            waits=[("V", v_hp)])
        top(lambda E: E.matmul(bank(0), ONES, p2(P2["growb"], 512, 0, 1),
                               start=True, stop=True),
            waits=[("dROW", dROW_b2), ("V", max(v_ones, v_hp))])
        t_g = top(lambda E: E.matmul(bank(1), ONES, p2(P2["growb"] + 512, 512, 0, 1),
                                     start=True, stop=True), inc=True)
        vop(lambda E: E.tensor_copy(GT[:, 0:512], bank(0)), waits=[("T", t_g)])
        v_gt = vop(lambda E: E.tensor_copy(GT[:, 512:1024], bank(1)))
        top(lambda E: E.matmul(bank(2), ONES, p2(P2["betrb"], 512, 0, 1),
                               start=True, stop=True), waits=[("V", v_gt)])
        t_b = top(lambda E: E.matmul(bank(3), ONES, p2(P2["betrb"] + 512, 512, 0, 1),
                                     start=True, stop=True), inc=True)
        vop(lambda E: E.tensor_copy(BT[:, 0:512], bank(2)), waits=[("T", t_b)])
        v_bt = vop(lambda E: E.tensor_copy(BT[:, 512:1024], bank(3)))
        v_wend = v_bt

        if not skip_f:
            for g in range(4):
                w1w = [("dW1", 16 * g)] if g else []
                if g == 1:
                    wait("gpsimd", "dTP0", 16 * 8, future=True)
                dma("gpsimd", "dW1",
                    lambda E, g=g: E.dma_start(
                        out=bass.AP(pool2, P2["W1"] + g * 1024,
                                    [[P2_SIZE, 128], [FF, 8], [1, 1024]]),
                        in_=bass.AP(w1f_t, g * 1024,
                                    [[FF, 128], [128 * FF, 8], [1, 1024]])),
                    waits=w1w)
                dma("gpsimd", "dW2",
                    lambda E, g=g: E.dma_start(
                        out=bass.AP(pool2, P2["W2"] + g * 8 * D,
                                    [[P2_SIZE, 128], [D, 8], [1, D]]),
                        in_=bass.AP(w2f_t, g * 8 * 128 * D,
                                    [[D, 128], [128 * D, 8], [1, D]])),
                    waits=[("dW2", 16 * g)] if g else ())
            wi0_cnt = [None, None]
            for li in range(2):
                src0 = win2_t if (li % 16) // 8 else win0_t
                wi0_cnt[li] = dma(
                    "gpsimd", "dWI%d" % li, lambda E, src0=src0, li=li:
                    E.dma_start(
                        out=p2(P2["wiA"] + li * D, D),
                        in_=bass.AP(src0, (li % 8) * 128,
                                    [[D, 128], [128 * D, 8], [1, 128]])))

        tW = cnt["T"]
        NCH = len(F_CHUNKS)
        relV_eo = [None] * NCH
        t_m4end = [None] * NCH
        sc_h1 = [[None] * 32 for _ in range(NCH)]
        t_m5 = [[None] * 3 for _ in range(NCH)]
        sc_cast = [None] * NBLK
        xfr_rel = [None] * NBLK
        xfr_relv = [None] * NBLK
        xtl_rel = [None] * NBLK
        dscw_cnt = [None] * NBLK
        dx_cnt = [None] * NBLK
        dxt_cnt = [None] * NBLK
        dg_cnt = [None] * NBLK
        sc_silu = [None] * NBLK
        dbgxf_cnt = [None] * NBLK
        dvt_chain = [0]
        mu1v = {}
        r1v = {}

        def front_thunks(ci):
            b0, nb = F_CHUNKS[ci]
            slot = ci % 3
            st_ = {}
            thunks = []

            def blockA(bl):
                beta = b0 + bl
                w = []
                if beta >= 3:
                    w = [("SC", xfr_rel[beta - 3]), ("V", xfr_relv[beta - 3])]
                dx_cnt[beta] = dma("sync", "dX%d" % (beta % 3),
                                   lambda E, beta=beta: E.dma_start(
                    out=p4(P4["xfr"] + (beta % 3) * D, D),
                    in_=x_t.ap()[beta * 128:(beta + 1) * 128, :]), waits=w)
                xfr_relv[beta] = vop(
                    lambda E, beta=beta, bl=bl: E.reduce_sum(
                        p4(P4["stats"] + ST["ssum"] + bl, 1),
                        p4(P4["xfr"] + (beta % 3) * D, D), AX.X),
                    waits=[("dX%d" % (beta % 3), dx_cnt[beta])])
                wsq = [("dX%d" % (beta % 3), dx_cnt[beta])]
                if beta == 0:
                    wsq.append(("T", tW))
                sop(lambda E, beta=beta, bl=bl: E.activation(
                    p4(P4["sqscr"], D), p4(P4["xfr"] + (beta % 3) * D, D),
                    AF.Square, accum_out=p4(P4["stats"] + ST["ssq"] + bl, 1)),
                    waits=wsq)
            for bl in range(nb):
                thunks.append(lambda bl=bl: blockA(bl))

            stv = lambda k, n=nb: p4(P4["stats"] + k, n)

            def rstd_a():
                mu1 = p4(P4["stats"] + ST_MU1(slot), nb)
                mu1v[ci] = mu1
                r1v[ci] = p4(P4["stats"] + ST_R1(slot), nb)
                vop(lambda E, mu1=mu1: E.tensor_scalar(mu1, stv(ST["ssum"]), 1.0 / D,
                                              None, ALU.mult),
                    waits=[("SC", cnt["SC"])])
                vop(lambda E, mu1=mu1: E.tensor_tensor(stv(ST["vtmp"]), mu1, mu1,
                                                       ALU.mult))
                vop(lambda E: E.scalar_tensor_tensor(
                    stv(ST["vtmp"]), stv(ST["ssq"]), 1.0 / D, stv(ST["vtmp"]),
                    ALU.mult, ALU.subtract))
                vop(lambda E: E.tensor_scalar(stv(ST["vtmp"]), stv(ST["vtmp"]),
                                              EPS, None, ALU.add))

            def rstd_b():
                r1 = r1v[ci]
                _rsqrt(vop, stv(ST["vtmp"]), r1, stv(ST["yA"]), stv(ST["yB"]), nb)
                nbias = p4(P4["stats"] + ST_NB(slot), nb)
                vop(lambda E, nbias=nbias, r1=r1: E.scalar_tensor_tensor(
                    nbias, mu1v[ci], -1.0, r1, ALU.mult, ALU.mult))
                st_["v_stats"] = cnt["V"]
            thunks.append(rstd_a)
            thunks.append(rstd_b)

            def blockC(bl):
                beta = b0 + bl
                r1 = r1v[ci]
                w = [("V", st_["v_stats"])]
                if beta >= 2:
                    w.append(("dSCw%d" % (beta % 2), dscw_cnt[beta - 2]))
                sc_cast[beta] = sop(
                    lambda E, beta=beta, bl=bl, r1=r1, slot=slot: E.activation(
                        p2(P2["xnb"] + (beta % 2) * D, D),
                        p4(P4["xfr"] + (beta % 3) * D, D), AF.Identity,
                        bias=p4(P4["stats"] + ST_NB(slot) + bl, 1),
                        scale=r1[:, bl:bl + 1]), waits=w)
                xfr_rel[beta] = sc_cast[beta]
                dscw_cnt[beta] = dma(
                    "scalar", "dSCw%d" % (beta % 2), lambda E, beta=beta: E.dma_start(
                        out=s_xnbf.ap()[beta * 128:(beta + 1) * 128, :],
                        in_=p2(P2["xnb"] + (beta % 2) * D, D)),
                    waits=[("SC", sc_cast[beta])])
            for bl in range(nb):
                thunks.append(lambda bl=bl: blockC(bl))

            def transposes():
                w = [("dSCw%d" % ((b0 + k) % 2), dscw_cnt[b0 + k]) for k in range(nb)]
                if ci == 0:
                    w.append(("dHP", 16))
                if ci >= 2:
                    w.append(("T", t_m4end[ci - 2]))
                for dc in range(8):
                    dma("sync", "dTP%d" % (ci % 2),
                        lambda E, dc=dc, b0=b0, nb=nb, ci=ci: E.dma_start(
                        out=p2(P2["xnt"] + (ci % 2) * 3072 + dc * 384, nb * 128),
                        in_=s_xnbf.ap()[b0 * 128:(b0 + nb) * 128,
                                        dc * 128:(dc + 1) * 128],
                        transpose=True), waits=w if dc == 0 else ())
            thunks.append(transposes)
            return thunks

        def emit_mid(ci, pending=None):
            b0, nb = F_CHUNKS[ci]
            def drain(k=1):
                for _ in range(k):
                    if pending:
                        pending.popleft()()
            dtp_need = 16 * 8 * (ci + 1) + 16 * 128
            def m5p(fc):
                w = [("V", sc_h1[ci][fc])]
                if ci == 0 and fc % 8 == 0:
                    w.append(("dW2", 16 * (fc // 8 + 1)))
                if fc == 0:
                    w.append(("dROW", dROW_b2))
                    if ci == 0:
                        w.append(("V", v_wend))
                    if ci >= 1 and relV_eo[ci - 1] is not None:
                        w.append(("V", relV_eo[ci - 1]))
                for bl in range(nb):
                    lhsT = p2(P2["h1t"] + (fc % 2) * 384 + bl * 128, 128)
                    for half in range(2):
                        top(lambda E, lhsT=lhsT, fc=fc, bl=bl, half=half: E.matmul(
                            bank(2 + 2 * bl + half),
                            lhsT, p2(P2["W2"] + fc * D + half * 512, 512),
                            start=(fc == 0), stop=False),
                            waits=w if (bl == 0 and half == 0) else ())

            ndtp = 16 * 8 * (ci // 2 + 1)
            for fc in range(32):
                w = [("dTP%d" % (ci % 2), ndtp)]
                if ci == 0 and fc % 8 == 0:
                    w.append(("dW1", 16 * (fc // 8 + 1)))
                    if fc == 0:
                        w.append(("V", v_wend))
                if fc >= 2:
                    w.append(("V", sc_h1[ci][fc - 2]))
                bk = fc % 2
                tlast = None
                for dc in range(8):
                    tlast = top(
                        lambda E, dc=dc, fc=fc, bk=bk, ci=ci, nb=nb: E.matmul(
                            bank(bk, 0, nb * 128),
                            p2(P2["W1"] + dc * FF + fc * 128, 128),
                            p2(P2["xnt"] + (ci % 2) * 3072 + dc * 384, nb * 128),
                            start=(dc == 0), stop=(dc == 7)),
                        waits=w if dc == 0 else (), inc=(dc == 7))
                t_m4fc = tlast
                sc_h1[ci][fc] = vop(
                    lambda E, fc=fc, bk=bk, nb=nb: E.tensor_scalar(
                        p2(P2["h1t"] + (fc % 2) * 384, nb * 128),
                        bank(bk, 0, nb * 128), b1tl[:, fc:fc + 1], 0.0,
                        ALU.add, ALU.max),
                    waits=[("T", t_m4fc), ("dB1", 16)])
                if fc >= 1:
                    m5p(fc - 1)
                drain(1)
            t_m4end[ci] = cnt["T"]
            m5p(31)
            for bl in range(nb):
                top(lambda E, bl=bl: E.matmul(
                    bank(2 + 2 * bl), ONES, ROWB[0:1, 0:512],
                    start=False, stop=True))
                t_m5[ci][bl] = top(lambda E, bl=bl: E.matmul(
                    bank(3 + 2 * bl), ONES, ROWB[0:1, 512:1024],
                    start=False, stop=True), inc=True)

        def emit_tail_eo(ci):
            b0, nb = F_CHUNKS[ci]
            for bl in range(nb):
                beta = b0 + bl
                wcp = [("T", t_m5[ci][bl])]
                if debug and bl == 0 and ci >= 1:
                    pb0, pnb = F_CHUNKS[ci - 1]
                    if dbgxf_cnt[pb0 + pnb - 1] is not None:
                        wcp.append(("dVT", dbgxf_cnt[pb0 + pnb - 1]))
                v_xf = vop(lambda E, bl=bl: E.tensor_copy(
                    p4(P4["xfb"] + bl * D, D),
                    pball[:, (2 + 2 * bl) * 512:(2 + 2 * bl) * 512 + 1024]),
                    waits=wcp)
                if debug:
                    dbgxf_cnt[beta] = v_xf
            relV_eo[ci] = cnt["V"]
            for bl in range(min(nb, 2)):
                xtl_load(b0 + bl)

        def xtl_load(beta):
            w = []
            if beta >= 2:
                w = [("V", xtl_rel[beta - 2])]
                assert xtl_rel[beta - 2] is not None
            dxt_cnt[beta] = dma("sync", "dXT%d" % (beta % 2),
                                lambda E, beta=beta: E.dma_start(
                out=p4(P4["xtl"] + (beta % 2) * D, D),
                in_=x_t.ap()[beta * 128:(beta + 1) * 128, :]), waits=w)

        def rest_thunks(ci):
            b0, nb = F_CHUNKS[ci]
            thunks = []

            def blockE1(bl):
                beta = b0 + bl
                mu1 = mu1v[ci]
                r1 = r1v[ci]
                vop(lambda E, beta=beta, bl=bl, mu1=mu1, r1=r1: E.tensor_scalar(
                    p4(P4["t1"], D), p4(P4["xtl"] + (beta % 2) * D, D),
                    mu1[:, bl:bl + 1], r1[:, bl:bl + 1], ALU.subtract, ALU.mult),
                    waits=[("dXT%d" % (beta % 2), dxt_cnt[beta])])
                xtl_rel[beta] = cnt["V"]
                if bl + 2 < nb:
                    xtl_load(b0 + bl + 2)

            def blockE2(bl):
                beta = b0 + bl
                v_xf = vop(lambda E, bl=bl: E.tensor_tensor(
                    p4(P4["xfb"] + bl * D, D), p4(P4["xfb"] + bl * D, D),
                    p4(P4["t1"], D), ALU.add))
                if debug:
                    dbgxf_cnt[beta] = dma(
                        "gpsimd", "dVT", lambda E, beta=beta, bl=bl: E.dma_start(
                            out=dxf_t.ap()[beta * 128:(beta + 1) * 128, :],
                            in_=p4(P4["xfb"] + bl * D, D)),
                        waits=[("V", v_xf), ("dVT", dvt_chain[0])])
                    dvt_chain[0] = dbgxf_cnt[beta]

            def blockE3(bl):
                vop(lambda E, bl=bl: E.reduce_sum(
                    p4(P4["stats"] + ST["s2sum"] + bl, 1),
                    p4(P4["xfb"] + bl * D, D), AX.X))
                sop(lambda E, bl=bl: E.activation(
                    p4(P4["sqscr"], D), p4(P4["xfb"] + bl * D, D), AF.Square,
                    accum_out=p4(P4["stats"] + ST["s2sq"] + bl, 1)),
                    waits=[("V", cnt["V"])])
            for bl in range(nb):
                thunks.append(lambda bl=bl: blockE1(bl))
                thunks.append(lambda bl=bl: blockE2(bl))
                thunks.append(lambda bl=bl: blockE3(bl))

            stv = lambda k, n=nb: p4(P4["stats"] + k, n)

            def rstd2_a():
                vop(lambda E: E.tensor_scalar(stv(ST["mu2"]), stv(ST["s2sum"]),
                                              1.0 / D, None, ALU.mult),
                    waits=[("SC", cnt["SC"])])
                vop(lambda E: E.tensor_tensor(stv(ST["vt2"]), stv(ST["mu2"]),
                                              stv(ST["mu2"]), ALU.mult))
                vop(lambda E: E.scalar_tensor_tensor(
                    stv(ST["vt2"]), stv(ST["s2sq"]), 1.0 / D, stv(ST["vt2"]),
                    ALU.mult, ALU.subtract))
                vop(lambda E: E.tensor_scalar(stv(ST["vt2"]), stv(ST["vt2"]),
                                              EPS, None, ALU.add))

            def rstd2_b():
                _rsqrt(vop, stv(ST["vt2"]), stv(ST["r2"]), stv(ST["yA"]),
                       stv(ST["yB"]), nb)
            thunks.append(rstd2_a)
            thunks.append(rstd2_b)

            def blockG1(bl):
                vop(lambda E, bl=bl: E.scalar_tensor_tensor(
                    p4(P4["t1"], D), p4(P4["xfb"] + bl * D, D),
                    p4(P4["stats"] + ST["mu2"] + bl, 1), GT,
                    ALU.subtract, ALU.mult))

            def blockG2(bl):
                beta = b0 + bl
                wg = []
                if beta >= 1:
                    wg = [("SC", sc_silu[beta - 1])]
                v_gp = vop(lambda E, bl=bl: E.scalar_tensor_tensor(
                    p4(P4["gpre"], D), p4(P4["t1"], D),
                    p4(P4["stats"] + ST["r2"] + bl, 1), BT,
                    ALU.mult, ALU.add), waits=wg)
                w2_ = [("V", v_gp)]
                if beta >= 1:
                    w2_.append(("dG", dg_cnt[beta - 1]))
                sc_silu[beta] = sop(lambda E: E.activation(
                    p4(P4["gbuf"], D), p4(P4["gpre"], D), AF.Silu), waits=w2_)
                dg_cnt[beta] = dma("scalar", "dG", lambda E, beta=beta: E.dma_start(
                    out=s_g.ap()[beta * 128:(beta + 1) * 128, :],
                    in_=p4(P4["gbuf"], D)), waits=[("SC", sc_silu[beta])])
            for bl in range(nb):
                thunks.append(lambda bl=bl: blockG1(bl))
                thunks.append(lambda bl=bl: blockG2(bl))
            return thunks

        from collections import deque
        if not skip_f:
            pending = deque()
            for it in range(NCH + 2):
                if it < NCH:
                    pending.extend(front_thunks(it))
                if 0 <= it - 2 < NCH:
                    pending.extend(rest_thunks(it - 2))
                if 0 <= it - 1 < NCH:
                    emit_mid(it - 1, pending)
                    emit_tail_eo(it - 1)
                else:
                    while pending:
                        pending.popleft()()
            while pending:
                pending.popleft()()

        tEndF = cnt["T"]
        vEndF = cnt["V"]
        scEndF = cnt["SC"]
        gpEndF = cnt["GP"]
        BARRIER = [("T", tEndF), ("V", vEndF), ("SC", scEndF), ("GP", gpEndF)]

        for dc in range(8):
            dma("gpsimd", "dWO", lambda E, dc=dc: E.dma_start(
                out=p2(P2["wout"] + dc * D, D),
                in_=wout_t.ap()[dc * 128:(dc + 1) * 128, :]),
                waits=list(BARRIER) + [("dHP", 16)])
        dma("gpsimd", "dROW", lambda E: E.dma_start(out=ROWB, in_=bout_t.ap()),
            waits=BARRIER)
        dROW_bout = cnt["dROW"]

        t_m1 = [[None] * 32 for _ in range(BC)]
        t_m1end = [None] * BC
        t_m2 = [{} for _ in range(BC)]
        t_m2end = [None] * BC
        t_m3 = [[None] * 16 for _ in range(BC)]
        t_m3end = [None] * BC
        v_zevac = [[None] * 32 for _ in range(BC)]
        v_a4 = [{} for _ in range(BC)]
        v_a4end = [None] * BC
        v_tail = [[None] * 16 for _ in range(BC)]
        dwi_cnt = [[None] * 32 for _ in range(BC)]
        dxta_cnt = [None] * BC
        dvt_cnt = [[None] * 16 for _ in range(BC)]
        dvn_h0 = [None] * BC
        dvn_cnt = [None] * BC
        dgi_cnt = [[None] * 16 for _ in range(BC)]
        gp_tail = [[None] * 16 for _ in range(BC)]
        dxi_cnt = [[None] * 16 for _ in range(BC)]
        dy_cnt = [[None] * 16 for _ in range(BC)]
        gp_vt = [[None] * 16 for _ in range(BC)]

        def wi_load(b, li):
            ecg = li % 16
            st2 = ecg // 8
            ec = ecg % 8
            src = win2_t if st2 else win0_t
            w2_ = []
            if li >= 2:
                t_prev = t_m1[b][li - 2]
                assert t_prev is not None
                w2_.append(("T", t_prev))
            elif b >= 1:
                t_prev = t_m1[b - 1][30 + li]
                assert t_prev is not None
                w2_.append(("T", t_prev))
            else:
                dwi_cnt[0][li] = wi0_cnt[li]
                return
            dwi_cnt[b][li] = dma(
                "gpsimd", "dWI%d" % (li % 2),
                lambda E, src=src, ec=ec, li=li: E.dma_start(
                    out=p2(P2["wiA"] + (li % 2) * D, D),
                    in_=bass.AP(src, ec * 128,
                                [[D, 128], [128 * D, 8], [1, 128]])), waits=w2_)

        def emit_prep(b):
            if b == 0:
                w = [("T", tEndF),
                     ("dSCw0", cnt["dSCw0"]), ("dSCw1", cnt["dSCw1"])]
            else:
                w = list(BARRIER) + [("dSCw0", cnt["dSCw0"]),
                                     ("dSCw1", cnt["dSCw1"])]
            if b >= 1:
                w.append(("T", t_m1end[b - 1]))
            for dc in range(8):
                dxta_cnt[b] = dma("sync", "dXTA", lambda E, dc=dc, b=b: E.dma_start(
                    out=p2(P2["xntA"] + dc * S, S),
                    in_=s_xnbf.ap()[b * S:(b + 1) * S, dc * 128:(dc + 1) * 128],
                    transpose=True), waits=w if dc == 0 else ())
            for li in range(2):
                wi_load(b, li)

        rr_cnt = [0]
        rr_evac = {}

        def emit_a2(b, pending):
            if b == 0:
                wm = list(BARRIER) + [("dG", cnt["dG"])]
                if debug and dbgxf_cnt[NBLK - 1] is not None:
                    wm.append(("dVT", dbgxf_cnt[NBLK - 1]))
                vop(lambda E: E.memset(p4(P4["stage"], 2), 0.0), waits=wm)
            last_sct = None
            vfin_last = [None]
            HS = 1024
            for half in range(2):
                for ecg in range(16):
                    st2 = ecg // 8
                    ec = ecg % 8
                    li = half * 16 + ecg
                    assert dwi_cnt[b][li] is not None
                    w = [("dWI%d" % (li % 2), dwi_cnt[b][li]),
                         ("dXTA", dxta_cnt[b])]
                    if b == 0 and li == 0:
                        w.append(("V", relV_eo[NCH - 1]))
                        w.append(("V", relV_eo[NCH - 2]))
                    rr = rr_cnt[0]
                    rr_cnt[0] += 1
                    p = 4 + 2 * (rr % 2)
                    wr = list(w)
                    if rr >= 2:
                        wr.append(("V", rr_evac[rr - 2]))
                    first = True
                    tlast = None
                    for ss in (0, 1):
                        for dc in range(8):
                            tlast = top(
                                lambda E, ss=ss, dc=dc, p=p, half=half, li=li:
                                E.matmul(
                                    bank(p + ss),
                                    p2(P2["wiA"] + (li % 2) * D + dc * 128, 128),
                                    p2(P2["xntA"] + dc * S +
                                       (half * 2 + ss) * 512, 512),
                                    start=(dc == 0), stop=(dc == 7)),
                                waits=wr if first else (),
                                inc=(ss == 1 and dc == 7))
                            first = False
                    bin_off = SV["bin2"] if st2 else SV["bin0"]
                    rr_evac[rr] = vop(
                        lambda E, p=p, half=half, bin_off=bin_off, ec=ec:
                        E.tensor_scalar(
                            p4(P4["stage"] + 2 + half * HS, HS),
                            pball[:, p * 512:p * 512 + 1024],
                            sv1(bin_off + ec), None, ALU.add),
                        waits=[("T", tlast), ("dVEC", dvec_total)] +
                              ([("SC", last_sct)] if last_sct is not None
                               else []))
                    t_m1[b][li] = tlast
                    v_zevac[b][li] = rr_evac[rr]
                    if li + 2 < 32:
                        wi_load(b, li + 2)
                    if half == 0 and ecg < 8:
                        for _ in range(2):
                            if pending:
                                pending.popleft()()
                    bc_off = SV["bc2"] if st2 else SV["bc0"]
                    tp = "t2" if st2 else "t0"
                    if half == 0:
                        sop(lambda E, ecg=ecg: E.activation(
                            p4(P4["carry"] + ecg * 2, 2),
                            p4(P4["stage"] + HS, 2), AF.Identity),
                            waits=[("V", v_zevac[b][li])])
                    else:
                        assert vfin_last[0] is not None
                        sop(lambda E, ecg=ecg: E.activation(
                            p4(P4["stage"] + HS, 2),
                            p4(P4["carry"] + ecg * 2, 2), AF.Identity),
                            waits=[("V", vfin_last[0])])
                    sc_t = sop(lambda E, tp=tp, bc_off=bc_off, ec=ec, half=half:
                               E.activation(
                        p4(P4["tconv"], HS), p4(P4["stage"] + 2 + half * HS, HS),
                        AF.Identity,
                        scale=sv1(SV[tp + "2"] + ec),
                        bias=sv1(bc_off + ec)),
                        waits=[("V", v_zevac[b][li]), ("dVEC", dvec_total)])
                    last_sct = sc_t
                    vop(lambda E, tp=tp, ec=ec, half=half:
                        E.scalar_tensor_tensor(
                        p4(P4["tconv"], HS), p4(P4["stage"] + 1 + half * HS, HS),
                        sv1(SV[tp + "1"] + ec), p4(P4["tconv"], HS),
                        ALU.mult, ALU.add), waits=[("SC", sc_t)])
                    if st2 == 0:
                        v_fin = vop(lambda E, tp=tp, ec=ec, half=half:
                                    E.scalar_tensor_tensor(
                            p2(P2["z0c"] + ec * S + half * HS, HS),
                            p4(P4["stage"] + half * HS, HS),
                            sv1(SV[tp + "0"] + ec), p4(P4["tconv"], HS),
                            ALU.mult, ALU.add))
                        vfin_last[0] = v_fin
                        if debug and b == 3 and half == 1:
                            dvt_chain[0] = dma(
                                "gpsimd", "dVT", lambda E, ec=ec: E.dma_start(
                                    out=dz0_t.ap()[ec * 128:(ec + 1) * 128, :],
                                    in_=p2(P2["z0c"] + ec * S, S)),
                                waits=[("V", v_fin), ("dVT", dvt_chain[0])])
                    else:
                        hx = half * 8 + ec
                        wz = []
                        if ec >= 1 and gp_vt[b][hx - 1] is not None:
                            wz = [("GP", gp_vt[b][hx - 1])]
                        v_fin = vop(lambda E, tp=tp, ec=ec, half=half:
                                    E.scalar_tensor_tensor(
                            p4(P4["z2c"] + half * HS, HS),
                            p4(P4["stage"] + half * HS, HS),
                            sv1(SV[tp + "0"] + ec), p4(P4["tconv"], HS),
                            ALU.mult, ALU.add), waits=wz)
                        vfin_last[0] = v_fin
                        wv = [("V", v_fin)]
                        if b >= 1:
                            wv.append(("T", t_m3end[b - 1]))
                        gp_vt[b][hx] = gop(lambda E, ec=ec, half=half:
                                           E.tensor_tensor(
                            p2(P2["vt"] + ec * S + half * HS, HS),
                            p4(P4["z2c"] + half * HS, HS),
                            p2(P2["z0c"] + ec * S + half * HS, HS),
                            ALU.mult), waits=wv)
                        wv2 = [("GP", gp_vt[b][hx]), ("dVT", dvt_chain[0])]
                        if b >= 1:
                            wv2.append(("dVN", dvn_cnt[b - 1]))
                        dvt_cnt[b][hx] = dma(
                            "gpsimd", "dVT", lambda E, ec=ec, half=half:
                            E.dma_start(
                                out=s_vt.ap()[ec * 128:(ec + 1) * 128,
                                              half * HS:half * HS + HS],
                                in_=p2(P2["vt"] + ec * S + half * HS, HS)),
                            waits=wv2)
                        dvt_chain[0] = dvt_cnt[b][hx]
                        if ec == 7:
                            wt_ = [("dVT", dvt_cnt[b][hx])]
                            if half == 0 and b >= 1:
                                wt_.append(("T", t_m2end[b - 1]))
                            if half == 1:
                                wt_.append(("dVN", dvn_h0[b]))
                            for jj in range(8):
                                j = half * 8 + jj
                                dvn_cnt[b] = dma(
                                    "sync", "dVN", lambda E, j=j: E.dma_start(
                                        out=p2(P2["vnat"] + j * D, D),
                                        in_=s_vt.ap()[:, j * 128:(j + 1) * 128],
                                        transpose=True),
                                    waits=wt_ if jj == 0 else ())
                            if half == 0:
                                dvn_h0[b] = dvn_cnt[b]
            t_m1end[b] = t_m1[b][31]

        def emit_a4(b):
            started = set()
            seglist = []
            p2_start = None
            for ph in range(2):
                if ph == 1:
                    p2_start = len(seglist)
                for dc in range(8):
                    for G in (ph * 2, ph * 2 + 1):
                        for j in range(16):
                            for d in range(9):
                                if j + d < 16 and (j + d) // 4 == G:
                                    seglist.append([dc, j, d, 1, G, j + d])
            firsts = {}
            lasts = {}
            for k, (dc, j, d, dlen, G, i0) in enumerate(seglist):
                if (dc, G) not in firsts:
                    firsts[(dc, G)] = k
                lasts[(dc, G)] = k
            for k, (dc, j, d, dlen, G, i0) in enumerate(seglist):
                w = []
                if k == 0:
                    w = [("dTall", 16 * 128), ("dVN", dvn_h0[b])]
                    if b >= 1:
                        assert v_tail[b - 1][15] is not None
                        w.append(("V", v_tail[b - 1][15]))
                elif k == p2_start:
                    w = [("dVN", dvn_cnt[b])]
                if firsts[(dc, G)] == k and dc >= 1:
                    w.append(("V", v_a4[b][(dc - 1, G)]))
                st = (firsts[(dc, G)] == k)
                sp = (lasts[(dc, G)] == k)
                tv = top(lambda E, dc=dc, j=j, d=d, dlen=dlen, G=G, i0=i0, st=st,
                         sp=sp: E.matmul(
                             bank(G, (i0 % 4) * 128, dlen * 128),
                             p2(P2["vnat"] + j * D + dc * 128, 128),
                             p2(P2["tall"] + d * 128, dlen * 128),
                             start=st, stop=sp),
                         waits=w, inc=sp)
                if sp:
                    t_m2[b][(dc, G)] = tv
                    vop(lambda E, dc=dc, G=G: E.scalar_tensor_tensor(
                        p4(P4["vfb"], 512), p2(P2["vt"] + dc * S + G * 512, 512),
                        sv1(SV["fb"] + dc), bank(G),
                        ALU.mult, ALU.add), waits=[("T", tv)])
                    v_a4[b][(dc, G)] = vop(lambda E, dc=dc, G=G: E.tensor_tensor(
                        p2(P2["vt"] + dc * S + G * 512, 512), p4(P4["vfb"], 512),
                        p2(P2["z0c"] + dc * S + G * 512, 512), ALU.mult))
            t_m2end[b] = cnt["T"]
            v_a4end[b] = cnt["V"]

        def a5_unit(b, i):
            par = i % 2
            w = [("V", v_a4[b][(7, i // 4)]),
                 ("V", v_a4[b][(7, par * 2)]),
                 ("V", v_a4[b][(7, par * 2 + 1)]),
                 ("dWO", 16 * 8), ("dROW", dROW_bout)]
            if i >= 2 and v_tail[b][i - 2] is not None:
                w.append(("V", v_tail[b][i - 2]))
            elif i < 2 and b >= 1:
                w.append(("V", v_tail[b - 1][14 + i]))
            if b == 0 and i < 2:
                w.append(("V", vEndF))
            first = True
            for dc in range(8):
                for half in range(2):
                    top(lambda E, dc=dc, half=half, par=par, i=i: E.matmul(
                        bank(par * 2 + half),
                        p2(P2["vt"] + dc * S + i * 128, 128),
                        p2(P2["wout"] + dc * D + half * 512, 512),
                        start=(dc == 0), stop=False),
                        waits=w if first else ())
                    first = False
            top(lambda E, par=par: E.matmul(
                bank(par * 2), ONES, ROWB[0:1, 0:512],
                start=False, stop=True))
            t_m3[b][i] = top(lambda E, par=par: E.matmul(
                bank(par * 2 + 1), ONES, ROWB[0:1, 512:1024],
                start=False, stop=True), inc=True)
            wg = []
            wx = []
            if i >= 2 and v_tail[b][i - 2] is not None:
                wg = [("V", v_tail[b][i - 2])]
                wx = [("GP", gp_tail[b][i - 2])]
            elif i < 2 and b >= 1:
                wg = [("V", v_tail[b - 1][14 + i])]
                wx = [("GP", gp_tail[b - 1][14 + i])]
            row0 = b * S + i * 128
            dgi_cnt[b][i] = dma("sync", "dGI%d" % (i % 2), lambda E, row0=row0, i=i:
                                E.dma_start(out=p4(P4["ginA"] + (i % 2) * D, D),
                                            in_=s_g.ap()[row0:row0 + 128, :]),
                                waits=wg)
            dxi_cnt[b][i] = dma("sync", "dXI%d" % (i % 2), lambda E, row0=row0, i=i:
                                E.dma_start(out=p4(P4["xinA"] + (i % 2) * D, D),
                                            in_=x_t.ap()[row0:row0 + 128, :]),
                                waits=wx)
            wv = [("T", t_m3[b][i]), ("dGI%d" % (i % 2), dgi_cnt[b][i])]
            if (b, i) != (0, 0):
                pi = (b, i - 1) if i >= 1 else (b - 1, 15)
                wv.append(("dY", dy_cnt[pi[0]][pi[1]]))
                wv.append(("GP", gp_tail[pi[0]][pi[1]]))
            v_tail[b][i] = vop(lambda E, i=i, par=par: E.scalar_tensor_tensor(
                p4(P4["obuf"], D), p4(P4["ginA"] + (i % 2) * D, D), 1.0,
                pball[:, (par * 2) * 512:(par * 2) * 512 + 1024],
                ALU.mult, ALU.mult), waits=wv)
            gp_tail[b][i] = gop(lambda E, i=i: E.tensor_tensor(
                p4(P4["obuf"], D), p4(P4["obuf"], D),
                p4(P4["xinA"] + (i % 2) * D, D), ALU.add),
                waits=[("V", v_tail[b][i]), ("dXI%d" % (i % 2), dxi_cnt[b][i])])
            dy_cnt[b][i] = dma("scalar", "dY", lambda E, row0=row0: E.dma_start(
                out=y_t.ap()[row0:row0 + 128, :], in_=p4(P4["obuf"], D)),
                waits=[("GP", gp_tail[b][i])])
            if i == 15:
                t_m3end[b] = t_m3[b][15]

        def a5_thunks(b):
            return [(lambda b=b, i=i: a5_unit(b, i)) for i in range(16)]

        emit_prep(0)
        for b in range(128):
            dma("sync", "dTall", lambda E, b=b: E.dma_start(
                out=TALL[b:b + 1, :],
                in_=bass.AP(s_hpad, 127 - b, [[1, 1], [1, 1152]])),
                waits=[("dHP", 16)] if b == 0 else ())
        pendingA = deque()
        for b in range(BC):
            emit_a2(b, pendingA)
            assert not pendingA
            if b + 1 < BC:
                emit_prep(b + 1)
            emit_a4(b)
            pendingA.extend(a5_thunks(b))
        while pendingA:
            pendingA.popleft()()

        wait("sync", "dY", 16 * NBLK)
        if debug:
            dma("sync", "dDBG", lambda E: E.dma_start(out=dvt_t.ap(), in_=s_vt.ap()),
                waits=[("dY", 16 * NBLK)])
            for dc in range(8):
                dma("sync", "dDBG", lambda E, dc=dc: E.dma_start(
                    out=dwt_t.ap()[dc * 128:(dc + 1) * 128, :],
                    in_=p2(P2["vt"] + dc * S, S)))
            for dc in range(8):
                dma("sync", "dDBG", lambda E, dc=dc: E.dma_start(
                    out=dxa_t.ap()[dc * 128:(dc + 1) * 128, :],
                    in_=p2(P2["xntA"] + dc * S, S)))
            dma("sync", "dDBG", lambda E: E.dma_start(
                out=dsv_t.ap(), in_=svtl[:, 0:88]))
            dma("sync", "dDBG", lambda E: E.dma_start(
                out=dwi_t.ap(), in_=p2(P2["wi"], 2048)))
            dma("sync", "dDBG", lambda E: E.dma_start(out=dxn_t.ap(), in_=s_xnbf.ap()),
                waits=[("dSCw0", cnt["dSCw0"]), ("dSCw1", cnt["dSCw1"])])
            dma("sync", "dDBG", lambda E: E.dma_start(out=dg_t.ap(), in_=s_g.ap()),
                waits=[("dG", cnt["dG"])])
            wait("sync", "dDBG", cnt["dDBG"])

        @blk.sync
        def _(E):
            for f in qs["sync"]:
                f(E)

        @blk.tensor
        def _(E):
            for f in qs["tensor"]:
                f(E)

        @blk.vector
        def _(E):
            for f in qs["vector"]:
                f(E)

        @blk.scalar
        def _(E):
            for f in qs["scalar"]:
                f(E)

        @blk.gpsimd
        def _(E):
            for f in qs["gpsimd"]:
                f(E)

    return nc


def _rsqrt(vop, c_ap, out_ap, ya, yb, nb):
    ci = c_ap.bitcast(I32)
    yi = ya.bitcast(I32)
    vop(lambda E: E.tensor_scalar(yi, ci, 1, None, ALU.arith_shift_right))
    vop(lambda E: E.tensor_scalar(yi, yi, -1, MAGIC, ALU.mult, ALU.add))
    cur = ya
    other = yb
    for _ in range(2):
        vop(lambda E, cur=cur, other=other: E.tensor_tensor(other, cur, cur, ALU.mult))
        vop(lambda E, other=other: E.tensor_tensor(other, other, c_ap, ALU.mult))
        vop(lambda E, other=other: E.tensor_scalar(other, other, -0.5, 1.5,
                                                   ALU.mult, ALU.add))
        vop(lambda E, cur=cur, other=other: E.tensor_tensor(other, cur, other,
                                                            ALU.mult))
        cur, other = other, cur
    vop(lambda E, cur=cur: E.tensor_copy(out_ap, cur))


def x_slice(t, idx):
    return t.ap()[idx * 128:(idx + 1) * 128, :]


_NC_CACHE = {}


def kernel(L0_attn_x, L0_attn_w_in, L0_attn_b_in, L0_attn_w_conv, L0_attn_b_conv,
           L0_attn_w1, L0_attn_b1, L0_attn_w2, L0_attn_b2, L0_attn_w3,
           L0_attn_filt_bias, L0_attn_w_out, L0_attn_b_out, L0_ffn_router_w,
           L0_ffn_router_b, L0_ffn_w1, L0_ffn_b1, L0_ffn_w2, L0_ffn_b2,
           L0_ffn_gamma, L0_ffn_beta):
    if "nc" not in _NC_CACHE:
        _NC_CACHE["nc"] = build()
    nc = _NC_CACHE["nc"]
    A = np.ascontiguousarray
    shared = {
        "win0": A(L0_attn_w_in[:, 0:1024], dtype=np.float32),
        "win2": A(L0_attn_w_in[:, 2048:3072], dtype=np.float32),
        "bin0": A(L0_attn_b_in[0:1024], dtype=np.float32),
        "bin2": A(L0_attn_b_in[2048:3072], dtype=np.float32),
        "wc0": A(L0_attn_w_conv[0:1024, 0, :], dtype=np.float32),
        "wc2": A(L0_attn_w_conv[2048:3072, 0, :], dtype=np.float32),
        "bc0": A(L0_attn_b_conv[0:1024], dtype=np.float32),
        "bc2": A(L0_attn_b_conv[2048:3072], dtype=np.float32),
        "ab1": A(L0_attn_b1, dtype=np.float32),
        "aw2": A(L0_attn_w2, dtype=np.float32),
        "ab2": A(L0_attn_b2, dtype=np.float32),
        "aw3": A(L0_attn_w3, dtype=np.float32),
        "fb": A(L0_attn_filt_bias, dtype=np.float32),
        "wout": A(L0_attn_w_out, dtype=np.float32),
        "bout": A(L0_attn_b_out, dtype=np.float32).reshape(1, 1024),
        "w1f": A(L0_ffn_w1[0], dtype=np.float32),
        "b1f": A(L0_ffn_b1[0], dtype=np.float32),
        "w2f": A(L0_ffn_w2[0], dtype=np.float32),
        "b2f": A(L0_ffn_b2[0], dtype=np.float32).reshape(1, 1024),
        "gam": A(L0_ffn_gamma, dtype=np.float32).reshape(1, 1024),
        "bet": A(L0_ffn_beta, dtype=np.float32).reshape(1, 1024),
    }
    x = np.asarray(L0_attn_x, dtype=np.float32)
    in_maps = []
    for c in range(8):
        m = dict(shared)
        m["x"] = A(x[c * BC:(c + 1) * BC]).reshape(NTOK, D)
        in_maps.append(m)
    res = run_bass_kernel_spmd(nc, in_maps, core_ids=list(range(8)))
    y = np.concatenate([r["y"].reshape(BC, S, D) for r in res.results], axis=0)
    return y
